# version 19
# baseline (speedup 1.0000x reference)
import numpy as np
import ml_dtypes
from contextlib import ExitStack

import concourse.bass as bass
import concourse.mybir as mybir
import bass_rust
from concourse.bass_utils import run_bass_kernel_spmd
from concourse.alu_op_type import AluOpType as ALU

F32 = mybir.dt.float32
BF16 = mybir.dt.bfloat16
U32 = mybir.dt.uint32
I32 = mybir.dt.int32
AF = mybir.ActivationFunctionType
AX = mybir.AxisListType.X
bf = ml_dtypes.bfloat16

B, T, D, H, HD, FF, E = 8, 4096, 1024, 16, 64, 4096, 8
EPS = 1e-5
NTILE = T // 128
NTC = 8
DIFF_DEN = float(B * H * T * HD)

ENGS = ("sync", "tensor", "vector", "scalar", "gpsimd")
SELF_ENGS = ("vector", "scalar", "gpsimd")


class Prog:
    def __init__(self, selfsems, selfcnt):
        self.items = {e: [] for e in ENGS}
        self.selfsems = selfsems
        self.selfcnt = selfcnt

    def add(self, eng, fn, waits=(), incs=(), isdma=False):
        incs = list(incs)
        if eng in SELF_ENGS and not isdma:
            assert not incs, "self-engines carry only the self inc"
            self.selfcnt[eng] += 1
            incs = [(self.selfsems[eng], 1)]
            edge = (self.selfsems[eng], self.selfcnt[eng])
        else:
            edge = None
        assert len(incs) <= 1
        self.items[eng].append((tuple(waits), fn, tuple(incs)))
        return edge

    def realize(self, block):
        decs = {"sync": block.sync, "tensor": block.tensor,
                "vector": block.vector, "scalar": block.scalar,
                "gpsimd": block.gpsimd}
        for name in ENGS:
            items = self.items[name]
            if not items:
                continue
            def body(eng, items=items):
                for waits, fn, incs in items:
                    for s, cnt in waits:
                        if cnt > 0:
                            eng.wait_ge(s, cnt)
                    inst = fn(eng)
                    for s, amt in incs:
                        inst.then_inc(s, amt)
            decs[name](body)


class Chain:
    def __init__(self, prog, sem, count=0, last_edge=None):
        self.prog = prog
        self.sem = sem
        self.count = count
        self.last_edge = last_edge

    def step(self, eng, fn, dma=False, extra_waits=()):
        waits = list(extra_waits)
        if self.last_edge is not None:
            waits.append(self.last_edge)
        if eng in SELF_ENGS and not dma:
            self.last_edge = self.prog.add(eng, fn, waits=waits)
        else:
            inc = 16 if dma else 1
            self.prog.add(eng, fn, waits=waits,
                          incs=[(self.sem, inc)], isdma=dma)
            self.count += inc
            self.last_edge = (self.sem, self.count)

    def pstep(self, eng, fn):
        self.prog.add(eng, fn, waits=(), incs=[(self.sem, 16)], isdma=True)
        self.count += 16

    def barrier(self):
        self.last_edge = (self.sem, self.count)


def build_nc():
    nc = bass.Bass("TRN2", target_bir_lowering=False)

    xb = nc.dram_tensor("xb", [T, D], F32, kind="ExternalInput")
    x16 = nc.dram_tensor("x16", [128, D], F32, kind="ExternalInput")
    qwT = nc.dram_tensor("qwT", [128, 8 * D], F32, kind="ExternalInput")
    kwT = nc.dram_tensor("kwT", [128, 8 * D], F32, kind="ExternalInput")
    vwT = nc.dram_tensor("vwT", [128, 8 * D], F32, kind="ExternalInput")
    owT = nc.dram_tensor("owT", [128, 8 * D], F32, kind="ExternalInput")
    qb = nc.dram_tensor("qb", [D], F32, kind="ExternalInput")
    vb = nc.dram_tensor("vb", [D], F32, kind="ExternalInput")
    ob = nc.dram_tensor("ob", [D], F32, kind="ExternalInput")
    lna = nc.dram_tensor("lna", [D], F32, kind="ExternalInput")
    lnb = nc.dram_tensor("lnb", [D], F32, kind="ExternalInput")
    lnc = nc.dram_tensor("lnc", [HD], F32, kind="ExternalInput")
    lnd = nc.dram_tensor("lnd", [HD], F32, kind="ExternalInput")
    lqwT = nc.dram_tensor("lqwT", [HD, HD], F32, kind="ExternalInput")
    lkwT = nc.dram_tensor("lkwT", [HD, HD], F32, kind="ExternalInput")
    lvwT = nc.dram_tensor("lvwT", [HD, HD], F32, kind="ExternalInput")
    lqb = nc.dram_tensor("lqb", [HD], F32, kind="ExternalInput")
    lkb = nc.dram_tensor("lkb", [HD], F32, kind="ExternalInput")
    lvb = nc.dram_tensor("lvb", [HD], F32, kind="ExternalInput")
    gate_w = nc.dram_tensor("gate_w", [D, E], F32, kind="ExternalInput")
    w1 = nc.dram_tensor("w1", [E, D, FF], F32, kind="ExternalInput")
    b1 = nc.dram_tensor("b1", [E, FF], F32, kind="ExternalInput")
    w2 = nc.dram_tensor("w2", [E, FF, D], F32, kind="ExternalInput")
    b2 = nc.dram_tensor("b2", [E, D], F32, kind="ExternalInput")

    out = nc.dram_tensor("out", [T, D], F32, kind="ExternalOutput")
    dbg = nc.dram_tensor("dbg", [1, 64], F32, kind="ExternalOutput")

    ln1T_d = nc.dram_tensor("ln1T_d", [8, 128, T], BF16)
    x1s_d = nc.dram_tensor("x1s_d", [T, D], F32)
    y0x_d = nc.dram_tensor("y0x_d", [T, D], F32)
    b1r_d = nc.dram_tensor("b1r_d", [2, FF], BF16)

    ident_f = nc.inline_tensor(np.eye(128, dtype=np.float32), name="ident_f")
    ident_b = nc.inline_tensor(np.eye(128, dtype=bf), name="ident_b")
    ones_r_f = nc.inline_tensor(np.ones((1, 128), np.float32), name="ones_r_f")
    ones_c_f = nc.inline_tensor(np.ones((128, 1), np.float32), name="ones_c_f")
    ones_r_b = nc.inline_tensor(np.ones((1, 128), bf), name="ones_r_b")
    ones_c_b = nc.inline_tensor(np.ones((128, 1), bf), name="ones_c_b")
    ones_rw_b = nc.inline_tensor(np.ones((1, 512), bf), name="ones_rw_b")
    iota8_c = nc.inline_tensor(np.arange(8, dtype=np.float32)[None, :], name="iota8_c")
    iw1_c = nc.inline_tensor(
        (np.arange(128)[:, None] + 128 * np.arange(8)[None, :]).astype(np.float32),
        name="iw1_c")
    iw2_c = nc.inline_tensor(
        (np.arange(128)[:, None] + 128 * np.arange(32)[None, :]).astype(np.float32),
        name="iw2_c")
    bm = np.full((128, 128), -1e30, np.float32)
    for g in range(8):
        bm[g * 16:(g + 1) * 16, g * 16:(g + 1) * 16] = 0.0
    mask_c = nc.inline_tensor(bm, name="mask_c")

    with ExitStack() as top:
        def sbuf(*a, **k):
            return top.enter_context(nc.sbuf_tensor(*a, **k))
        def sem(name):
            return top.enter_context(nc.semaphore(name))

        identf = sbuf("identf", [128, 128], F32)
        identb = sbuf("identb", [128, 128], BF16)
        onesrf = sbuf("onesrf", [1, 128], F32)
        onescf = sbuf("onescf", [128, 1], F32)
        onesrb = sbuf("onesrb", [1, 128], BF16)
        onescb = sbuf("onescb", [128, 1], BF16)
        onesrwb = sbuf("onesrwb", [1, 512], BF16)
        iota8 = sbuf("iota8", [1, 8], F32)
        iw1 = sbuf("iw1", [128, 8], F32)
        iw2 = sbuf("iw2", [128, 32], F32)
        maskt = sbuf("maskt", [128, 128], F32)
        epst = sbuf("epst", [128, 1], F32)
        c0_128 = sbuf("c0_128", [128, 1], F32)
        c1_128 = sbuf("c1_128", [128, 1], F32)
        idxw1 = sbuf("idxw1", [128, 16], U32)
        idxw2 = sbuf("idxw2", [128, 64], U32)
        cb2c_bf = sbuf("cb2c_bf", [1, D], BF16)
        dbg_t = sbuf("dbg_t", [1, 64], F32)

        mid = ExitStack()
        ob_bc = mid.enter_context(nc.sbuf_tensor("ob_bc", [128, D], F32))
        o16 = mid.enter_context(nc.sbuf_tensor("o16", [128, D], F32))
        seqr = mid.enter_context(nc.sbuf_tensor("seqr", [1, D], F32))

        sV = sem("sV"); sS = sem("sS"); sG = sem("sG")
        selfsems = {"vector": sV, "scalar": sS, "gpsimd": sG}
        selfcnt = {"vector": 0, "scalar": 0, "gpsimd": 0}

        ch = sem("ch")
        dX = [sem("dXa"), sem("dXb"), sem("dXc"), sem("dXd")]
        dXO = [sem("dXOa"), sem("dXOb"), sem("dXOc"), sem("dXOd")]
        dLT = [sem("dLTa"), sem("dLTb")]
        dL = [sem("dLa"), sem("dLb")]
        peB = sem("peB")
        dX1 = [sem("dX1a"), sem("dX1b")]
        dY0 = [sem("dY0a"), sem("dY0b")]
        dY0r = [sem("dY0ra"), sem("dY0rb")]
        dOUT = [sem("dOUTa"), sem("dOUTb")]
        gW1a = sem("gW1a"); gW1b = sem("gW1b"); gW2 = sem("gW2")
        p1p = sem("p1p"); pe1 = sem("pe1"); pe2 = sem("pe2")
        peS = sem("peS"); dB1 = sem("dB1")

        p0t = sem("p0t")
        dW = sem("dW")
        with ExitStack() as p0:
            def s0o(*a, **k):
                return p0.enter_context(nc.sbuf_tensor(*a, **k))
            def ps0o(*a, **k):
                return p0.enter_context(nc.psum_tensor(*a, **k))

            NB = 4
            xt = [s0o(f"xt{j}", [128, D], F32) for j in range(NB)]
            x1t = [s0o(f"x1t{j}", [128, D], F32) for j in range(NB)]
            lnbt = [s0o(f"lnbt{j}", [128, D], BF16) for j in range(NB)]
            stageT = [s0o("stageT0", [128, D], BF16),
                      s0o("stageT1", [128, D], BF16)]
            nmP = [s0o(f"nmP{j}", [128, 1], F32) for j in range(NB)]
            nmeP = [s0o(f"nmeP{j}", [128, 1], F32) for j in range(NB)]
            vsP = [s0o(f"vsP{j}", [128, 1], F32) for j in range(NB)]
            lnvP = [s0o(f"lnvP{j}", [128, 1], F32) for j in range(NB)]
            rstdP = [s0o(f"rstdP{j}", [128, 1], F32) for j in range(NB)]
            nmrP = [s0o(f"nmrP{j}", [128, 1], F32) for j in range(NB)]
            psT = ps0o("psT", [128, 1024], BF16)
            ps_seq = [ps0o("ps_seq0", [1, 512], F32),
                      ps0o("ps_seq1", [1, 512], F32)]

            pp0 = ExitStack()
            def s0(*a, **k):
                return pp0.enter_context(nc.sbuf_tensor(*a, **k))
            def ps0_(*a, **k):
                return pp0.enter_context(nc.psum_tensor(*a, **k))

            wT_t = s0("wT_t", [128, 8 * D], BF16)
            xt16 = s0("xt16", [128, D], F32)
            xin = s0("xin", [128, D], F32)
            t_ln = s0("t_ln", [128, D], BF16)
            qcur = s0("qcur", [128, D], F32)
            k16 = s0("k16", [128, D], F32)
            v16 = s0("v16", [128, D], F32)
            kn = s0("kn", [128, D], F32)
            vi = s0("vi", [128, D], F32)
            qpre = s0("qpre", [128, D], F32)
            qn = s0("qn", [128, D], F32)
            iterout = s0("iterout", [128, D], F32)
            prevattn = s0("prevattn", [128, D], F32)
            tmpb = s0("tmpb", [128, D], F32)
            sqscr = s0("sqscr", [128, D], F32)
            trT = s0("trT", [128, D], BF16)
            trH = s0("trH", [64, H * 128], F32)
            knT64 = s0("knT64", [64, H * 128], F32)
            EX = s0("EX", [128, H * 128], F32)
            lna_bc = s0("lna_bc", [128, D], F32)
            lnb_bc = s0("lnb_bc", [128, D], F32)
            lnc_bc = s0("lnc_bc", [128, HD], F32)
            lnd_bc = s0("lnd_bc", [128, HD], F32)
            qb_bc = s0("qb_bc", [128, D], F32)
            vb_bc = s0("vb_bc", [128, D], F32)
            lqb_bc = s0("lqb_bc", [128, HD], F32)
            lkb_bc = s0("lkb_bc", [128, HD], F32)
            lvb_bc = s0("lvb_bc", [128, HD], F32)
            lqw2 = s0("lqw2", [64, HD], F32)
            lkw2 = s0("lkw2", [64, HD], F32)
            lvw2 = s0("lvw2", [64, HD], F32)
            mask128 = s0("mask128", [128, 128], F32)
            nm = s0("nm", [128, 1], F32)
            vs = s0("vs", [128, 1], F32)
            lnv = s0("lnv", [128, 1], F32)
            rstd = s0("rstd", [128, 1], F32)
            nmr = s0("nmr", [128, 1], F32)
            nm16 = s0("nm16", [128, H], F32)
            v16s = s0("v16s", [128, H], F32)
            rstd16 = s0("rstd16", [128, H], F32)
            rowS = s0("rowS", [128, 1], F32)
            sc1 = s0("sc1", [1, 1], F32)
            diffv = s0("diffv", [1, 1], F32)
            tthr = s0("tthr", [1, 1], F32)
            brkr = s0("brkr", [1, 1], F32)
            brk = s0("brk", [1, 1], F32)
            done = s0("done", [1, 1], F32)
            ndone = s0("ndone", [1, 1], F32)
            d128 = s0("d128", [128, 1], F32)
            d128i = s0("d128i", [128, 1], I32)
            g128 = s0("g128", [128, 1], F32)

            ps_tr = ps0_("ps_tr", [128, 512], F32)
            ps_a = ps0_("ps_a", [128, 512], F32)
            ps_b = ps0_("ps_b", [128, 512], F32)
            ps_big = ps0_("ps_big", [128, 1024], F32)

            pr0 = Prog(selfsems, selfcnt)
            cur = {"v": None, "s": None, "pe": 0, "ch": 0}
            evE = {"tr": None, "a": None, "b": None, "big": None}

            def V(fn, extra=()):
                w = [e for e in ([cur["v"]] + list(extra)) if e is not None]
                cur["v"] = pr0.add("vector", fn, waits=w)
                return cur["v"]

            def Sc(fn, extra=()):
                w = [e for e in ([cur["s"]] + list(extra)) if e is not None]
                cur["s"] = pr0.add("scalar", fn, waits=w)
                return cur["s"]

            def PE(fn, extra=()):
                w = [e for e in extra if e is not None]
                cur["pe"] += 1
                pr0.add("tensor", fn, waits=w, incs=[(p0t, 1)])
                return (p0t, cur["pe"])

            def SY(fn, extra=()):
                w = [e for e in extra if e is not None]
                pr0.add("sync", fn, waits=w, incs=[(ch, 16)], isdma=True)
                cur["ch"] += 16
                return (ch, cur["ch"])

            def ld(dst, srcap):
                SY(lambda e, d=dst, s=srcap: e.dma_start(out=d, in_=s))
            ld(identf[:, :], ident_f.ap())
            ld(identb[:, :], ident_b.ap())
            ld(onesrf[0:1, :], ones_r_f.ap())
            ld(onescf[:, :], ones_c_f.ap())
            ld(onesrb[0:1, :], ones_r_b.ap())
            ld(onescb[:, :], ones_c_b.ap())
            ld(onesrwb[0:1, :], ones_rw_b.ap())
            ld(iota8[0:1, :], iota8_c.ap())
            ld(iw1[:, :], iw1_c.ap())
            ld(iw2[:, :], iw2_c.ap())
            ld(mask128[:, :], mask_c.ap())
            ld(xt16[:, :], x16.ap())
            for bct, vec in ((ob_bc, ob), (lna_bc, lna), (lnb_bc, lnb),
                             (qb_bc, qb), (vb_bc, vb)):
                ld(bct[0:1, :], vec.ap().unsqueeze(0))
            for bct, vec in ((lnc_bc, lnc), (lnd_bc, lnd), (lqb_bc, lqb),
                             (lkb_bc, lkb), (lvb_bc, lvb)):
                ld(bct[0:1, :], vec.ap().unsqueeze(0))
            for dst, srct in ((lqw2, lqwT), (lkw2, lkwT), (lvw2, lvwT)):
                ld(dst[0:HD, :], srct.ap())
            LOADS = (ch, cur["ch"])
            dw_cnt = [0]

            def ldw(srct, extra=()):
                w = [e for e in extra if e is not None]
                pr0.add("gpsimd", lambda e, srct=srct: e.dma_start(
                    out=wT_t[:, :], in_=srct.ap()),
                    waits=w, incs=[(dW, 16)], isdma=True)
                dw_cnt[0] += 16
                return (dW, dw_cnt[0])

            ldq = ldw(qwT)

            V(lambda e: e.memset(epst[:, :], EPS), extra=[LOADS])
            V(lambda e: e.memset(done[:, :], 0.0))
            V(lambda e: e.memset(dbg_t[0:1, :], 0.0))
            for bct in (ob_bc, lna_bc, lnb_bc, qb_bc, vb_bc):
                for ih in range(2):
                    bank = ("a", ps_a) if ih == 0 else ("b", ps_b)
                    PE(lambda e, bct=bct, ih=ih, bank=bank: e.matmul(
                        bank[1][:, :], onesrf[0:1, :],
                        bct[0:1, ih * 512: ih * 512 + 512],
                        start=True, stop=True),
                        extra=[LOADS, evE[bank[0]]])
                    evE[bank[0]] = V(lambda e, bct=bct, ih=ih, bank=bank:
                                     e.tensor_copy(
                        bct[:, ih * 512: ih * 512 + 512], bank[1][:, :]),
                        extra=[(p0t, cur["pe"])])
            for bct in (lnc_bc, lnd_bc, lqb_bc, lkb_bc, lvb_bc):
                PE(lambda e, bct=bct: e.matmul(
                    ps_a[:, 0:HD], onesrf[0:1, :], bct[0:1, 0:HD],
                    start=True, stop=True), extra=[LOADS, evE["a"]])
                evE["a"] = V(lambda e, bct=bct: e.tensor_copy(
                    bct[:, :], ps_a[:, 0:HD]), extra=[(p0t, cur["pe"])])
            E_BC = evE["a"]
            E_OB_SET = cur["v"]

            def tr1024(src, dstT, src_edge=None):
                psb16 = ps_tr[:, :].bitcast(BF16)
                for j in range(8):
                    ee = [src_edge, evE["tr"]] if j == 0 else []
                    PE(lambda e, j=j: e.transpose(
                        psb16[:, j * 128:(j + 1) * 128],
                        src[:, j * 128:(j + 1) * 128], identb[:, :]),
                        extra=ee)
                evE["tr"] = Sc(lambda e: e.activation(
                    out=dstT[:, :], in_=psb16[:, :], func=AF.Identity),
                    extra=[(p0t, cur["pe"])])
                return evE["tr"]

            def tr_heads(src, dstT64, src_edge=None):
                for g in range(4):
                    ee = [src_edge, evE["tr"]] if g == 0 else [evE["tr"]]
                    for j in range(4):
                        h = g * 4 + j
                        PE(lambda e, h=h, j=j: e.transpose(
                            ps_tr[0:64, j * 128:(j + 1) * 128],
                            src[:, h * 64:(h + 1) * 64], identf[:, :]),
                            extra=(ee if j == 0 else []))
                    evE["tr"] = Sc(lambda e, g=g: e.activation(
                        out=dstT64[0:64, g * 512:(g + 1) * 512],
                        in_=ps_tr[0:64, 0:512], func=AF.Identity),
                        extra=[(p0t, cur["pe"])])
                return evE["tr"]

            def proj(srcT, dst, bias_bc, src_edge=None, w_edge=None):
                out_edges = []
                for ih in range(2):
                    bank = ("a", ps_a) if ih == 0 else ("b", ps_b)
                    for hc in range(8):
                        ee = ([src_edge, w_edge, evE[bank[0]]] if hc == 0 else [])
                        PE(lambda e, ih=ih, hc=hc, bank=bank: e.matmul(
                            bank[1][:, :], srcT[:, hc * 128:(hc + 1) * 128],
                            wT_t[:, hc * D + ih * 512: hc * D + ih * 512 + 512],
                            start=(hc == 0), stop=(hc == 7)), extra=ee)
                    if bias_bc is None:
                        evE[bank[0]] = V(lambda e, ih=ih, bank=bank: e.tensor_copy(
                            dst[:, ih * 512: ih * 512 + 512], bank[1][:, :]),
                            extra=[(p0t, cur["pe"])])
                    else:
                        evE[bank[0]] = V(lambda e, ih=ih, bank=bank: e.tensor_tensor(
                            dst[:, ih * 512: ih * 512 + 512], bank[1][:, :],
                            bias_bc[:, ih * 512: ih * 512 + 512], op=ALU.add),
                            extra=[(p0t, cur["pe"])])
                return evE["b"]

            def headproj(srcT64, w2t, bias_bc, dst, src_edge=None):
                for bank_i in range(2):
                    bank = ("a", ps_a) if bank_i == 0 else ("b", ps_b)
                    for hh in range(8):
                        h = bank_i * 8 + hh
                        ee = ([src_edge, evE[bank[0]]] if hh == 0 else [])
                        PE(lambda e, h=h, hh=hh, bank=bank: e.matmul(
                            bank[1][:, hh * 64:(hh + 1) * 64],
                            srcT64[0:64, h * 128:(h + 1) * 128],
                            w2t[0:64, :],
                            start=True, stop=True), extra=ee)
                    evE[bank[0]] = V(lambda e, bank=bank, bank_i=bank_i:
                                     e.tensor_tensor(
                        dst[:, bank_i * 512:(bank_i + 1) * 512].rearrange(
                            "p (a b) -> p a b", b=HD),
                        bank[1][:, :].rearrange("p (a b) -> p a b", b=HD),
                        bias_bc[:, :].unsqueeze(1).broadcast_to((128, 8, HD)),
                        op=ALU.add),
                        extra=[(p0t, cur["pe"])])
                return evE["b"]

            def row_ln(src, dst, w_bc=None, src_edge=None):
                ev = V(lambda e: e.tensor_reduce(nm[:, :], src[:, :], axis=AX,
                                                 op=ALU.add, negate=True),
                       extra=[src_edge])
                V(lambda e: e.tensor_scalar(nm[:, :], nm[:, :], 1.0 / D,
                                            None, op0=ALU.mult))
                es = Sc(lambda e: e.activation(out=sqscr[:, :], in_=src[:, :],
                                               func=AF.Square, bias=nm[:, 0:1],
                                               accum_out=vs[:, 0:1]),
                        extra=[cur["v"], src_edge])
                Sc(lambda e: e.activation(out=lnv[:, :], in_=vs[:, :],
                                          func=AF.Ln, scale=1.0 / D,
                                          bias=epst[:, 0:1]))
                es = Sc(lambda e: e.activation(out=rstd[:, :], in_=lnv[:, :],
                                               func=AF.Exp, scale=-0.5))
                V(lambda e: e.tensor_tensor(nmr[:, :], nm[:, :],
                                            rstd[:, :], op=ALU.mult),
                  extra=[es])
                ed = Sc(lambda e: e.activation(out=dst[:, :], in_=src[:, :],
                                               func=AF.Identity,
                                               bias=nmr[:, 0:1],
                                               scale=rstd[:, 0:1]),
                        extra=[cur["v"]])
                if w_bc is not None:
                    ed = V(lambda e: e.tensor_tensor(dst[:, :], dst[:, :],
                                                     w_bc[:, :], op=ALU.mult),
                           extra=[ed])
                return ed

            def seg_ln(src, dst, w_bc, src_edge=None):
                V(lambda e: e.tensor_reduce(
                    nm16[:, :], src[:, :].rearrange("p (a b) -> p a b", b=HD),
                    axis=AX, op=ALU.add, negate=True), extra=[src_edge])
                V(lambda e: e.tensor_scalar(nm16[:, :], nm16[:, :],
                                            1.0 / HD, None, op0=ALU.mult))
                ev = V(lambda e: e.tensor_tensor(
                    dst[:, :].rearrange("p (a b) -> p a b", b=HD),
                    src[:, :].rearrange("p (a b) -> p a b", b=HD),
                    nm16[:, :].unsqueeze(2).broadcast_to((128, H, HD)),
                    op=ALU.add))
                es = Sc(lambda e: e.activation(out=sqscr[:, :], in_=dst[:, :],
                                               func=AF.Square), extra=[ev])
                V(lambda e: e.tensor_reduce(
                    v16s[:, :], sqscr[:, :].rearrange("p (a b) -> p a b", b=HD),
                    axis=AX, op=ALU.add), extra=[es])
                Sc(lambda e: e.activation(out=v16s[:, :], in_=v16s[:, :],
                                          func=AF.Ln, scale=1.0 / HD,
                                          bias=epst[:, 0:1]), extra=[cur["v"]])
                es = Sc(lambda e: e.activation(out=rstd16[:, :], in_=v16s[:, :],
                                               func=AF.Exp, scale=-0.5))
                V(lambda e: e.tensor_tensor(
                    dst[:, :].rearrange("p (a b) -> p a b", b=HD),
                    dst[:, :].rearrange("p (a b) -> p a b", b=HD),
                    rstd16[:, :].unsqueeze(2).broadcast_to((128, H, HD)),
                    op=ALU.mult), extra=[es])
                return V(lambda e: e.tensor_tensor(
                    dst[:, :].rearrange("p (a b) -> p a b", b=HD),
                    dst[:, :].rearrange("p (a b) -> p a b", b=HD),
                    w_bc[:, :].unsqueeze(1).broadcast_to((128, H, HD)),
                    op=ALU.mult))

            pr1 = pr0
            ORDER = list(range(1, NTILE)) + [0]
            E_v1 = {}; E_v7 = {}; E_a4 = {}; E_a3 = {}; E_v6 = {}; E_v4 = {}
            E_OB = [E_OB_SET]; E_O16 = [None]

            def emit_xin(pos, wextra=()):
                tile = ORDER[pos]
                pr1.add("sync", lambda e, tile=tile, pos=pos: e.dma_start(
                    out=xt[pos % NB][:, :],
                    in_=xb.ap()[tile * 128:(tile + 1) * 128, :]),
                    waits=wextra, incs=[(dX[pos % NB], 16)], isdma=True)

            def stageA(pos):
                tile = ORDER[pos]
                bb = pos % NB
                w = [(dX[bb], 16 * (pos // NB + 1))]
                if pos == 0:
                    w.append(E_OB[0])
                if pos >= NB:
                    w.append((dXO[bb], 16 * (pos // NB)))
                if tile == 0:
                    w.append(E_O16[0])
                E_v1[pos] = pr1.add("gpsimd", lambda e, bb=bb: e.tensor_tensor(
                    x1t[bb][:, :], xt[bb][:, :], ob_bc[:, :], op=ALU.add),
                    waits=w)
                if tile == 0:
                    E_v1[pos] = pr1.add("gpsimd", lambda e, bb=bb: e.tensor_tensor(
                        x1t[bb][0:16, :], x1t[bb][0:16, :], o16[0:16, :],
                        op=ALU.add), waits=[E_v1[pos]])
                e_v3 = pr1.add("vector", lambda e, bb=bb: e.tensor_reduce(
                    nmP[bb][:, :], x1t[bb][:, :], axis=AX, op=ALU.add,
                    negate=True), waits=[E_v1[pos]])
                E_v4[pos] = pr1.add("vector", lambda e, bb=bb: e.tensor_scalar(
                    nmeP[bb][:, :], nmP[bb][:, :], 1.0 / D, None, op0=ALU.mult),
                    waits=[e_v3])
                wa1 = [E_v4[pos]]
                if pos >= NB:
                    wa1.append((p1p, pos - NB + 1))
                e_a1 = pr1.add("scalar", lambda e, bb=bb: e.activation(
                    out=lnbt[bb][:, :], in_=x1t[bb][:, :], func=AF.Square,
                    bias=nmeP[bb][:, 0:1], accum_out=vsP[bb][:, 0:1]),
                    waits=wa1)
                e_a2 = pr1.add("scalar", lambda e, bb=bb: e.activation(
                    out=lnvP[bb][:, :], in_=vsP[bb][:, :], func=AF.Ln,
                    scale=1.0 / D, bias=epst[:, 0:1]), waits=[e_a1])
                E_a3[pos] = pr1.add("scalar", lambda e, bb=bb: e.activation(
                    out=rstdP[bb][:, :], in_=lnvP[bb][:, :], func=AF.Exp,
                    scale=-0.5), waits=[e_a2])

            def stageB1(pos):
                bb = pos % NB
                E_v6[pos] = pr1.add("vector", lambda e, bb=bb: e.tensor_tensor(
                    nmrP[bb][:, :], nmeP[bb][:, :], rstdP[bb][:, :],
                    op=ALU.mult), waits=[E_a3[pos]])
                E_a4[pos] = pr1.add("scalar", lambda e, bb=bb: e.activation(
                    out=lnbt[bb][:, :], in_=x1t[bb][:, :], func=AF.Identity,
                    bias=nmrP[bb][:, 0:1], scale=rstdP[bb][:, 0:1]),
                    waits=[E_v6[pos]])
                w = [E_a4[pos]]
                if pos >= 1:
                    w.append(E_v7[pos - 1])
                for blkj in range(8):
                    pr1.add("tensor", lambda e, bb=bb, j=blkj: e.transpose(
                        psT[:, j * 128:(j + 1) * 128],
                        lnbt[bb][:, j * 128:(j + 1) * 128], identb[:, :]),
                        waits=(w if blkj == 0 else []))
                pr1.add("tensor", lambda e, bb=bb, pos=pos: e.matmul(
                    ps_seq[0][0:1, :], onescb[:, 0:1], lnbt[bb][:, 0:512],
                    start=(pos == 0), stop=(pos == NTILE - 1)))
                pr1.add("tensor", lambda e, bb=bb, pos=pos: e.matmul(
                    ps_seq[1][0:1, :], onescb[:, 0:1], lnbt[bb][:, 512:1024],
                    start=(pos == 0), stop=(pos == NTILE - 1)), incs=[(p1p, 1)])

            def stageB2(pos):
                tile = ORDER[pos]
                bb = pos % NB
                w = [(p1p, pos + 1)]
                if pos >= 2:
                    w.append((dLT[pos % 2], 16 * (pos // 2)))
                E_v7[pos] = pr1.add("vector", lambda e, pos=pos: e.tensor_copy(
                    stageT[pos % 2][:, :], psT[:, :]), waits=w)
                pr1.add("scalar", lambda e, pos=pos, tile=tile: e.dma_start(
                    out=ln1T_d.ap()[:, :, tile * 128:(tile + 1) * 128].rearrange(
                        "a p t -> p a t"),
                    in_=stageT[pos % 2][:, :].rearrange("p (a t) -> p a t", a=8)),
                    waits=[E_v7[pos]], incs=[(dLT[pos % 2], 16)], isdma=True)
                if pos + NB < NTILE:
                    emit_xin(pos + NB, wextra=[E_v1[pos]])
                pr1.add("sync", lambda e, bb=bb, tile=tile: e.dma_start(
                    out=x1s_d.ap()[tile * 128:(tile + 1) * 128, :],
                    in_=x1t[bb][:, :]),
                    waits=[E_a4[pos]], incs=[(dXO[bb], 16)], isdma=True)

            def p1_gen():
                for j in range(NB):
                    emit_xin(j)
                stageA(0)
                stageA(1)
                yield
                for k in range(NTILE - 1):
                    stageB1(k)
                    if k + 2 <= NTILE - 2:
                        stageA(k + 2)
                    stageB2(k)
                    yield
            _gen = p1_gen()

            def p1tick(n=1):
                for _ in range(n):
                    try:
                        next(_gen)
                    except StopIteration:
                        break

            e_xin = row_ln(xt16, xin, src_edge=LOADS)
            e_aln = row_ln(xin, t_ln, lna_bc, src_edge=e_xin)
            e_trT = tr1024(t_ln, trT, src_edge=e_aln)
            e_q = proj(trT, qcur, qb_bc, src_edge=e_trT, w_edge=ldq)
            p1tick()
            ldk = ldw(kwT, extra=[(p0t, cur["pe"])])
            e_bln = row_ln(xin, t_ln, lnb_bc, src_edge=e_aln)
            p1tick()
            e_trT = tr1024(t_ln, trT, src_edge=e_bln)
            e_k = proj(trT, k16, None, src_edge=e_trT, w_edge=ldk)
            p1tick()
            ldv = ldw(vwT, extra=[(p0t, cur["pe"])])
            e_v = proj(trT, v16, vb_bc, src_edge=e_trT, w_edge=ldv)
            p1tick()
            e_trT = tr_heads(k16, trH, src_edge=e_k)
            e_kp = headproj(trH, lkw2, lkb_bc, qpre, src_edge=e_trT)
            e_kn = seg_ln(qpre, kn, lnd_bc, src_edge=e_kp)
            e_trT = tr_heads(v16, trH, src_edge=e_v)
            e_vi = headproj(trH, lvw2, lvb_bc, vi, src_edge=e_trT)
            e_knT = tr_heads(kn, knT64, src_edge=e_kn)
            p1tick()

            for it in range(3):
                e_trT = tr_heads(qcur, trH, src_edge=cur["v"])
                e_qp = headproj(trH, lqw2, lqb_bc, qpre, src_edge=e_trT)
                p1tick()
                e_qn = seg_ln(qpre, qn, lnc_bc, src_edge=e_qp)
                p1tick()
                e_trT = tr_heads(qn, trH, src_edge=e_qn)
                p1tick(2)
                for half in range(2):
                    for hh in range(8):
                        h = half * 8 + hh
                        ee = ([e_trT, e_knT, evE["big"]] if hh == 0 else [])
                        PE(lambda e, h=h, hh=hh: e.matmul(
                            ps_big[:, hh * 128:(hh + 1) * 128],
                            trH[0:64, h * 128:(h + 1) * 128],
                            knT64[0:64, h * 128:(h + 1) * 128],
                            start=True, stop=True), extra=ee)
                    for g4 in range(2):
                        evE["big"] = V(lambda e, half=half, g4=g4:
                                       e.scalar_tensor_tensor(
                            EX[:, half * 1024 + g4 * 512:
                               half * 1024 + g4 * 512 + 512].rearrange(
                                "p (a b) -> p a b", b=128),
                            ps_big[:, g4 * 512:(g4 + 1) * 512].rearrange(
                                "p (a b) -> p a b", b=128),
                            0.125,
                            mask128[:, :].unsqueeze(1).broadcast_to((128, 4, 128)),
                            op0=ALU.mult, op1=ALU.add),
                            extra=([(p0t, cur["pe"])] if g4 == 0 else []))
                p1tick(2)
                ev = V(lambda e: e.tensor_reduce(
                    nm[:, :], EX[:, :], axis=AX, op=ALU.max, negate=True))
                es = Sc(lambda e: e.activation(out=EX[:, :], in_=EX[:, :],
                                               func=AF.Exp, bias=nm[:, 0:1]),
                        extra=[ev])
                V(lambda e: e.tensor_reduce(
                    v16s[:, :], EX[:, :].rearrange("p (a b) -> p a b", b=128),
                    axis=AX, op=ALU.add), extra=[es])
                V(lambda e: e.reciprocal(rstd16[:, :], v16s[:, :]))
                e_P = V(lambda e: e.tensor_tensor(
                    EX[:, :].rearrange("p (a b) -> p a b", b=128),
                    EX[:, :].rearrange("p (a b) -> p a b", b=128),
                    rstd16[:, :].unsqueeze(2).broadcast_to((128, H, 128)),
                    op=ALU.mult))
                ev_pt = {}
                for g in range(4):
                    for j in range(4):
                        h = g * 4 + j
                        ee = []
                        if h == 0:
                            ee = [e_P, evE["big"]]
                        elif j == 0 and g >= 2:
                            ee = [ev_pt[g - 2]]
                        PE(lambda e, h=h, j=j, g=g: e.transpose(
                            ps_big[:, (g % 2) * 512 + j * 128:
                                   (g % 2) * 512 + j * 128 + 128],
                            EX[:, h * 128:(h + 1) * 128], identf[:, :]),
                            extra=ee)
                    ev_pt[g] = Sc(lambda e, g=g: e.activation(
                        out=EX[:, g * 512:(g + 1) * 512],
                        in_=ps_big[:, (g % 2) * 512:(g % 2) * 512 + 512],
                        func=AF.Identity),
                        extra=[(p0t, cur["pe"])])
                evE["big"] = ev_pt[3]
                p1tick(2)
                for bank_i in range(2):
                    bank = ("a", ps_a) if bank_i == 0 else ("b", ps_b)
                    for hh in range(8):
                        h = bank_i * 8 + hh
                        ee = ([ev_pt[1] if bank_i == 0 else ev_pt[3],
                               e_vi, evE[bank[0]]] if hh == 0 else [])
                        PE(lambda e, h=h, hh=hh, bank=bank: e.matmul(
                            bank[1][:, hh * 64:(hh + 1) * 64],
                            EX[:, h * 128:(h + 1) * 128],
                            vi[:, h * 64:(h + 1) * 64], start=True, stop=True),
                            extra=ee)
                    evE[bank[0]] = Sc(lambda e, bank=bank, bank_i=bank_i:
                                      e.activation(
                        out=iterout[:, bank_i * 512:(bank_i + 1) * 512],
                        in_=bank[1][:, :], func=AF.Identity),
                        extra=[(p0t, cur["pe"])])
                p1tick()

                if it == 0:
                    V(lambda e: e.tensor_copy(prevattn[:, :], iterout[:, :]),
                      extra=[evE["b"]])
                    V(lambda e: e.tensor_add(qcur[:, :], qcur[:, :],
                                             iterout[:, :]))
                else:
                    if it == 1:
                        V(lambda e: e.tensor_tensor(
                            tmpb[:, :], iterout[:, :], prevattn[:, :],
                            op=ALU.subtract), extra=[evE["b"]])
                        e_rs = V(lambda e: e.tensor_reduce(
                            rowS[:, :], tmpb[:, :], axis=AX, op=ALU.add,
                            apply_absolute_value=True))
                        pee = PE(lambda e: e.matmul(ps_tr[0:1, 0:1],
                                                    rowS[:, 0:1], onescf[:, 0:1],
                                                    start=True, stop=True),
                                 extra=[e_rs, evE["tr"]])
                        V(lambda e: e.tensor_copy(sc1[:, :], ps_tr[0:1, 0:1]),
                          extra=[pee])
                        evE["tr"] = cur["v"]
                        V(lambda e: e.tensor_scalar(
                            diffv[:, :], sc1[:, :], 1.0 / DIFF_DEN, None,
                            op0=ALU.mult))
                        V(lambda e: e.tensor_scalar(
                            tthr[:, :], diffv[:, :], 0.1, 0.01,
                            op0=ALU.mult, op1=ALU.add))
                        V(lambda e: e.tensor_tensor(
                            brkr[:, :], diffv[:, :], tthr[:, :], op=ALU.is_lt))
                        V(lambda e: e.tensor_scalar(
                            ndone[:, :], done[:, :], -1.0, 1.0,
                            op0=ALU.mult, op1=ALU.add))
                        V(lambda e: e.tensor_tensor(
                            brk[:, :], brkr[:, :], ndone[:, :], op=ALU.mult))
                    else:
                        V(lambda e: e.tensor_scalar(
                            ndone[:, :], done[:, :], -1.0, 1.0,
                            op0=ALU.mult, op1=ALU.add), extra=[evE["b"]])
                    pee = PE(lambda e: e.matmul(ps_tr[:, 1:2], onesrf[0:1, :],
                                                ndone[0:1, 0:1],
                                                start=True, stop=True),
                             extra=[cur["v"], evE["tr"]])
                    V(lambda e: e.tensor_copy(d128[:, :], ps_tr[:, 1:2]),
                      extra=[pee])
                    evE["tr"] = cur["v"]
                    V(lambda e: e.tensor_copy(d128i[:, :], d128[:, :]))
                    V(lambda e: e.copy_predicated(
                        prevattn[:, :], d128i[:, 0:1].broadcast_to((128, D)),
                        iterout[:, :]))
                    if it == 1:
                        V(lambda e: e.tensor_tensor(
                            done[:, :], done[:, :], brk[:, :], op=ALU.max))
                        V(lambda e: e.tensor_scalar(
                            ndone[:, :], done[:, :], -1.0, 1.0,
                            op0=ALU.mult, op1=ALU.add))
                        pee = PE(lambda e: e.matmul(
                            ps_tr[:, 2:3], onesrf[0:1, :], ndone[0:1, 0:1],
                            start=True, stop=True),
                            extra=[cur["v"], evE["tr"]])
                        V(lambda e: e.tensor_copy(g128[:, :], ps_tr[:, 2:3]),
                          extra=[pee])
                        evE["tr"] = cur["v"]
                        V(lambda e: e.tensor_scalar(
                            tmpb[:, :], iterout[:, :], g128[:, 0:1], None,
                            op0=ALU.mult))
                        V(lambda e: e.tensor_add(qcur[:, :], qcur[:, :],
                                                 tmpb[:, :]))

            ldo = ldw(owT, extra=[(p0t, cur["pe"])])
            e_pb = Sc(lambda e: e.activation(out=t_ln[:, :], in_=prevattn[:, :],
                                             func=AF.Identity),
                      extra=[cur["v"]])
            e_trT = tr1024(t_ln, trT, src_edge=e_pb)
            p1tick()
            E_O16[0] = proj(trT, o16, None, src_edge=e_trT, w_edge=ldo)
            V(lambda e: e.tensor_copy(dbg_t[0:1, 32:33], diffv[0:1, :]))
            V(lambda e: e.tensor_copy(dbg_t[0:1, 33:34], done[0:1, :]))

            p1tick(64)
            stageA(NTILE - 1)
            stageB1(NTILE - 1)
            stageB2(NTILE - 1)

            P0_EDGE = cur["v"]
            CH_P0 = cur["ch"]
            with nc.Block() as blk0:
                pr0.realize(blk0)
            pp0.close()

            with ExitStack() as rr:
                def sr(*a, **k):
                    return rr.enter_context(nc.sbuf_tensor(*a, **k))
                def psr(*a, **k):
                    return rr.enter_context(nc.psum_tensor(*a, **k))
                gate_t = sr("gate_t", [128, 8 * E], F32)
                b1_t = sr("b1_t", [8, FF], F32)
                b2_t = sr("b2_t", [8, D], F32)
                seqT = sr("seqT", [128, 8], F32)
                r8 = sr("r8", [8, 1], F32)
                lg8 = sr("lg8", [1, 8], F32)
                pr8 = sr("pr8", [1, 8], F32)
                pm8 = sr("pm8", [1, 8], F32)
                is1 = sr("is1", [1, 8], F32)
                is2 = sr("is2", [1, 8], F32)
                t8 = sr("t8", [1, 8], F32)
                scrow = sr("scrow", [1, 8], F32)
                m1 = sr("m1", [1, 1], F32)
                m2 = sr("m2", [1, 1], F32)
                e0v = sr("e0v", [1, 1], F32)
                e1v = sr("e1v", [1, 1], F32)
                c0v = sr("c0v", [1, 1], F32)
                c1v = sr("c1v", [1, 1], F32)
                s8 = sr("s8", [1, 1], F32)
                rec = sr("rec", [1, 1], F32)
                ebc = sr("ebc", [128, 4], F32)
                idxf = sr("idxf", [128, 32], F32)
                vt8 = sr("vt8", [8, 2], F32)
                b1row = sr("b1row", [1, FF], F32)
                b1row_bf = sr("b1row_bf", [1, FF], BF16)
                cb2r = sr("cb2r", [1, D], F32)
                ps_r = psr("ps_r", [128, 8], F32)
                ps_w = [psr("ps_w0", [1, 512], F32), psr("ps_w1", [1, 512], F32)]

                prR = Prog(selfsems, selfcnt)
                rpe = sem("rpe")
                curR = {"v": None, "s": None, "pe": 0, "ch": CH_P0}

                def RV(fn, extra=()):
                    w = [e for e in ([curR["v"]] + list(extra)) if e is not None]
                    curR["v"] = prR.add("vector", fn, waits=w)
                    return curR["v"]

                def RS(fn, extra=()):
                    w = [e for e in ([curR["s"]] + list(extra)) if e is not None]
                    curR["s"] = prR.add("scalar", fn, waits=w)
                    return curR["s"]

                def RPE(fn, extra=()):
                    w = [e for e in extra if e is not None]
                    curR["pe"] += 1
                    prR.add("tensor", fn, waits=w, incs=[(rpe, 1)])
                    return (rpe, curR["pe"])

                def RSY(fn, extra=()):
                    w = [e for e in extra if e is not None]
                    prR.add("sync", fn, waits=w, incs=[(ch, 16)], isdma=True)
                    curR["ch"] += 16
                    return (ch, curR["ch"])

                first_extra = [(p1p, NTILE), E_v7[NTILE - 1], E_a4[NTILE - 1],
                               (dLT[0], 16 * 16), (dLT[1], 16 * 16)] + [
                               (dXO[j], 16 * (NTILE // 4)) for j in range(4)]

                RSY(lambda e: e.dma_start(
                    out=gate_t[:, :].rearrange("p (hc e) -> p hc e", hc=8),
                    in_=gate_w.ap().rearrange("(hc p) e -> p hc e", p=128)),
                    extra=first_extra)
                RSY(lambda e: e.dma_start(out=b1_t[:, :], in_=b1.ap()))
                RSY(lambda e: e.dma_start(out=b2_t[:, :], in_=b2.ap()))
                LD_ALL = (ch, curR["ch"])

                ev1 = RV(lambda e: e.tensor_copy(seqr[0:1, 0:512],
                                                 ps_seq[0][0:1, :]),
                         extra=[(p1p, NTILE)])
                ev2 = RV(lambda e: e.tensor_copy(seqr[0:1, 512:1024],
                                                 ps_seq[1][0:1, :]))
                for hc in range(8):
                    RPE(lambda e, hc=hc: e.matmul(
                        ps_r[:, hc:hc + 1], seqr[0:1, hc * 128:(hc + 1) * 128],
                        identf[0:1, 0:1], start=True, stop=True),
                        extra=([ev2] if hc == 0 else []))
                evT = RV(lambda e: e.tensor_copy(seqT[:, :], ps_r[:, :]),
                         extra=[(rpe, curR["pe"])])
                for hc in range(8):
                    RPE(lambda e, hc=hc: e.matmul(
                        ps_r[0:8, 0:1], gate_t[:, hc * 8:(hc + 1) * 8],
                        seqT[:, hc:hc + 1], start=(hc == 0), stop=(hc == 7)),
                        extra=([evT, LD_ALL] if hc == 0 else []))
                evr = RV(lambda e: e.tensor_copy(r8[:, :], ps_r[0:8, 0:1]),
                         extra=[(rpe, curR["pe"])])
                RPE(lambda e: e.matmul(ps_r[0:1, 0:8], r8[:, 0:1],
                                       identf[0:8, 0:8], start=True, stop=True),
                    extra=[evr])
                RV(lambda e: e.tensor_copy(lg8[:, :], ps_r[0:1, 0:8]),
                   extra=[(rpe, curR["pe"])])
                RV(lambda e: e.tensor_scalar(lg8[:, :], lg8[:, :], 1.0 / T,
                                             None, op0=ALU.mult))
                RV(lambda e: e.tensor_reduce(m1[:, :], lg8[:, :], axis=AX,
                                             op=ALU.max, negate=True))
                RS(lambda e: e.activation(out=pr8[:, :], in_=lg8[:, :],
                                          func=AF.Exp, bias=m1[0:1, 0:1]),
                   extra=[curR["v"]])
                RV(lambda e: e.tensor_reduce(s8[:, :], pr8[:, :], axis=AX,
                                             op=ALU.add), extra=[curR["s"]])
                RV(lambda e: e.reciprocal(rec[:, :], s8[:, :]))
                RV(lambda e: e.tensor_scalar(pr8[:, :], pr8[:, :],
                                             rec[0:1, 0:1], None, op0=ALU.mult))
                RV(lambda e: e.tensor_reduce(m1[:, :], pr8[:, :], axis=AX,
                                             op=ALU.max))
                RV(lambda e: e.tensor_scalar(is1[:, :], pr8[:, :],
                                             m1[0:1, 0:1], None,
                                             op0=ALU.is_equal))
                RV(lambda e: e.scalar_tensor_tensor(
                    pm8[:, :], is1[:, :], -1e30, pr8[:, :],
                    op0=ALU.mult, op1=ALU.add))
                RV(lambda e: e.tensor_reduce(m2[:, :], pm8[:, :], axis=AX,
                                             op=ALU.max))
                RV(lambda e: e.tensor_scalar(is2[:, :], pm8[:, :],
                                             m2[0:1, 0:1], None,
                                             op0=ALU.is_equal))
                RV(lambda e: e.tensor_scalar(t8[:, :], is1[:, :], -1000.0,
                                             1000.0, op0=ALU.mult, op1=ALU.add))
                RV(lambda e: e.tensor_tensor(t8[:, :], t8[:, :], iota8[0:1, :],
                                             op=ALU.add))
                RV(lambda e: e.tensor_reduce(e0v[:, :], t8[:, :], axis=AX,
                                             op=ALU.min))
                RV(lambda e: e.tensor_scalar(t8[:, :], is2[:, :], -1000.0,
                                             1000.0, op0=ALU.mult, op1=ALU.add))
                RV(lambda e: e.tensor_tensor(t8[:, :], t8[:, :], iota8[0:1, :],
                                             op=ALU.add))
                RV(lambda e: e.tensor_reduce(e1v[:, :], t8[:, :], axis=AX,
                                             op=ALU.min))
                RV(lambda e: e.tensor_tensor(s8[:, :], m1[:, :], m2[:, :],
                                             op=ALU.add))
                RV(lambda e: e.reciprocal(rec[:, :], s8[:, :]))
                RV(lambda e: e.tensor_tensor(c0v[:, :], m1[:, :], rec[:, :],
                                             op=ALU.mult))
                E_C = RV(lambda e: e.tensor_tensor(c1v[:, :], m2[:, :],
                                                   rec[:, :], op=ALU.mult))
                for j, v in enumerate((e0v, e1v, c0v, c1v)):
                    RPE(lambda e, j=j, v=v: e.matmul(
                        ps_r[:, j:j + 1], onesrf[0:1, :], v[0:1, 0:1],
                        start=True, stop=True),
                        extra=([E_C] if j == 0 else []))
                RV(lambda e: e.tensor_copy(ebc[:, :], ps_r[:, 0:4]),
                   extra=[(rpe, curR["pe"])])
                RV(lambda e: e.tensor_copy(c0_128[:, :], ebc[:, 2:3]))
                RV(lambda e: e.tensor_copy(c1_128[:, :], ebc[:, 3:4]))
                for si in range(2):
                    RV(lambda e, si=si: e.scalar_tensor_tensor(
                        idxf[:, 0:8], ebc[:, si:si + 1].broadcast_to((128, 8)),
                        1024.0, iw1[:, :], op0=ALU.mult, op1=ALU.add))
                    RV(lambda e, si=si: e.tensor_copy(
                        idxw1[:, si * 8:(si + 1) * 8], idxf[:, 0:8]))
                    RV(lambda e, si=si: e.scalar_tensor_tensor(
                        idxf[:, 0:32], ebc[:, si:si + 1].broadcast_to((128, 32)),
                        4096.0, iw2[:, :], op0=ALU.mult, op1=ALU.add))
                    RV(lambda e, si=si: e.tensor_copy(
                        idxw2[:, si * 32:(si + 1) * 32], idxf[:, 0:32]))
                eb1st = [None]
                for si, isx in ((0, is1), (1, is2)):
                    RPE(lambda e, isx=isx, si=si: e.matmul(
                        ps_r[0:8, 4 + si:5 + si], isx[0:1, :],
                        identf[0:1, 0:1], start=True, stop=True),
                        extra=[E_C])
                    evis = RV(lambda e, si=si: e.tensor_copy(
                        vt8[:, si:si + 1], ps_r[0:8, 4 + si:5 + si]),
                        extra=[(rpe, curR["pe"])])
                    pee = {}
                    evj = {}
                    for j in range(8):
                        we = [evis, LD_ALL] if j == 0 else []
                        if j >= 2:
                            we = we + [evj[j - 2]]
                        pee[j] = RPE(lambda e, j=j, si=si: e.matmul(
                            ps_w[j % 2][0:1, :], vt8[:, si:si + 1],
                            b1_t[:, j * 512:(j + 1) * 512],
                            start=True, stop=True), extra=we)
                        evj[j] = RV(lambda e, j=j: e.tensor_copy(
                            b1row[0:1, j * 512:(j + 1) * 512],
                            ps_w[j % 2][0:1, :]), extra=[pee[j]])
                    RV(lambda e: e.tensor_copy(b1row_bf[0:1, :], b1row[0:1, :]),
                       extra=([eb1st[0]] if si == 1 else []))
                    eb1st[0] = RSY(lambda e, si=si: e.dma_start(
                        out=b1r_d.ap()[si:si + 1, :], in_=b1row_bf[0:1, :]),
                        extra=[curR["v"]])
                RV(lambda e: e.tensor_scalar(scrow[:, :], is1[:, :],
                                             c0v[0:1, 0:1], None, op0=ALU.mult))
                RV(lambda e: e.scalar_tensor_tensor(
                    scrow[:, :], is2[:, :], c1v[0:1, 0:1], scrow[:, :],
                    op0=ALU.mult, op1=ALU.add))
                RPE(lambda e: e.matmul(ps_r[0:8, 6:7], scrow[0:1, :],
                                       identf[0:1, 0:1], start=True, stop=True),
                    extra=[curR["v"]])
                RV(lambda e: e.tensor_copy(vt8[:, 0:1], ps_r[0:8, 6:7]),
                   extra=[(rpe, curR["pe"])])
                for j in range(2):
                    pee = RPE(lambda e, j=j: e.matmul(
                        ps_w[j][0:1, :], vt8[:, 0:1],
                        b2_t[:, j * 512:(j + 1) * 512],
                        start=True, stop=True), extra=[curR["v"]])
                    RV(lambda e, j=j: e.tensor_copy(
                        cb2r[0:1, j * 512:(j + 1) * 512], ps_w[j][0:1, :]),
                        extra=[pee])
                RV(lambda e: e.reciprocal(rec[:, :], c1v[:, :]))
                RV(lambda e: e.tensor_scalar(cb2c_bf[0:1, :], cb2r[0:1, :],
                                             rec[0:1, 0:1], None, op0=ALU.mult))
                RV(lambda e: e.tensor_copy(dbg_t[0:1, 0:8], pr8[0:1, :]))
                RV(lambda e: e.tensor_copy(dbg_t[0:1, 8:16], lg8[0:1, :]))
                RV(lambda e: e.tensor_copy(dbg_t[0:1, 16:17], e0v[0:1, :]))
                RV(lambda e: e.tensor_copy(dbg_t[0:1, 17:18], e1v[0:1, :]))
                RV(lambda e: e.tensor_copy(dbg_t[0:1, 18:19], c0v[0:1, :]))
                RV(lambda e: e.tensor_copy(dbg_t[0:1, 19:20], c1v[0:1, :]))
                edbg = RSY(lambda e: e.dma_start(out=dbg.ap(),
                                                 in_=dbg_t[0:1, :]),
                           extra=[curR["v"]])
                prR.add("sync", lambda e: e.nop(), waits=[edbg])

                R_EDGES = [curR["v"], curR["s"], (rpe, curR["pe"]),
                           (ch, curR["ch"])]
                with nc.Block() as blkR:
                    prR.realize(blkR)

        mid.close()
        with ExitStack() as ss:
            def s2(*a, **k):
                return ss.enter_context(nc.sbuf_tensor(*a, **k))
            def ps2(*a, **k):
                return ss.enter_context(nc.psum_tensor(*a, **k))
            b1rs = s2("b1rs", [1, FF], BF16)
            b1rT = s2("b1rT", [128, 32], F32)
            w1s = s2("w1s", [128, 8 * FF], BF16)
            w2s = s2("w2s", [128, 32 * D], BF16)
            hmidT = s2("hmidT", [128, 32 * 512], BF16)
            ln1c = [s2("ln1c0", [128, 8 * 512], BF16),
                    s2("ln1c1", [128, 8 * 512], BF16)]
            x1c = [s2("x1c0", [128, D], F32), s2("x1c1", [128, D], F32)]
            ostage = [s2("ostage0", [128, D], F32), s2("ostage1", [128, D], F32)]
            ps_b1 = ps2("ps_b1", [128, 32], F32)
            m1p = [ps2("m1p0", [128, 512], F32), ps2("m1p1", [128, 512], F32)]
            m2p = [ps2("m2p0", [128, 512], F32), ps2("m2p1", [128, 512], F32),
                   ps2("m2p2", [128, 512], F32), ps2("m2p3", [128, 512], F32)]

            prS = Prog(selfsems, selfcnt)
            pe1_cnt = [0]
            pe2_cnt = [0]
            E_gelu = {}
            E_evac = {}
            dl_cnt = [0]

            E_b1e = {}
            for s in range(2):
                wb1 = list(R_EDGES) if s == 0 else [(peB, 1)]
                prS.add("sync", lambda e, s=s: e.dma_start(
                    out=b1rs[0:1, :], in_=b1r_d.ap()[s:s + 1, :]),
                    waits=wb1, incs=[(dB1, 16)], isdma=True)
                for fc in range(32):
                    w = [(dB1, 16 * (s + 1))] if fc == 0 else []
                    if fc == 0 and s == 1:
                        w.append(E_b1e[0])
                    prS.add("tensor", lambda e, fc=fc: e.matmul(
                        ps_b1[:, fc:fc + 1], b1rs[0:1, fc * 128:(fc + 1) * 128],
                        identb[0:1, 0:1], start=True, stop=True),
                        waits=w, incs=([(peB, 1)] if fc == 31 else []))
                wev = [(peB, s + 1)]
                if s == 1:
                    wev.append(E_gelu[255])
                E_b1e[s] = prS.add("vector", lambda e: e.tensor_copy(
                    b1rT[:, :], ps_b1[:, :]), waits=wev)

                gwaits = list(R_EDGES) if s == 0 else [(pe1, 256)]
                first = True
                for fh in range(2):
                    gsem = gW1a if fh == 0 else gW1b
                    for hc in range(8):
                        w = gwaits if first else []
                        first = False
                        prS.add("gpsimd", lambda e, s=s, fh=fh, hc=hc:
                                e.indirect_dma_start(
                                    w1s[:, hc * FF + fh * 2048: hc * FF + fh * 2048 + 2048],
                                    None,
                                    w1.ap().rearrange("e h f -> (e h) f"),
                                    bass_rust.IndirectOffsetOnAxis(
                                        idxw1[:, s * 8 + hc: s * 8 + hc + 1], 0),
                                    element_offset=fh * 2048),
                                waits=w, incs=[(gsem, 16)], isdma=True)
                for fc in range(32):
                    w = [(peS, 1)] if (fc == 0 and s == 1) else []
                    prS.add("gpsimd", lambda e, s=s, fc=fc: e.indirect_dma_start(
                        w2s[:, fc * D: (fc + 1) * D], None,
                        w2.ap().rearrange("e f d -> (e f) d"),
                        bass_rust.IndirectOffsetOnAxis(
                            idxw2[:, s * 32 + fc: s * 32 + fc + 1], 0),
                        element_offset=0),
                        waits=w, incs=[(gW2, 16)], isdma=True)
                GW1 = 16 * 8 * (s + 1)
                GW2ALL = 16 * 32 * (s + 1)

                for tc in range(NTC):
                    q = s * NTC + tc
                    w = []
                    if q == 0:
                        w += [(dLT[0], 16 * (NTILE // 2)),
                              (dLT[1], 16 * (NTILE // 2))] + list(R_EDGES)
                    if q >= 2:
                        w.append((pe1, 32 * (q - 1)))
                    prS.add("scalar", lambda e, tc=tc, q=q: e.dma_start(
                        out=ln1c[q % 2][:, :].rearrange("p (a t) -> p a t", a=8),
                        in_=ln1T_d.ap()[:, :, tc * 512:(tc + 1) * 512].rearrange(
                            "a p t -> p a t")),
                        waits=w, incs=[(dL[q % 2], 16)], isdma=True)
                    DL_W = (dL[q % 2], 16 * (q // 2 + 1))

                    for j in range(32):
                        gidx1 = pe1_cnt[0]
                        w = []
                        if j == 0:
                            w.append(DL_W)
                        if tc == 0:
                            if j == 0:
                                w.append((gW1a, GW1))
                            if j == 16:
                                w.append((gW1b, GW1))
                        if gidx1 >= 2:
                            w.append(E_gelu[gidx1 - 2])
                        for hc in range(8):
                            prS.add("tensor", lambda e, j=j, hc=hc, q=q: e.matmul(
                                m1p[j % 2][:, :],
                                w1s[:, hc * FF + j * 128: hc * FF + j * 128 + 128],
                                ln1c[q % 2][:, hc * 512:(hc + 1) * 512],
                                start=(hc == 0), stop=(hc == 7)),
                                waits=(w if hc == 0 else []),
                                incs=([(pe1, 1)] if hc == 7 else []))
                        pe1_cnt[0] += 1
                        wg = [(pe1, pe1_cnt[0])]
                        if gidx1 == 256 * s:
                            wg.append(E_b1e[s])
                        E_gelu[gidx1] = prS.add("scalar", lambda e, j=j, s=s: e.activation(
                            out=hmidT[:, j * 512:(j + 1) * 512],
                            in_=m1p[j % 2][:, :], func=AF.Gelu,
                            bias=b1rT[:, j:j + 1]),
                            waits=wg)

                    def load_evac_edge(load_g):
                        g2 = load_g - 2
                        s_, rem = divmod(g2, 32)
                        tc_, ts_ = divmod(rem, 4)
                        return E_evac[s_ * 64 + tc_ * 8 + ts_ * 2 + 1]

                    def emit_ld(ts):
                        gidx = tc * 4 + ts
                        gall = s * 32 + gidx
                        row0 = gidx * 128
                        w = []
                        if gall >= 2:
                            w.append(load_evac_edge(gall))
                        if s == 0:
                            if gidx == 0:
                                w += [(dXO[j], 16 * (NTILE // 4))
                                      for j in range(4)] + list(R_EDGES)
                            prS.add("sync", lambda e, row0=row0, gidx=gidx:
                                    e.dma_start(out=x1c[gidx % 2][:, :],
                                                in_=x1s_d.ap()[row0:row0 + 128, :]),
                                    waits=w, incs=[(dX1[gidx % 2], 16)], isdma=True)
                        else:
                            if gidx == 0:
                                w += [(dY0[0], 16 * 16), (dY0[1], 16 * 16)]
                            prS.add("sync", lambda e, row0=row0, gidx=gidx:
                                    e.dma_start(out=x1c[gidx % 2][:, :],
                                                in_=y0x_d.ap()[row0:row0 + 128, :]),
                                    waits=w, incs=[(dY0r[gidx % 2], 16)], isdma=True)
                    emit_ld(0)
                    emit_ld(1)

                    for ts in range(4):
                        for dh in range(2):
                            g = ts * 2 + dh
                            G = s * 64 + tc * 8 + g
                            w = []
                            if g == 0:
                                w.append(E_gelu[pe1_cnt[0] - 1])
                            if tc == 0 and g == 0:
                                w.append((gW2, GW2ALL))
                            if G >= 4:
                                w.append(E_evac[G - 4])
                            for fc in range(32):
                                wf = list(w) if fc == 0 else []
                                last = (fc == 31 and s == 0)
                                prS.add("tensor",
                                        lambda e, ts=ts, dh=dh, fc=fc, G=G, last=last:
                                        e.matmul(
                                    m2p[G % 4][:, :],
                                    hmidT[:, fc * 512 + ts * 128: fc * 512 + ts * 128 + 128],
                                    w2s[:, fc * D + dh * 512: fc * D + dh * 512 + 512],
                                    start=(fc == 0), stop=last),
                                    waits=wf,
                                    incs=([(pe2, 1)] if last else []))
                            if s == 1:
                                prS.add("tensor", lambda e, dh=dh, G=G: e.matmul(
                                    m2p[G % 4][:, :], onesrb[0:1, :],
                                    cb2c_bf[0:1, dh * 512: dh * 512 + 512],
                                    start=False, stop=True), incs=[(pe2, 1)])
                            pe2_cnt[0] += 1

                        gidx = tc * 4 + ts
                        gall = s * 32 + gidx
                        row0 = gidx * 128
                        csrc = c0_128 if s == 0 else c1_128
                        dsem = (dX1 if s == 0 else dY0r)[gidx % 2]
                        dcnt = 16 * (gidx // 2 + 1)
                        ost = ostage[gall % 2]
                        for dh in range(2):
                            G = s * 64 + tc * 8 + ts * 2 + dh
                            w = [(pe2, G + 1), (dsem, dcnt)]
                            if dh == 0 and gall >= 2:
                                if s == 0:
                                    w.append((dY0[gidx % 2], 16 * (gidx // 2)))
                                elif gidx >= 2:
                                    w.append((dOUT[gidx % 2], 16 * (gidx // 2)))
                                else:
                                    w.append((dY0[gidx % 2], 16 * 16))
                            ed = prS.add("vector", lambda e, dh=dh, G=G, csrc=csrc,
                                         gidx=gidx, ost=ost:
                                         e.scalar_tensor_tensor(
                                             ost[:, dh * 512:(dh + 1) * 512],
                                             m2p[G % 4][:, :], csrc[:, 0:1],
                                             x1c[gidx % 2][:, dh * 512:(dh + 1) * 512],
                                             op0=ALU.mult, op1=ALU.add),
                                         waits=w)
                            E_evac[G] = ed
                        gG = s * 64 + tc * 8 + ts * 2
                        if s == 0:
                            prS.add("sync", lambda e, row0=row0, ost=ost: e.dma_start(
                                out=y0x_d.ap()[row0:row0 + 128, :], in_=ost[:, :]),
                                waits=[E_evac[gG + 1]],
                                incs=[(dY0[gidx % 2], 16)], isdma=True)
                        else:
                            prS.add("sync", lambda e, row0=row0, ost=ost: e.dma_start(
                                out=out.ap()[row0:row0 + 128, :], in_=ost[:, :]),
                                waits=[E_evac[gG + 1]],
                                incs=[(dOUT[gidx % 2], 16)], isdma=True)
                        if ts < 2:
                            emit_ld(ts + 2)

                if s == 0:
                    prS.add("tensor", lambda e: e.nop(),
                            waits=[(pe2, pe2_cnt[0])], incs=[(peS, 1)])

            prS.add("sync", lambda e: e.nop(),
                    waits=[(dOUT[0], 16 * 16), (dOUT[1], 16 * 16)])

            with nc.Block() as blkS:
                prS.realize(blkS)

    return nc


_NC_CACHE = {}


def kernel(**inputs):
    x = np.ascontiguousarray(np.asarray(inputs["L0_attn_x"], dtype=np.float32))
    tp = lambda a: np.ascontiguousarray(np.asarray(a, dtype=np.float32).T)
    cc = lambda a: np.ascontiguousarray(np.asarray(a, dtype=np.float32))
    wr = lambda a: np.ascontiguousarray(
        np.asarray(a, dtype=np.float32).T.reshape(8, 128, D)
        .transpose(1, 0, 2).reshape(128, 8 * D))
    shared = {
        "qwT": wr(inputs["L0_attn_qw"]), "kwT": wr(inputs["L0_attn_kw"]),
        "vwT": wr(inputs["L0_attn_vw"]), "owT": wr(inputs["L0_attn_ow"]),
        "qb": cc(inputs["L0_attn_qb"]), "vb": cc(inputs["L0_attn_vb"]),
        "ob": cc(inputs["L0_attn_ob"]),
        "lna": cc(inputs["L0_attn_lna_w"]), "lnb": cc(inputs["L0_attn_lnb_w"]),
        "lnc": cc(inputs["L0_attn_lnc_w"]), "lnd": cc(inputs["L0_attn_lnd_w"]),
        "lqwT": tp(inputs["L0_attn_lqw"]), "lkwT": tp(inputs["L0_attn_lkw"]),
        "lvwT": tp(inputs["L0_attn_lvw"]),
        "lqb": cc(inputs["L0_attn_lqb"]), "lkb": cc(inputs["L0_attn_lkb"]),
        "lvb": cc(inputs["L0_attn_lvb"]),
        "gate_w": cc(inputs["L0_ffn_gate_w"]),
        "w1": cc(inputs["L0_ffn_w1"]), "b1": cc(inputs["L0_ffn_b1"]),
        "w2": cc(inputs["L0_ffn_w2"]), "b2": cc(inputs["L0_ffn_b2"]),
    }
    x16_full = x[:, :16, :]
    in_maps = []
    for c in range(8):
        rot = np.ascontiguousarray(
            np.concatenate([x16_full[(c + i) % 8] for i in range(8)], axis=0))
        m = dict(shared)
        m["xb"] = np.ascontiguousarray(x[c])
        m["x16"] = rot
        in_maps.append(m)

    if "nc" not in _NC_CACHE:
        _NC_CACHE["nc"] = build_nc()
    nc = _NC_CACHE["nc"]
    res = run_bass_kernel_spmd(nc, in_maps, core_ids=list(range(8)))
    outp = np.stack([np.asarray(res.results[c]["out"], dtype=np.float32)
                     for c in range(8)], axis=0)
    return outp


# version 20
# speedup vs baseline: 1.0086x; 1.0086x over previous
import numpy as np
import ml_dtypes
from contextlib import ExitStack

import concourse.bass as bass
import concourse.mybir as mybir
import bass_rust
from concourse.bass_utils import run_bass_kernel_spmd
from concourse.alu_op_type import AluOpType as ALU

F32 = mybir.dt.float32
BF16 = mybir.dt.bfloat16
U32 = mybir.dt.uint32
I32 = mybir.dt.int32
AF = mybir.ActivationFunctionType
AX = mybir.AxisListType.X
bf = ml_dtypes.bfloat16

B, T, D, H, HD, FF, E = 8, 4096, 1024, 16, 64, 4096, 8
EPS = 1e-5
NTILE = T // 128
NTC = 8
DIFF_DEN = float(B * H * T * HD)

ENGS = ("sync", "tensor", "vector", "scalar", "gpsimd")
SELF_ENGS = ("vector", "scalar", "gpsimd")


class Prog:
    def __init__(self, selfsems, selfcnt):
        self.items = {e: [] for e in ENGS}
        self.selfsems = selfsems
        self.selfcnt = selfcnt

    def add(self, eng, fn, waits=(), incs=(), isdma=False):
        incs = list(incs)
        if eng in SELF_ENGS and not isdma:
            assert not incs, "self-engines carry only the self inc"
            self.selfcnt[eng] += 1
            incs = [(self.selfsems[eng], 1)]
            edge = (self.selfsems[eng], self.selfcnt[eng])
        else:
            edge = None
        assert len(incs) <= 1
        self.items[eng].append((tuple(waits), fn, tuple(incs)))
        return edge

    def realize(self, block):
        decs = {"sync": block.sync, "tensor": block.tensor,
                "vector": block.vector, "scalar": block.scalar,
                "gpsimd": block.gpsimd}
        for name in ENGS:
            items = self.items[name]
            if not items:
                continue
            def body(eng, items=items):
                for waits, fn, incs in items:
                    for s, cnt in waits:
                        if cnt > 0:
                            eng.wait_ge(s, cnt)
                    inst = fn(eng)
                    for s, amt in incs:
                        inst.then_inc(s, amt)
            decs[name](body)


class Chain:
    def __init__(self, prog, sem, count=0, last_edge=None):
        self.prog = prog
        self.sem = sem
        self.count = count
        self.last_edge = last_edge

    def step(self, eng, fn, dma=False, extra_waits=()):
        waits = list(extra_waits)
        if self.last_edge is not None:
            waits.append(self.last_edge)
        if eng in SELF_ENGS and not dma:
            self.last_edge = self.prog.add(eng, fn, waits=waits)
        else:
            inc = 16 if dma else 1
            self.prog.add(eng, fn, waits=waits,
                          incs=[(self.sem, inc)], isdma=dma)
            self.count += inc
            self.last_edge = (self.sem, self.count)

    def pstep(self, eng, fn):
        self.prog.add(eng, fn, waits=(), incs=[(self.sem, 16)], isdma=True)
        self.count += 16

    def barrier(self):
        self.last_edge = (self.sem, self.count)


def build_nc():
    nc = bass.Bass("TRN2", target_bir_lowering=False)

    xb = nc.dram_tensor("xb", [T, D], F32, kind="ExternalInput")
    x16 = nc.dram_tensor("x16", [128, D], F32, kind="ExternalInput")
    qwT = nc.dram_tensor("qwT", [128, 8 * D], F32, kind="ExternalInput")
    kwT = nc.dram_tensor("kwT", [128, 8 * D], F32, kind="ExternalInput")
    vwT = nc.dram_tensor("vwT", [128, 8 * D], F32, kind="ExternalInput")
    owT = nc.dram_tensor("owT", [128, 8 * D], F32, kind="ExternalInput")
    qb = nc.dram_tensor("qb", [D], F32, kind="ExternalInput")
    vb = nc.dram_tensor("vb", [D], F32, kind="ExternalInput")
    ob = nc.dram_tensor("ob", [D], F32, kind="ExternalInput")
    lna = nc.dram_tensor("lna", [D], F32, kind="ExternalInput")
    lnb = nc.dram_tensor("lnb", [D], F32, kind="ExternalInput")
    lnc = nc.dram_tensor("lnc", [HD], F32, kind="ExternalInput")
    lnd = nc.dram_tensor("lnd", [HD], F32, kind="ExternalInput")
    lqwT = nc.dram_tensor("lqwT", [HD, HD], F32, kind="ExternalInput")
    lkwT = nc.dram_tensor("lkwT", [HD, HD], F32, kind="ExternalInput")
    lvwT = nc.dram_tensor("lvwT", [HD, HD], F32, kind="ExternalInput")
    lqb = nc.dram_tensor("lqb", [HD], F32, kind="ExternalInput")
    lkb = nc.dram_tensor("lkb", [HD], F32, kind="ExternalInput")
    lvb = nc.dram_tensor("lvb", [HD], F32, kind="ExternalInput")
    gate_w = nc.dram_tensor("gate_w", [D, E], F32, kind="ExternalInput")
    w1 = nc.dram_tensor("w1", [E, D, FF], F32, kind="ExternalInput")
    b1 = nc.dram_tensor("b1", [E, FF], F32, kind="ExternalInput")
    w2 = nc.dram_tensor("w2", [E, FF, D], F32, kind="ExternalInput")
    b2 = nc.dram_tensor("b2", [E, D], F32, kind="ExternalInput")

    out = nc.dram_tensor("out", [T, D], F32, kind="ExternalOutput")
    dbg = nc.dram_tensor("dbg", [1, 64], F32, kind="ExternalOutput")

    ln1T_d = nc.dram_tensor("ln1T_d", [8, 128, T], BF16)
    x1s_d = nc.dram_tensor("x1s_d", [T, D], F32)
    y0x_d = nc.dram_tensor("y0x_d", [T, D], F32)
    b1r_d = nc.dram_tensor("b1r_d", [2, FF], BF16)

    ident_f = nc.inline_tensor(np.eye(128, dtype=np.float32), name="ident_f")
    ident_b = nc.inline_tensor(np.eye(128, dtype=bf), name="ident_b")
    ones_r_f = nc.inline_tensor(np.ones((1, 128), np.float32), name="ones_r_f")
    ones_c_f = nc.inline_tensor(np.ones((128, 1), np.float32), name="ones_c_f")
    ones_r_b = nc.inline_tensor(np.ones((1, 128), bf), name="ones_r_b")
    ones_c_b = nc.inline_tensor(np.ones((128, 1), bf), name="ones_c_b")
    ones_rw_b = nc.inline_tensor(np.ones((1, 512), bf), name="ones_rw_b")
    iota8_c = nc.inline_tensor(np.arange(8, dtype=np.float32)[None, :], name="iota8_c")
    iw1_c = nc.inline_tensor(
        (np.arange(128)[:, None] + 128 * np.arange(8)[None, :]).astype(np.float32),
        name="iw1_c")
    iw2_c = nc.inline_tensor(
        (np.arange(128)[:, None] + 128 * np.arange(32)[None, :]).astype(np.float32),
        name="iw2_c")
    bm = np.full((128, 128), -1e30, np.float32)
    for g in range(8):
        bm[g * 16:(g + 1) * 16, g * 16:(g + 1) * 16] = 0.0
    mask_c = nc.inline_tensor(bm, name="mask_c")

    with ExitStack() as top:
        def sbuf(*a, **k):
            return top.enter_context(nc.sbuf_tensor(*a, **k))
        def sem(name):
            return top.enter_context(nc.semaphore(name))

        identf = sbuf("identf", [128, 128], F32)
        identb = sbuf("identb", [128, 128], BF16)
        onesrf = sbuf("onesrf", [1, 128], F32)
        onescf = sbuf("onescf", [128, 1], F32)
        onesrb = sbuf("onesrb", [1, 128], BF16)
        onescb = sbuf("onescb", [128, 1], BF16)
        onesrwb = sbuf("onesrwb", [1, 512], BF16)
        iota8 = sbuf("iota8", [1, 8], F32)
        iw1 = sbuf("iw1", [128, 8], F32)
        iw2 = sbuf("iw2", [128, 32], F32)
        maskt = sbuf("maskt", [128, 128], F32)
        epst = sbuf("epst", [128, 1], F32)
        c0_128 = sbuf("c0_128", [128, 1], F32)
        c1_128 = sbuf("c1_128", [128, 1], F32)
        idxw1 = sbuf("idxw1", [128, 16], U32)
        idxw2 = sbuf("idxw2", [128, 64], U32)
        cb2c_bf = sbuf("cb2c_bf", [1, D], BF16)
        dbg_t = sbuf("dbg_t", [1, 64], F32)

        mid = ExitStack()
        ob_bc = mid.enter_context(nc.sbuf_tensor("ob_bc", [128, D], F32))
        o16 = mid.enter_context(nc.sbuf_tensor("o16", [128, D], F32))
        seqr = mid.enter_context(nc.sbuf_tensor("seqr", [1, D], F32))

        sV = sem("sV"); sS = sem("sS"); sG = sem("sG")
        selfsems = {"vector": sV, "scalar": sS, "gpsimd": sG}
        selfcnt = {"vector": 0, "scalar": 0, "gpsimd": 0}

        ch = sem("ch")
        dX = [sem("dXa"), sem("dXb"), sem("dXc"), sem("dXd")]
        dXO = [sem("dXOa"), sem("dXOb"), sem("dXOc"), sem("dXOd")]
        dLT = [sem("dLTa"), sem("dLTb")]
        dL = [sem("dLa"), sem("dLb")]
        peB = sem("peB")
        dX1 = [sem("dX1a"), sem("dX1b")]
        dY0 = [sem("dY0a"), sem("dY0b")]
        dY0r = [sem("dY0ra"), sem("dY0rb")]
        dOUT = [sem("dOUTa"), sem("dOUTb")]
        gW1a = sem("gW1a"); gW1b = sem("gW1b"); gW2 = sem("gW2")
        p1p = sem("p1p"); pe1 = sem("pe1"); pe2 = sem("pe2")
        peS = sem("peS"); dB1 = sem("dB1")

        p0t = sem("p0t")
        dW = sem("dW")
        with ExitStack() as p0:
            def s0o(*a, **k):
                return p0.enter_context(nc.sbuf_tensor(*a, **k))
            def ps0o(*a, **k):
                return p0.enter_context(nc.psum_tensor(*a, **k))

            NB = 4
            xt = [s0o(f"xt{j}", [128, D], F32) for j in range(NB)]
            x1t = [s0o(f"x1t{j}", [128, D], F32) for j in range(NB)]
            lnbt = [s0o(f"lnbt{j}", [128, D], BF16) for j in range(NB)]
            stageT = [s0o("stageT0", [128, D], BF16),
                      s0o("stageT1", [128, D], BF16)]
            nmP = [s0o(f"nmP{j}", [128, 1], F32) for j in range(NB)]
            nmeP = [s0o(f"nmeP{j}", [128, 1], F32) for j in range(NB)]
            vsP = [s0o(f"vsP{j}", [128, 1], F32) for j in range(NB)]
            lnvP = [s0o(f"lnvP{j}", [128, 1], F32) for j in range(NB)]
            rstdP = [s0o(f"rstdP{j}", [128, 1], F32) for j in range(NB)]
            nmrP = [s0o(f"nmrP{j}", [128, 1], F32) for j in range(NB)]
            psT = ps0o("psT", [128, 1024], BF16)
            ps_seq = [ps0o("ps_seq0", [1, 512], F32),
                      ps0o("ps_seq1", [1, 512], F32)]

            pp0 = ExitStack()
            def s0(*a, **k):
                return pp0.enter_context(nc.sbuf_tensor(*a, **k))
            def ps0_(*a, **k):
                return pp0.enter_context(nc.psum_tensor(*a, **k))

            wT_t = s0("wT_t", [128, 8 * D], BF16)
            xt16 = s0("xt16", [128, D], F32)
            xin = s0("xin", [128, D], F32)
            t_ln = s0("t_ln", [128, D], BF16)
            qcur = s0("qcur", [128, D], F32)
            k16 = s0("k16", [128, D], F32)
            v16 = s0("v16", [128, D], F32)
            kn = s0("kn", [128, D], F32)
            vi = s0("vi", [128, D], F32)
            qpre = s0("qpre", [128, D], F32)
            qn = s0("qn", [128, D], F32)
            iterout = s0("iterout", [128, D], F32)
            prevattn = s0("prevattn", [128, D], F32)
            tmpb = s0("tmpb", [128, D], F32)
            sqscr = s0("sqscr", [128, D], F32)
            trT = s0("trT", [128, D], BF16)
            trH = s0("trH", [64, H * 128], F32)
            knT64 = s0("knT64", [64, H * 128], F32)
            EX = s0("EX", [128, H * 128], F32)
            lna_bc = s0("lna_bc", [128, D], F32)
            lnb_bc = s0("lnb_bc", [128, D], F32)
            lnc_bc = s0("lnc_bc", [128, HD], F32)
            lnd_bc = s0("lnd_bc", [128, HD], F32)
            qb_bc = s0("qb_bc", [128, D], F32)
            vb_bc = s0("vb_bc", [128, D], F32)
            lqb_bc = s0("lqb_bc", [128, HD], F32)
            lkb_bc = s0("lkb_bc", [128, HD], F32)
            lvb_bc = s0("lvb_bc", [128, HD], F32)
            lqw2 = s0("lqw2", [64, HD], F32)
            lkw2 = s0("lkw2", [64, HD], F32)
            lvw2 = s0("lvw2", [64, HD], F32)
            mask128 = s0("mask128", [128, 128], F32)
            nm = s0("nm", [128, 1], F32)
            vs = s0("vs", [128, 1], F32)
            lnv = s0("lnv", [128, 1], F32)
            rstd = s0("rstd", [128, 1], F32)
            nmr = s0("nmr", [128, 1], F32)
            nm16 = s0("nm16", [128, H], F32)
            v16s = s0("v16s", [128, H], F32)
            rstd16 = s0("rstd16", [128, H], F32)
            rowS = s0("rowS", [128, 1], F32)
            sc1 = s0("sc1", [1, 1], F32)
            diffv = s0("diffv", [1, 1], F32)
            tthr = s0("tthr", [1, 1], F32)
            brkr = s0("brkr", [1, 1], F32)
            brk = s0("brk", [1, 1], F32)
            done = s0("done", [1, 1], F32)
            ndone = s0("ndone", [1, 1], F32)
            d128 = s0("d128", [128, 1], F32)
            d128i = s0("d128i", [128, 1], I32)
            g128 = s0("g128", [128, 1], F32)

            ps_tr = ps0_("ps_tr", [128, 512], F32)
            ps_a = ps0_("ps_a", [128, 512], F32)
            ps_b = ps0_("ps_b", [128, 512], F32)
            ps_big = ps0_("ps_big", [128, 1024], F32)

            pr0 = Prog(selfsems, selfcnt)
            cur = {"v": None, "s": None, "pe": 0, "ch": 0}
            evE = {"tr": None, "a": None, "b": None, "big": None}

            def V(fn, extra=()):
                w = [e for e in ([cur["v"]] + list(extra)) if e is not None]
                cur["v"] = pr0.add("vector", fn, waits=w)
                return cur["v"]

            def Sc(fn, extra=()):
                w = [e for e in ([cur["s"]] + list(extra)) if e is not None]
                cur["s"] = pr0.add("scalar", fn, waits=w)
                return cur["s"]

            def PE(fn, extra=(), inc=True):
                w = [e for e in extra if e is not None]
                if inc:
                    cur["pe"] += 1
                    pr0.add("tensor", fn, waits=w, incs=[(p0t, 1)])
                    return (p0t, cur["pe"])
                pr0.add("tensor", fn, waits=w, incs=[])
                return None

            def SY(fn, extra=()):
                w = [e for e in extra if e is not None]
                pr0.add("sync", fn, waits=w, incs=[(ch, 16)], isdma=True)
                cur["ch"] += 16
                return (ch, cur["ch"])

            def ld(dst, srcap):
                SY(lambda e, d=dst, s=srcap: e.dma_start(out=d, in_=s))
            ld(identf[:, :], ident_f.ap())
            ld(identb[:, :], ident_b.ap())
            ld(onesrf[0:1, :], ones_r_f.ap())
            ld(onescf[:, :], ones_c_f.ap())
            ld(onesrb[0:1, :], ones_r_b.ap())
            ld(onescb[:, :], ones_c_b.ap())
            ld(onesrwb[0:1, :], ones_rw_b.ap())
            ld(iota8[0:1, :], iota8_c.ap())
            ld(iw1[:, :], iw1_c.ap())
            ld(iw2[:, :], iw2_c.ap())
            ld(mask128[:, :], mask_c.ap())
            ld(xt16[:, :], x16.ap())
            for bct, vec in ((ob_bc, ob), (lna_bc, lna), (lnb_bc, lnb),
                             (qb_bc, qb), (vb_bc, vb)):
                ld(bct[0:1, :], vec.ap().unsqueeze(0))
            for bct, vec in ((lnc_bc, lnc), (lnd_bc, lnd), (lqb_bc, lqb),
                             (lkb_bc, lkb), (lvb_bc, lvb)):
                ld(bct[0:1, :], vec.ap().unsqueeze(0))
            for dst, srct in ((lqw2, lqwT), (lkw2, lkwT), (lvw2, lvwT)):
                ld(dst[0:HD, :], srct.ap())
            LOADS = (ch, cur["ch"])
            dw_cnt = [0]

            def ldw(srct, extra=()):
                w = [e for e in extra if e is not None]
                pr0.add("gpsimd", lambda e, srct=srct: e.dma_start(
                    out=wT_t[:, :], in_=srct.ap()),
                    waits=w, incs=[(dW, 16)], isdma=True)
                dw_cnt[0] += 16
                return (dW, dw_cnt[0])

            ldq = ldw(qwT)

            V(lambda e: e.memset(epst[:, :], EPS), extra=[LOADS])
            V(lambda e: e.memset(done[:, :], 0.0))
            V(lambda e: e.memset(dbg_t[0:1, :], 0.0))
            for bct in (ob_bc, lna_bc, lnb_bc, qb_bc, vb_bc):
                for ih in range(2):
                    bank = ("a", ps_a) if ih == 0 else ("b", ps_b)
                    PE(lambda e, bct=bct, ih=ih, bank=bank: e.matmul(
                        bank[1][:, :], onesrf[0:1, :],
                        bct[0:1, ih * 512: ih * 512 + 512],
                        start=True, stop=True),
                        extra=[LOADS, evE[bank[0]]])
                    evE[bank[0]] = V(lambda e, bct=bct, ih=ih, bank=bank:
                                     e.tensor_copy(
                        bct[:, ih * 512: ih * 512 + 512], bank[1][:, :]),
                        extra=[(p0t, cur["pe"])])
            for bct in (lnc_bc, lnd_bc, lqb_bc, lkb_bc, lvb_bc):
                PE(lambda e, bct=bct: e.matmul(
                    ps_a[:, 0:HD], onesrf[0:1, :], bct[0:1, 0:HD],
                    start=True, stop=True), extra=[LOADS, evE["a"]])
                evE["a"] = V(lambda e, bct=bct: e.tensor_copy(
                    bct[:, :], ps_a[:, 0:HD]), extra=[(p0t, cur["pe"])])
            E_BC = evE["a"]
            E_OB_SET = cur["v"]

            def tr1024(src, dstT, src_edge=None):
                psb16 = ps_tr[:, :].bitcast(BF16)
                for j in range(8):
                    ee = [src_edge, evE["tr"]] if j == 0 else []
                    PE(lambda e, j=j: e.transpose(
                        psb16[:, j * 128:(j + 1) * 128],
                        src[:, j * 128:(j + 1) * 128], identb[:, :]),
                        extra=ee, inc=(j == 7))
                evE["tr"] = Sc(lambda e: e.activation(
                    out=dstT[:, :], in_=psb16[:, :], func=AF.Identity),
                    extra=[(p0t, cur["pe"])])
                return evE["tr"]

            def tr_heads(src, dstT64, src_edge=None):
                for g in range(4):
                    ee = [src_edge, evE["tr"]] if g == 0 else [evE["tr"]]
                    for j in range(4):
                        h = g * 4 + j
                        PE(lambda e, h=h, j=j: e.transpose(
                            ps_tr[0:64, j * 128:(j + 1) * 128],
                            src[:, h * 64:(h + 1) * 64], identf[:, :]),
                            extra=(ee if j == 0 else []), inc=(j == 3))
                    evE["tr"] = Sc(lambda e, g=g: e.activation(
                        out=dstT64[0:64, g * 512:(g + 1) * 512],
                        in_=ps_tr[0:64, 0:512], func=AF.Identity),
                        extra=[(p0t, cur["pe"])])
                return evE["tr"]

            def proj(srcT, dst, bias_bc, src_edge=None, w_edge=None):
                out_edges = []
                for ih in range(2):
                    bank = ("a", ps_a) if ih == 0 else ("b", ps_b)
                    for hc in range(8):
                        ee = ([src_edge, w_edge, evE[bank[0]]] if hc == 0 else [])
                        PE(lambda e, ih=ih, hc=hc, bank=bank: e.matmul(
                            bank[1][:, :], srcT[:, hc * 128:(hc + 1) * 128],
                            wT_t[:, hc * D + ih * 512: hc * D + ih * 512 + 512],
                            start=(hc == 0), stop=(hc == 7)), extra=ee,
                            inc=(hc == 7))
                    if bias_bc is None:
                        evE[bank[0]] = V(lambda e, ih=ih, bank=bank: e.tensor_copy(
                            dst[:, ih * 512: ih * 512 + 512], bank[1][:, :]),
                            extra=[(p0t, cur["pe"])])
                    else:
                        evE[bank[0]] = V(lambda e, ih=ih, bank=bank: e.tensor_tensor(
                            dst[:, ih * 512: ih * 512 + 512], bank[1][:, :],
                            bias_bc[:, ih * 512: ih * 512 + 512], op=ALU.add),
                            extra=[(p0t, cur["pe"])])
                return evE["b"]

            def headproj(srcT64, w2t, bias_bc, dst, src_edge=None):
                for bank_i in range(2):
                    bank = ("a", ps_a) if bank_i == 0 else ("b", ps_b)
                    for hh in range(8):
                        h = bank_i * 8 + hh
                        ee = ([src_edge, evE[bank[0]]] if hh == 0 else [])
                        PE(lambda e, h=h, hh=hh, bank=bank: e.matmul(
                            bank[1][:, hh * 64:(hh + 1) * 64],
                            srcT64[0:64, h * 128:(h + 1) * 128],
                            w2t[0:64, :],
                            start=True, stop=True), extra=ee, inc=(hh == 7))
                    evE[bank[0]] = V(lambda e, bank=bank, bank_i=bank_i:
                                     e.tensor_tensor(
                        dst[:, bank_i * 512:(bank_i + 1) * 512].rearrange(
                            "p (a b) -> p a b", b=HD),
                        bank[1][:, :].rearrange("p (a b) -> p a b", b=HD),
                        bias_bc[:, :].unsqueeze(1).broadcast_to((128, 8, HD)),
                        op=ALU.add),
                        extra=[(p0t, cur["pe"])])
                return evE["b"]

            def row_ln(src, dst, w_bc=None, src_edge=None):
                ev = V(lambda e: e.tensor_reduce(nm[:, :], src[:, :], axis=AX,
                                                 op=ALU.add, negate=True),
                       extra=[src_edge])
                V(lambda e: e.tensor_scalar(nm[:, :], nm[:, :], 1.0 / D,
                                            None, op0=ALU.mult))
                es = Sc(lambda e: e.activation(out=sqscr[:, :], in_=src[:, :],
                                               func=AF.Square, bias=nm[:, 0:1],
                                               accum_out=vs[:, 0:1]),
                        extra=[cur["v"], src_edge])
                Sc(lambda e: e.activation(out=lnv[:, :], in_=vs[:, :],
                                          func=AF.Ln, scale=1.0 / D,
                                          bias=epst[:, 0:1]))
                es = Sc(lambda e: e.activation(out=rstd[:, :], in_=lnv[:, :],
                                               func=AF.Exp, scale=-0.5))
                V(lambda e: e.tensor_tensor(nmr[:, :], nm[:, :],
                                            rstd[:, :], op=ALU.mult),
                  extra=[es])
                ed = Sc(lambda e: e.activation(out=dst[:, :], in_=src[:, :],
                                               func=AF.Identity,
                                               bias=nmr[:, 0:1],
                                               scale=rstd[:, 0:1]),
                        extra=[cur["v"]])
                if w_bc is not None:
                    ed = V(lambda e: e.tensor_tensor(dst[:, :], dst[:, :],
                                                     w_bc[:, :], op=ALU.mult),
                           extra=[ed])
                return ed

            def seg_ln(src, dst, w_bc, src_edge=None):
                V(lambda e: e.tensor_reduce(
                    nm16[:, :], src[:, :].rearrange("p (a b) -> p a b", b=HD),
                    axis=AX, op=ALU.add, negate=True), extra=[src_edge])
                V(lambda e: e.tensor_scalar(nm16[:, :], nm16[:, :],
                                            1.0 / HD, None, op0=ALU.mult))
                ev = V(lambda e: e.tensor_tensor(
                    dst[:, :].rearrange("p (a b) -> p a b", b=HD),
                    src[:, :].rearrange("p (a b) -> p a b", b=HD),
                    nm16[:, :].unsqueeze(2).broadcast_to((128, H, HD)),
                    op=ALU.add))
                es = Sc(lambda e: e.activation(out=sqscr[:, :], in_=dst[:, :],
                                               func=AF.Square), extra=[ev])
                V(lambda e: e.tensor_reduce(
                    v16s[:, :], sqscr[:, :].rearrange("p (a b) -> p a b", b=HD),
                    axis=AX, op=ALU.add), extra=[es])
                Sc(lambda e: e.activation(out=v16s[:, :], in_=v16s[:, :],
                                          func=AF.Ln, scale=1.0 / HD,
                                          bias=epst[:, 0:1]), extra=[cur["v"]])
                es = Sc(lambda e: e.activation(out=rstd16[:, :], in_=v16s[:, :],
                                               func=AF.Exp, scale=-0.5))
                V(lambda e: e.tensor_tensor(
                    dst[:, :].rearrange("p (a b) -> p a b", b=HD),
                    dst[:, :].rearrange("p (a b) -> p a b", b=HD),
                    rstd16[:, :].unsqueeze(2).broadcast_to((128, H, HD)),
                    op=ALU.mult), extra=[es])
                return V(lambda e: e.tensor_tensor(
                    dst[:, :].rearrange("p (a b) -> p a b", b=HD),
                    dst[:, :].rearrange("p (a b) -> p a b", b=HD),
                    w_bc[:, :].unsqueeze(1).broadcast_to((128, H, HD)),
                    op=ALU.mult))

            pr1 = pr0
            ORDER = list(range(1, NTILE)) + [0]
            E_v1 = {}; E_v7 = {}; E_a4 = {}; E_a3 = {}; E_v6 = {}; E_v4 = {}
            E_OB = [E_OB_SET]; E_O16 = [None]

            def emit_xin(pos, wextra=()):
                tile = ORDER[pos]
                pr1.add("sync", lambda e, tile=tile, pos=pos: e.dma_start(
                    out=xt[pos % NB][:, :],
                    in_=xb.ap()[tile * 128:(tile + 1) * 128, :]),
                    waits=wextra, incs=[(dX[pos % NB], 16)], isdma=True)

            def stageA(pos):
                tile = ORDER[pos]
                bb = pos % NB
                w = [(dX[bb], 16 * (pos // NB + 1))]
                if pos == 0:
                    w.append(E_OB[0])
                if pos >= NB:
                    w.append((dXO[bb], 16 * (pos // NB)))
                if tile == 0:
                    w.append(E_O16[0])
                E_v1[pos] = pr1.add("gpsimd", lambda e, bb=bb: e.tensor_tensor(
                    x1t[bb][:, :], xt[bb][:, :], ob_bc[:, :], op=ALU.add),
                    waits=w)
                if tile == 0:
                    E_v1[pos] = pr1.add("gpsimd", lambda e, bb=bb: e.tensor_tensor(
                        x1t[bb][0:16, :], x1t[bb][0:16, :], o16[0:16, :],
                        op=ALU.add), waits=[E_v1[pos]])
                e_v3 = pr1.add("vector", lambda e, bb=bb: e.tensor_reduce(
                    nmP[bb][:, :], x1t[bb][:, :], axis=AX, op=ALU.add,
                    negate=True), waits=[E_v1[pos]])
                E_v4[pos] = pr1.add("vector", lambda e, bb=bb: e.tensor_scalar(
                    nmeP[bb][:, :], nmP[bb][:, :], 1.0 / D, None, op0=ALU.mult),
                    waits=[e_v3])
                wa1 = [E_v4[pos]]
                if pos >= NB:
                    wa1.append((p1p, pos - NB + 1))
                e_a1 = pr1.add("scalar", lambda e, bb=bb: e.activation(
                    out=lnbt[bb][:, :], in_=x1t[bb][:, :], func=AF.Square,
                    bias=nmeP[bb][:, 0:1], accum_out=vsP[bb][:, 0:1]),
                    waits=wa1)
                e_a2 = pr1.add("scalar", lambda e, bb=bb: e.activation(
                    out=lnvP[bb][:, :], in_=vsP[bb][:, :], func=AF.Ln,
                    scale=1.0 / D, bias=epst[:, 0:1]), waits=[e_a1])
                E_a3[pos] = pr1.add("scalar", lambda e, bb=bb: e.activation(
                    out=rstdP[bb][:, :], in_=lnvP[bb][:, :], func=AF.Exp,
                    scale=-0.5), waits=[e_a2])

            def stageB1(pos):
                bb = pos % NB
                E_v6[pos] = pr1.add("vector", lambda e, bb=bb: e.tensor_tensor(
                    nmrP[bb][:, :], nmeP[bb][:, :], rstdP[bb][:, :],
                    op=ALU.mult), waits=[E_a3[pos]])
                E_a4[pos] = pr1.add("scalar", lambda e, bb=bb: e.activation(
                    out=lnbt[bb][:, :], in_=x1t[bb][:, :], func=AF.Identity,
                    bias=nmrP[bb][:, 0:1], scale=rstdP[bb][:, 0:1]),
                    waits=[E_v6[pos]])
                w = [E_a4[pos]]
                if pos >= 1:
                    w.append(E_v7[pos - 1])
                for blkj in range(8):
                    pr1.add("tensor", lambda e, bb=bb, j=blkj: e.transpose(
                        psT[:, j * 128:(j + 1) * 128],
                        lnbt[bb][:, j * 128:(j + 1) * 128], identb[:, :]),
                        waits=(w if blkj == 0 else []))
                pr1.add("tensor", lambda e, bb=bb, pos=pos: e.matmul(
                    ps_seq[0][0:1, :], onescb[:, 0:1], lnbt[bb][:, 0:512],
                    start=(pos == 0), stop=(pos == NTILE - 1)))
                pr1.add("tensor", lambda e, bb=bb, pos=pos: e.matmul(
                    ps_seq[1][0:1, :], onescb[:, 0:1], lnbt[bb][:, 512:1024],
                    start=(pos == 0), stop=(pos == NTILE - 1)), incs=[(p1p, 1)])

            def stageB2(pos):
                tile = ORDER[pos]
                bb = pos % NB
                w = [(p1p, pos + 1)]
                if pos >= 2:
                    w.append((dLT[pos % 2], 16 * (pos // 2)))
                E_v7[pos] = pr1.add("vector", lambda e, pos=pos: e.tensor_copy(
                    stageT[pos % 2][:, :], psT[:, :]), waits=w)
                pr1.add("scalar", lambda e, pos=pos, tile=tile: e.dma_start(
                    out=ln1T_d.ap()[:, :, tile * 128:(tile + 1) * 128].rearrange(
                        "a p t -> p a t"),
                    in_=stageT[pos % 2][:, :].rearrange("p (a t) -> p a t", a=8)),
                    waits=[E_v7[pos]], incs=[(dLT[pos % 2], 16)], isdma=True)
                if pos + NB < NTILE:
                    emit_xin(pos + NB, wextra=[E_v1[pos]])
                pr1.add("sync", lambda e, bb=bb, tile=tile: e.dma_start(
                    out=x1s_d.ap()[tile * 128:(tile + 1) * 128, :],
                    in_=x1t[bb][:, :]),
                    waits=[E_a4[pos]], incs=[(dXO[bb], 16)], isdma=True)

            def p1_gen():
                for j in range(NB):
                    emit_xin(j)
                stageA(0)
                stageA(1)
                yield
                for k in range(NTILE - 1):
                    stageB1(k)
                    if k + 2 <= NTILE - 2:
                        stageA(k + 2)
                    stageB2(k)
                    yield
            _gen = p1_gen()

            def p1tick(n=1):
                for _ in range(n):
                    try:
                        next(_gen)
                    except StopIteration:
                        break

            e_xin = row_ln(xt16, xin, src_edge=LOADS)
            e_aln = row_ln(xin, t_ln, lna_bc, src_edge=e_xin)
            e_trT = tr1024(t_ln, trT, src_edge=e_aln)
            e_q = proj(trT, qcur, qb_bc, src_edge=e_trT, w_edge=ldq)
            p1tick()
            ldk = ldw(kwT, extra=[(p0t, cur["pe"])])
            e_bln = row_ln(xin, t_ln, lnb_bc, src_edge=e_aln)
            p1tick()
            e_trT = tr1024(t_ln, trT, src_edge=e_bln)
            e_k = proj(trT, k16, None, src_edge=e_trT, w_edge=ldk)
            p1tick()
            ldv = ldw(vwT, extra=[(p0t, cur["pe"])])
            e_v = proj(trT, v16, vb_bc, src_edge=e_trT, w_edge=ldv)
            p1tick()
            e_trT = tr_heads(k16, trH, src_edge=e_k)
            e_kp = headproj(trH, lkw2, lkb_bc, qpre, src_edge=e_trT)
            e_kn = seg_ln(qpre, kn, lnd_bc, src_edge=e_kp)
            e_trT = tr_heads(v16, trH, src_edge=e_v)
            e_vi = headproj(trH, lvw2, lvb_bc, vi, src_edge=e_trT)
            e_knT = tr_heads(kn, knT64, src_edge=e_kn)
            p1tick()

            for it in range(3):
                e_trT = tr_heads(qcur, trH, src_edge=cur["v"])
                e_qp = headproj(trH, lqw2, lqb_bc, qpre, src_edge=e_trT)
                p1tick()
                e_qn = seg_ln(qpre, qn, lnc_bc, src_edge=e_qp)
                p1tick()
                e_trT = tr_heads(qn, trH, src_edge=e_qn)
                p1tick(2)
                for half in range(2):
                    for hh in range(8):
                        h = half * 8 + hh
                        ee = ([e_trT, e_knT, evE["big"]] if hh == 0 else [])
                        PE(lambda e, h=h, hh=hh: e.matmul(
                            ps_big[:, hh * 128:(hh + 1) * 128],
                            trH[0:64, h * 128:(h + 1) * 128],
                            knT64[0:64, h * 128:(h + 1) * 128],
                            start=True, stop=True), extra=ee, inc=(hh == 7))
                    for g4 in range(2):
                        evE["big"] = V(lambda e, half=half, g4=g4:
                                       e.scalar_tensor_tensor(
                            EX[:, half * 1024 + g4 * 512:
                               half * 1024 + g4 * 512 + 512].rearrange(
                                "p (a b) -> p a b", b=128),
                            ps_big[:, g4 * 512:(g4 + 1) * 512].rearrange(
                                "p (a b) -> p a b", b=128),
                            0.125,
                            mask128[:, :].unsqueeze(1).broadcast_to((128, 4, 128)),
                            op0=ALU.mult, op1=ALU.add),
                            extra=([(p0t, cur["pe"])] if g4 == 0 else []))
                p1tick(2)
                ev = V(lambda e: e.tensor_reduce(
                    nm[:, :], EX[:, :], axis=AX, op=ALU.max, negate=True))
                es = Sc(lambda e: e.activation(out=EX[:, :], in_=EX[:, :],
                                               func=AF.Exp, bias=nm[:, 0:1]),
                        extra=[ev])
                V(lambda e: e.tensor_reduce(
                    v16s[:, :], EX[:, :].rearrange("p (a b) -> p a b", b=128),
                    axis=AX, op=ALU.add), extra=[es])
                V(lambda e: e.reciprocal(rstd16[:, :], v16s[:, :]))
                e_P = V(lambda e: e.tensor_tensor(
                    EX[:, :].rearrange("p (a b) -> p a b", b=128),
                    EX[:, :].rearrange("p (a b) -> p a b", b=128),
                    rstd16[:, :].unsqueeze(2).broadcast_to((128, H, 128)),
                    op=ALU.mult))
                ev_pt = {}
                for g in range(4):
                    for j in range(4):
                        h = g * 4 + j
                        ee = []
                        if h == 0:
                            ee = [e_P, evE["big"]]
                        elif j == 0 and g >= 2:
                            ee = [ev_pt[g - 2]]
                        PE(lambda e, h=h, j=j, g=g: e.transpose(
                            ps_big[:, (g % 2) * 512 + j * 128:
                                   (g % 2) * 512 + j * 128 + 128],
                            EX[:, h * 128:(h + 1) * 128], identf[:, :]),
                            extra=ee, inc=(j == 3))
                    ev_pt[g] = Sc(lambda e, g=g: e.activation(
                        out=EX[:, g * 512:(g + 1) * 512],
                        in_=ps_big[:, (g % 2) * 512:(g % 2) * 512 + 512],
                        func=AF.Identity),
                        extra=[(p0t, cur["pe"])])
                evE["big"] = ev_pt[3]
                p1tick(2)
                for bank_i in range(2):
                    bank = ("a", ps_a) if bank_i == 0 else ("b", ps_b)
                    for hh in range(8):
                        h = bank_i * 8 + hh
                        ee = ([ev_pt[1] if bank_i == 0 else ev_pt[3],
                               e_vi, evE[bank[0]]] if hh == 0 else [])
                        PE(lambda e, h=h, hh=hh, bank=bank: e.matmul(
                            bank[1][:, hh * 64:(hh + 1) * 64],
                            EX[:, h * 128:(h + 1) * 128],
                            vi[:, h * 64:(h + 1) * 64], start=True, stop=True),
                            extra=ee, inc=(hh == 7))
                    evE[bank[0]] = Sc(lambda e, bank=bank, bank_i=bank_i:
                                      e.activation(
                        out=iterout[:, bank_i * 512:(bank_i + 1) * 512],
                        in_=bank[1][:, :], func=AF.Identity),
                        extra=[(p0t, cur["pe"])])

                if it == 0:
                    V(lambda e: e.tensor_copy(prevattn[:, :], iterout[:, :]),
                      extra=[evE["b"]])
                    V(lambda e: e.tensor_add(qcur[:, :], qcur[:, :],
                                             iterout[:, :]))
                else:
                    if it == 1:
                        V(lambda e: e.tensor_tensor(
                            tmpb[:, :], iterout[:, :], prevattn[:, :],
                            op=ALU.subtract), extra=[evE["b"]])
                        e_rs = V(lambda e: e.tensor_reduce(
                            rowS[:, :], tmpb[:, :], axis=AX, op=ALU.add,
                            apply_absolute_value=True))
                        pee = PE(lambda e: e.matmul(ps_tr[0:1, 0:1],
                                                    rowS[:, 0:1], onescf[:, 0:1],
                                                    start=True, stop=True),
                                 extra=[e_rs, evE["tr"]])
                        V(lambda e: e.tensor_copy(sc1[:, :], ps_tr[0:1, 0:1]),
                          extra=[pee])
                        evE["tr"] = cur["v"]
                        V(lambda e: e.tensor_scalar(
                            diffv[:, :], sc1[:, :], 1.0 / DIFF_DEN, None,
                            op0=ALU.mult))
                        V(lambda e: e.tensor_scalar(
                            tthr[:, :], diffv[:, :], 0.1, 0.01,
                            op0=ALU.mult, op1=ALU.add))
                        V(lambda e: e.tensor_tensor(
                            brkr[:, :], diffv[:, :], tthr[:, :], op=ALU.is_lt))
                        V(lambda e: e.tensor_scalar(
                            ndone[:, :], done[:, :], -1.0, 1.0,
                            op0=ALU.mult, op1=ALU.add))
                        V(lambda e: e.tensor_tensor(
                            brk[:, :], brkr[:, :], ndone[:, :], op=ALU.mult))
                    else:
                        V(lambda e: e.tensor_scalar(
                            ndone[:, :], done[:, :], -1.0, 1.0,
                            op0=ALU.mult, op1=ALU.add), extra=[evE["b"]])
                    pee = PE(lambda e: e.matmul(ps_tr[:, 1:2], onesrf[0:1, :],
                                                ndone[0:1, 0:1],
                                                start=True, stop=True),
                             extra=[cur["v"], evE["tr"]])
                    V(lambda e: e.tensor_copy(d128[:, :], ps_tr[:, 1:2]),
                      extra=[pee])
                    evE["tr"] = cur["v"]
                    V(lambda e: e.tensor_copy(d128i[:, :], d128[:, :]))
                    V(lambda e: e.copy_predicated(
                        prevattn[:, :], d128i[:, 0:1].broadcast_to((128, D)),
                        iterout[:, :]))
                    if it == 1:
                        V(lambda e: e.tensor_tensor(
                            done[:, :], done[:, :], brk[:, :], op=ALU.max))
                        V(lambda e: e.tensor_scalar(
                            ndone[:, :], done[:, :], -1.0, 1.0,
                            op0=ALU.mult, op1=ALU.add))
                        pee = PE(lambda e: e.matmul(
                            ps_tr[:, 2:3], onesrf[0:1, :], ndone[0:1, 0:1],
                            start=True, stop=True),
                            extra=[cur["v"], evE["tr"]])
                        V(lambda e: e.tensor_copy(g128[:, :], ps_tr[:, 2:3]),
                          extra=[pee])
                        evE["tr"] = cur["v"]
                        V(lambda e: e.tensor_scalar(
                            tmpb[:, :], iterout[:, :], g128[:, 0:1], None,
                            op0=ALU.mult))
                        V(lambda e: e.tensor_add(qcur[:, :], qcur[:, :],
                                                 tmpb[:, :]))

            ldo = ldw(owT, extra=[(p0t, cur["pe"])])
            e_pb = Sc(lambda e: e.activation(out=t_ln[:, :], in_=prevattn[:, :],
                                             func=AF.Identity),
                      extra=[cur["v"]])
            e_trT = tr1024(t_ln, trT, src_edge=e_pb)
            p1tick()
            E_O16[0] = proj(trT, o16, None, src_edge=e_trT, w_edge=ldo)
            V(lambda e: e.tensor_copy(dbg_t[0:1, 32:33], diffv[0:1, :]))
            V(lambda e: e.tensor_copy(dbg_t[0:1, 33:34], done[0:1, :]))

            p1tick(64)
            stageA(NTILE - 1)
            stageB1(NTILE - 1)
            stageB2(NTILE - 1)

            P0_EDGE = cur["v"]
            CH_P0 = cur["ch"]
            with nc.Block() as blk0:
                pr0.realize(blk0)
            pp0.close()

            with ExitStack() as rr:
                def sr(*a, **k):
                    return rr.enter_context(nc.sbuf_tensor(*a, **k))
                def psr(*a, **k):
                    return rr.enter_context(nc.psum_tensor(*a, **k))
                gate_t = sr("gate_t", [128, 8 * E], F32)
                b1_t = sr("b1_t", [8, FF], F32)
                b2_t = sr("b2_t", [8, D], F32)
                seqT = sr("seqT", [128, 8], F32)
                r8 = sr("r8", [8, 1], F32)
                lg8 = sr("lg8", [1, 8], F32)
                pr8 = sr("pr8", [1, 8], F32)
                pm8 = sr("pm8", [1, 8], F32)
                is1 = sr("is1", [1, 8], F32)
                is2 = sr("is2", [1, 8], F32)
                t8 = sr("t8", [1, 8], F32)
                scrow = sr("scrow", [1, 8], F32)
                m1 = sr("m1", [1, 1], F32)
                m2 = sr("m2", [1, 1], F32)
                e0v = sr("e0v", [1, 1], F32)
                e1v = sr("e1v", [1, 1], F32)
                c0v = sr("c0v", [1, 1], F32)
                c1v = sr("c1v", [1, 1], F32)
                s8 = sr("s8", [1, 1], F32)
                rec = sr("rec", [1, 1], F32)
                ebc = sr("ebc", [128, 4], F32)
                idxf = sr("idxf", [128, 32], F32)
                vt8 = sr("vt8", [8, 2], F32)
                b1row = sr("b1row", [1, FF], F32)
                b1row_bf = sr("b1row_bf", [1, FF], BF16)
                cb2r = sr("cb2r", [1, D], F32)
                ps_r = psr("ps_r", [128, 8], F32)
                ps_w = [psr("ps_w0", [1, 512], F32), psr("ps_w1", [1, 512], F32)]

                prR = Prog(selfsems, selfcnt)
                rpe = sem("rpe")
                curR = {"v": None, "s": None, "pe": 0, "ch": CH_P0}

                def RV(fn, extra=()):
                    w = [e for e in ([curR["v"]] + list(extra)) if e is not None]
                    curR["v"] = prR.add("vector", fn, waits=w)
                    return curR["v"]

                def RS(fn, extra=()):
                    w = [e for e in ([curR["s"]] + list(extra)) if e is not None]
                    curR["s"] = prR.add("scalar", fn, waits=w)
                    return curR["s"]

                def RPE(fn, extra=(), inc=True):
                    w = [e for e in extra if e is not None]
                    if inc:
                        curR["pe"] += 1
                        prR.add("tensor", fn, waits=w, incs=[(rpe, 1)])
                        return (rpe, curR["pe"])
                    prR.add("tensor", fn, waits=w, incs=[])
                    return None

                def RSY(fn, extra=()):
                    w = [e for e in extra if e is not None]
                    prR.add("sync", fn, waits=w, incs=[(ch, 16)], isdma=True)
                    curR["ch"] += 16
                    return (ch, curR["ch"])

                first_extra = [(p1p, NTILE), E_v7[NTILE - 1], E_a4[NTILE - 1],
                               (dLT[0], 16 * 16), (dLT[1], 16 * 16)] + [
                               (dXO[j], 16 * (NTILE // 4)) for j in range(4)]

                RSY(lambda e: e.dma_start(
                    out=gate_t[:, :].rearrange("p (hc e) -> p hc e", hc=8),
                    in_=gate_w.ap().rearrange("(hc p) e -> p hc e", p=128)),
                    extra=first_extra)
                RSY(lambda e: e.dma_start(out=b1_t[:, :], in_=b1.ap()))
                RSY(lambda e: e.dma_start(out=b2_t[:, :], in_=b2.ap()))
                LD_ALL = (ch, curR["ch"])

                ev1 = RV(lambda e: e.tensor_copy(seqr[0:1, 0:512],
                                                 ps_seq[0][0:1, :]),
                         extra=[(p1p, NTILE)])
                ev2 = RV(lambda e: e.tensor_copy(seqr[0:1, 512:1024],
                                                 ps_seq[1][0:1, :]))
                for hc in range(8):
                    RPE(lambda e, hc=hc: e.matmul(
                        ps_r[:, hc:hc + 1], seqr[0:1, hc * 128:(hc + 1) * 128],
                        identf[0:1, 0:1], start=True, stop=True),
                        extra=([ev2] if hc == 0 else []), inc=(hc == 7))
                evT = RV(lambda e: e.tensor_copy(seqT[:, :], ps_r[:, :]),
                         extra=[(rpe, curR["pe"])])
                for hc in range(8):
                    RPE(lambda e, hc=hc: e.matmul(
                        ps_r[0:8, 0:1], gate_t[:, hc * 8:(hc + 1) * 8],
                        seqT[:, hc:hc + 1], start=(hc == 0), stop=(hc == 7)),
                        extra=([evT, LD_ALL] if hc == 0 else []), inc=(hc == 7))
                evr = RV(lambda e: e.tensor_copy(r8[:, :], ps_r[0:8, 0:1]),
                         extra=[(rpe, curR["pe"])])
                RPE(lambda e: e.matmul(ps_r[0:1, 0:8], r8[:, 0:1],
                                       identf[0:8, 0:8], start=True, stop=True),
                    extra=[evr])
                RV(lambda e: e.tensor_copy(lg8[:, :], ps_r[0:1, 0:8]),
                   extra=[(rpe, curR["pe"])])
                RV(lambda e: e.tensor_scalar(lg8[:, :], lg8[:, :], 1.0 / T,
                                             None, op0=ALU.mult))
                RV(lambda e: e.tensor_reduce(m1[:, :], lg8[:, :], axis=AX,
                                             op=ALU.max, negate=True))
                RS(lambda e: e.activation(out=pr8[:, :], in_=lg8[:, :],
                                          func=AF.Exp, bias=m1[0:1, 0:1]),
                   extra=[curR["v"]])
                RV(lambda e: e.tensor_reduce(s8[:, :], pr8[:, :], axis=AX,
                                             op=ALU.add), extra=[curR["s"]])
                RV(lambda e: e.reciprocal(rec[:, :], s8[:, :]))
                RV(lambda e: e.tensor_scalar(pr8[:, :], pr8[:, :],
                                             rec[0:1, 0:1], None, op0=ALU.mult))
                RV(lambda e: e.tensor_reduce(m1[:, :], pr8[:, :], axis=AX,
                                             op=ALU.max))
                RV(lambda e: e.tensor_scalar(is1[:, :], pr8[:, :],
                                             m1[0:1, 0:1], None,
                                             op0=ALU.is_equal))
                RV(lambda e: e.scalar_tensor_tensor(
                    pm8[:, :], is1[:, :], -1e30, pr8[:, :],
                    op0=ALU.mult, op1=ALU.add))
                RV(lambda e: e.tensor_reduce(m2[:, :], pm8[:, :], axis=AX,
                                             op=ALU.max))
                RV(lambda e: e.tensor_scalar(is2[:, :], pm8[:, :],
                                             m2[0:1, 0:1], None,
                                             op0=ALU.is_equal))
                RV(lambda e: e.tensor_scalar(t8[:, :], is1[:, :], -1000.0,
                                             1000.0, op0=ALU.mult, op1=ALU.add))
                RV(lambda e: e.tensor_tensor(t8[:, :], t8[:, :], iota8[0:1, :],
                                             op=ALU.add))
                RV(lambda e: e.tensor_reduce(e0v[:, :], t8[:, :], axis=AX,
                                             op=ALU.min))
                RV(lambda e: e.tensor_scalar(t8[:, :], is2[:, :], -1000.0,
                                             1000.0, op0=ALU.mult, op1=ALU.add))
                RV(lambda e: e.tensor_tensor(t8[:, :], t8[:, :], iota8[0:1, :],
                                             op=ALU.add))
                RV(lambda e: e.tensor_reduce(e1v[:, :], t8[:, :], axis=AX,
                                             op=ALU.min))
                RV(lambda e: e.tensor_tensor(s8[:, :], m1[:, :], m2[:, :],
                                             op=ALU.add))
                RV(lambda e: e.reciprocal(rec[:, :], s8[:, :]))
                RV(lambda e: e.tensor_tensor(c0v[:, :], m1[:, :], rec[:, :],
                                             op=ALU.mult))
                E_C = RV(lambda e: e.tensor_tensor(c1v[:, :], m2[:, :],
                                                   rec[:, :], op=ALU.mult))
                for j, v in enumerate((e0v, e1v, c0v, c1v)):
                    RPE(lambda e, j=j, v=v: e.matmul(
                        ps_r[:, j:j + 1], onesrf[0:1, :], v[0:1, 0:1],
                        start=True, stop=True),
                        extra=([E_C] if j == 0 else []), inc=(j == 3))
                RV(lambda e: e.tensor_copy(ebc[:, :], ps_r[:, 0:4]),
                   extra=[(rpe, curR["pe"])])
                RV(lambda e: e.tensor_copy(c0_128[:, :], ebc[:, 2:3]))
                RV(lambda e: e.tensor_copy(c1_128[:, :], ebc[:, 3:4]))
                for si in range(2):
                    RV(lambda e, si=si: e.scalar_tensor_tensor(
                        idxf[:, 0:8], ebc[:, si:si + 1].broadcast_to((128, 8)),
                        1024.0, iw1[:, :], op0=ALU.mult, op1=ALU.add))
                    RV(lambda e, si=si: e.tensor_copy(
                        idxw1[:, si * 8:(si + 1) * 8], idxf[:, 0:8]))
                    RV(lambda e, si=si: e.scalar_tensor_tensor(
                        idxf[:, 0:32], ebc[:, si:si + 1].broadcast_to((128, 32)),
                        4096.0, iw2[:, :], op0=ALU.mult, op1=ALU.add))
                    RV(lambda e, si=si: e.tensor_copy(
                        idxw2[:, si * 32:(si + 1) * 32], idxf[:, 0:32]))
                eb1st = [None]
                for si, isx in ((0, is1), (1, is2)):
                    RPE(lambda e, isx=isx, si=si: e.matmul(
                        ps_r[0:8, 4 + si:5 + si], isx[0:1, :],
                        identf[0:1, 0:1], start=True, stop=True),
                        extra=[E_C])
                    evis = RV(lambda e, si=si: e.tensor_copy(
                        vt8[:, si:si + 1], ps_r[0:8, 4 + si:5 + si]),
                        extra=[(rpe, curR["pe"])])
                    pee = {}
                    evj = {}
                    for j in range(8):
                        we = [evis, LD_ALL] if j == 0 else []
                        if j >= 2:
                            we = we + [evj[j - 2]]
                        pee[j] = RPE(lambda e, j=j, si=si: e.matmul(
                            ps_w[j % 2][0:1, :], vt8[:, si:si + 1],
                            b1_t[:, j * 512:(j + 1) * 512],
                            start=True, stop=True), extra=we)
                        evj[j] = RV(lambda e, j=j: e.tensor_copy(
                            b1row[0:1, j * 512:(j + 1) * 512],
                            ps_w[j % 2][0:1, :]), extra=[pee[j]])
                    RV(lambda e: e.tensor_copy(b1row_bf[0:1, :], b1row[0:1, :]),
                       extra=([eb1st[0]] if si == 1 else []))
                    eb1st[0] = RSY(lambda e, si=si: e.dma_start(
                        out=b1r_d.ap()[si:si + 1, :], in_=b1row_bf[0:1, :]),
                        extra=[curR["v"]])
                RV(lambda e: e.tensor_scalar(scrow[:, :], is1[:, :],
                                             c0v[0:1, 0:1], None, op0=ALU.mult))
                RV(lambda e: e.scalar_tensor_tensor(
                    scrow[:, :], is2[:, :], c1v[0:1, 0:1], scrow[:, :],
                    op0=ALU.mult, op1=ALU.add))
                RPE(lambda e: e.matmul(ps_r[0:8, 6:7], scrow[0:1, :],
                                       identf[0:1, 0:1], start=True, stop=True),
                    extra=[curR["v"]])
                RV(lambda e: e.tensor_copy(vt8[:, 0:1], ps_r[0:8, 6:7]),
                   extra=[(rpe, curR["pe"])])
                for j in range(2):
                    pee = RPE(lambda e, j=j: e.matmul(
                        ps_w[j][0:1, :], vt8[:, 0:1],
                        b2_t[:, j * 512:(j + 1) * 512],
                        start=True, stop=True), extra=[curR["v"]])
                    RV(lambda e, j=j: e.tensor_copy(
                        cb2r[0:1, j * 512:(j + 1) * 512], ps_w[j][0:1, :]),
                        extra=[pee])
                RV(lambda e: e.reciprocal(rec[:, :], c1v[:, :]))
                RV(lambda e: e.tensor_scalar(cb2c_bf[0:1, :], cb2r[0:1, :],
                                             rec[0:1, 0:1], None, op0=ALU.mult))
                RV(lambda e: e.tensor_copy(dbg_t[0:1, 0:8], pr8[0:1, :]))
                RV(lambda e: e.tensor_copy(dbg_t[0:1, 8:16], lg8[0:1, :]))
                RV(lambda e: e.tensor_copy(dbg_t[0:1, 16:17], e0v[0:1, :]))
                RV(lambda e: e.tensor_copy(dbg_t[0:1, 17:18], e1v[0:1, :]))
                RV(lambda e: e.tensor_copy(dbg_t[0:1, 18:19], c0v[0:1, :]))
                RV(lambda e: e.tensor_copy(dbg_t[0:1, 19:20], c1v[0:1, :]))
                edbg = RSY(lambda e: e.dma_start(out=dbg.ap(),
                                                 in_=dbg_t[0:1, :]),
                           extra=[curR["v"]])
                prR.add("sync", lambda e: e.nop(), waits=[edbg])

                R_EDGES = [curR["v"], curR["s"], (rpe, curR["pe"]),
                           (ch, curR["ch"])]
                with nc.Block() as blkR:
                    prR.realize(blkR)

        mid.close()
        with ExitStack() as ss:
            def s2(*a, **k):
                return ss.enter_context(nc.sbuf_tensor(*a, **k))
            def ps2(*a, **k):
                return ss.enter_context(nc.psum_tensor(*a, **k))
            b1rs = s2("b1rs", [1, FF], BF16)
            b1rT = s2("b1rT", [128, 32], F32)
            w1s = s2("w1s", [128, 8 * FF], BF16)
            w2s = s2("w2s", [128, 32 * D], BF16)
            hmidT = s2("hmidT", [128, 32 * 512], BF16)
            ln1c = [s2("ln1c0", [128, 8 * 512], BF16),
                    s2("ln1c1", [128, 8 * 512], BF16)]
            x1c = [s2("x1c0", [128, D], F32), s2("x1c1", [128, D], F32)]
            ostage = [s2("ostage0", [128, D], F32), s2("ostage1", [128, D], F32)]
            ps_b1 = ps2("ps_b1", [128, 32], F32)
            m1p = [ps2("m1p0", [128, 512], F32), ps2("m1p1", [128, 512], F32)]
            m2p = [ps2("m2p0", [128, 512], F32), ps2("m2p1", [128, 512], F32),
                   ps2("m2p2", [128, 512], F32), ps2("m2p3", [128, 512], F32)]

            prS = Prog(selfsems, selfcnt)
            pe1_cnt = [0]
            pe2_cnt = [0]
            E_gelu = {}
            E_evac = {}
            dl_cnt = [0]

            E_b1e = {}
            for s in range(2):
                wb1 = list(R_EDGES) if s == 0 else [(peB, 1)]
                prS.add("sync", lambda e, s=s: e.dma_start(
                    out=b1rs[0:1, :], in_=b1r_d.ap()[s:s + 1, :]),
                    waits=wb1, incs=[(dB1, 16)], isdma=True)
                for fc in range(32):
                    w = [(dB1, 16 * (s + 1))] if fc == 0 else []
                    if fc == 0 and s == 1:
                        w.append(E_b1e[0])
                    prS.add("tensor", lambda e, fc=fc: e.matmul(
                        ps_b1[:, fc:fc + 1], b1rs[0:1, fc * 128:(fc + 1) * 128],
                        identb[0:1, 0:1], start=True, stop=True),
                        waits=w, incs=([(peB, 1)] if fc == 31 else []))
                wev = [(peB, s + 1)]
                if s == 1:
                    wev.append(E_gelu[255])
                E_b1e[s] = prS.add("vector", lambda e: e.tensor_copy(
                    b1rT[:, :], ps_b1[:, :]), waits=wev)

                gwaits = list(R_EDGES) if s == 0 else [(pe1, 256)]
                first = True
                for fh in range(2):
                    gsem = gW1a if fh == 0 else gW1b
                    for hc in range(8):
                        w = gwaits if first else []
                        first = False
                        prS.add("gpsimd", lambda e, s=s, fh=fh, hc=hc:
                                e.indirect_dma_start(
                                    w1s[:, hc * FF + fh * 2048: hc * FF + fh * 2048 + 2048],
                                    None,
                                    w1.ap().rearrange("e h f -> (e h) f"),
                                    bass_rust.IndirectOffsetOnAxis(
                                        idxw1[:, s * 8 + hc: s * 8 + hc + 1], 0),
                                    element_offset=fh * 2048),
                                waits=w, incs=[(gsem, 16)], isdma=True)
                for fc in range(32):
                    w = [(peS, 1)] if (fc == 0 and s == 1) else []
                    prS.add("gpsimd", lambda e, s=s, fc=fc: e.indirect_dma_start(
                        w2s[:, fc * D: (fc + 1) * D], None,
                        w2.ap().rearrange("e f d -> (e f) d"),
                        bass_rust.IndirectOffsetOnAxis(
                            idxw2[:, s * 32 + fc: s * 32 + fc + 1], 0),
                        element_offset=0),
                        waits=w, incs=[(gW2, 16)], isdma=True)
                GW1 = 16 * 8 * (s + 1)
                GW2ALL = 16 * 32 * (s + 1)

                for tc in range(NTC):
                    q = s * NTC + tc
                    w = []
                    if q == 0:
                        w += [(dLT[0], 16 * (NTILE // 2)),
                              (dLT[1], 16 * (NTILE // 2))] + list(R_EDGES)
                    if q >= 2:
                        w.append((pe1, 32 * (q - 1)))
                    prS.add("scalar", lambda e, tc=tc, q=q: e.dma_start(
                        out=ln1c[q % 2][:, :].rearrange("p (a t) -> p a t", a=8),
                        in_=ln1T_d.ap()[:, :, tc * 512:(tc + 1) * 512].rearrange(
                            "a p t -> p a t")),
                        waits=w, incs=[(dL[q % 2], 16)], isdma=True)
                    DL_W = (dL[q % 2], 16 * (q // 2 + 1))

                    for j in range(32):
                        gidx1 = pe1_cnt[0]
                        w = []
                        if j == 0:
                            w.append(DL_W)
                        if tc == 0:
                            if j == 0:
                                w.append((gW1a, GW1))
                            if j == 16:
                                w.append((gW1b, GW1))
                        if gidx1 >= 2:
                            w.append(E_gelu[gidx1 - 2])
                        for hc in range(8):
                            prS.add("tensor", lambda e, j=j, hc=hc, q=q: e.matmul(
                                m1p[j % 2][:, :],
                                w1s[:, hc * FF + j * 128: hc * FF + j * 128 + 128],
                                ln1c[q % 2][:, hc * 512:(hc + 1) * 512],
                                start=(hc == 0), stop=(hc == 7)),
                                waits=(w if hc == 0 else []),
                                incs=([(pe1, 1)] if hc == 7 else []))
                        pe1_cnt[0] += 1
                        wg = [(pe1, pe1_cnt[0])]
                        if gidx1 == 256 * s:
                            wg.append(E_b1e[s])
                        E_gelu[gidx1] = prS.add("scalar", lambda e, j=j, s=s: e.activation(
                            out=hmidT[:, j * 512:(j + 1) * 512],
                            in_=m1p[j % 2][:, :], func=AF.Gelu,
                            bias=b1rT[:, j:j + 1]),
                            waits=wg)

                    def load_evac_edge(load_g):
                        g2 = load_g - 2
                        s_, rem = divmod(g2, 32)
                        tc_, ts_ = divmod(rem, 4)
                        return E_evac[s_ * 64 + tc_ * 8 + ts_ * 2 + 1]

                    def emit_ld(ts):
                        gidx = tc * 4 + ts
                        gall = s * 32 + gidx
                        row0 = gidx * 128
                        w = []
                        if gall >= 2:
                            w.append(load_evac_edge(gall))
                        if s == 0:
                            if gidx == 0:
                                w += [(dXO[j], 16 * (NTILE // 4))
                                      for j in range(4)] + list(R_EDGES)
                            prS.add("sync", lambda e, row0=row0, gidx=gidx:
                                    e.dma_start(out=x1c[gidx % 2][:, :],
                                                in_=x1s_d.ap()[row0:row0 + 128, :]),
                                    waits=w, incs=[(dX1[gidx % 2], 16)], isdma=True)
                        else:
                            if gidx == 0:
                                w += [(dY0[0], 16 * 16), (dY0[1], 16 * 16)]
                            prS.add("sync", lambda e, row0=row0, gidx=gidx:
                                    e.dma_start(out=x1c[gidx % 2][:, :],
                                                in_=y0x_d.ap()[row0:row0 + 128, :]),
                                    waits=w, incs=[(dY0r[gidx % 2], 16)], isdma=True)
                    emit_ld(0)
                    emit_ld(1)

                    for ts in range(4):
                        for dh in range(2):
                            g = ts * 2 + dh
                            G = s * 64 + tc * 8 + g
                            w = []
                            if g == 0:
                                w.append(E_gelu[pe1_cnt[0] - 1])
                            if tc == 0 and g == 0:
                                w.append((gW2, GW2ALL))
                            if G >= 4:
                                w.append(E_evac[G - 4])
                            for fc in range(32):
                                wf = list(w) if fc == 0 else []
                                last = (fc == 31 and s == 0)
                                prS.add("tensor",
                                        lambda e, ts=ts, dh=dh, fc=fc, G=G, last=last:
                                        e.matmul(
                                    m2p[G % 4][:, :],
                                    hmidT[:, fc * 512 + ts * 128: fc * 512 + ts * 128 + 128],
                                    w2s[:, fc * D + dh * 512: fc * D + dh * 512 + 512],
                                    start=(fc == 0), stop=last),
                                    waits=wf,
                                    incs=([(pe2, 1)] if last else []))
                            if s == 1:
                                prS.add("tensor", lambda e, dh=dh, G=G: e.matmul(
                                    m2p[G % 4][:, :], onesrb[0:1, :],
                                    cb2c_bf[0:1, dh * 512: dh * 512 + 512],
                                    start=False, stop=True), incs=[(pe2, 1)])
                            pe2_cnt[0] += 1

                        gidx = tc * 4 + ts
                        gall = s * 32 + gidx
                        row0 = gidx * 128
                        csrc = c0_128 if s == 0 else c1_128
                        dsem = (dX1 if s == 0 else dY0r)[gidx % 2]
                        dcnt = 16 * (gidx // 2 + 1)
                        ost = ostage[gall % 2]
                        for dh in range(2):
                            G = s * 64 + tc * 8 + ts * 2 + dh
                            w = [(pe2, G + 1), (dsem, dcnt)]
                            if dh == 0 and gall >= 2:
                                if s == 0:
                                    w.append((dY0[gidx % 2], 16 * (gidx // 2)))
                                elif gidx >= 2:
                                    w.append((dOUT[gidx % 2], 16 * (gidx // 2)))
                                else:
                                    w.append((dY0[gidx % 2], 16 * 16))
                            ed = prS.add("vector", lambda e, dh=dh, G=G, csrc=csrc,
                                         gidx=gidx, ost=ost:
                                         e.scalar_tensor_tensor(
                                             ost[:, dh * 512:(dh + 1) * 512],
                                             m2p[G % 4][:, :], csrc[:, 0:1],
                                             x1c[gidx % 2][:, dh * 512:(dh + 1) * 512],
                                             op0=ALU.mult, op1=ALU.add),
                                         waits=w)
                            E_evac[G] = ed
                        gG = s * 64 + tc * 8 + ts * 2
                        if s == 0:
                            prS.add("sync", lambda e, row0=row0, ost=ost: e.dma_start(
                                out=y0x_d.ap()[row0:row0 + 128, :], in_=ost[:, :]),
                                waits=[E_evac[gG + 1]],
                                incs=[(dY0[gidx % 2], 16)], isdma=True)
                        else:
                            prS.add("sync", lambda e, row0=row0, ost=ost: e.dma_start(
                                out=out.ap()[row0:row0 + 128, :], in_=ost[:, :]),
                                waits=[E_evac[gG + 1]],
                                incs=[(dOUT[gidx % 2], 16)], isdma=True)
                        if ts < 2:
                            emit_ld(ts + 2)

                if s == 0:
                    prS.add("tensor", lambda e: e.nop(),
                            waits=[(pe2, pe2_cnt[0])], incs=[(peS, 1)])

            prS.add("sync", lambda e: e.nop(),
                    waits=[(dOUT[0], 16 * 16), (dOUT[1], 16 * 16)])

            with nc.Block() as blkS:
                prS.realize(blkS)

    return nc


_NC_CACHE = {}


def kernel(**inputs):
    x = np.ascontiguousarray(np.asarray(inputs["L0_attn_x"], dtype=np.float32))
    tp = lambda a: np.ascontiguousarray(np.asarray(a, dtype=np.float32).T)
    cc = lambda a: np.ascontiguousarray(np.asarray(a, dtype=np.float32))
    wr = lambda a: np.ascontiguousarray(
        np.asarray(a, dtype=np.float32).T.reshape(8, 128, D)
        .transpose(1, 0, 2).reshape(128, 8 * D))
    shared = {
        "qwT": wr(inputs["L0_attn_qw"]), "kwT": wr(inputs["L0_attn_kw"]),
        "vwT": wr(inputs["L0_attn_vw"]), "owT": wr(inputs["L0_attn_ow"]),
        "qb": cc(inputs["L0_attn_qb"]), "vb": cc(inputs["L0_attn_vb"]),
        "ob": cc(inputs["L0_attn_ob"]),
        "lna": cc(inputs["L0_attn_lna_w"]), "lnb": cc(inputs["L0_attn_lnb_w"]),
        "lnc": cc(inputs["L0_attn_lnc_w"]), "lnd": cc(inputs["L0_attn_lnd_w"]),
        "lqwT": tp(inputs["L0_attn_lqw"]), "lkwT": tp(inputs["L0_attn_lkw"]),
        "lvwT": tp(inputs["L0_attn_lvw"]),
        "lqb": cc(inputs["L0_attn_lqb"]), "lkb": cc(inputs["L0_attn_lkb"]),
        "lvb": cc(inputs["L0_attn_lvb"]),
        "gate_w": cc(inputs["L0_ffn_gate_w"]),
        "w1": cc(inputs["L0_ffn_w1"]), "b1": cc(inputs["L0_ffn_b1"]),
        "w2": cc(inputs["L0_ffn_w2"]), "b2": cc(inputs["L0_ffn_b2"]),
    }
    x16_full = x[:, :16, :]
    in_maps = []
    for c in range(8):
        rot = np.ascontiguousarray(
            np.concatenate([x16_full[(c + i) % 8] for i in range(8)], axis=0))
        m = dict(shared)
        m["xb"] = np.ascontiguousarray(x[c])
        m["x16"] = rot
        in_maps.append(m)

    if "nc" not in _NC_CACHE:
        _NC_CACHE["nc"] = build_nc()
    nc = _NC_CACHE["nc"]
    res = run_bass_kernel_spmd(nc, in_maps, core_ids=list(range(8)))
    outp = np.stack([np.asarray(res.results[c]["out"], dtype=np.float32)
                     for c in range(8)], axis=0)
    return outp


# version 21
# speedup vs baseline: 1.0203x; 1.0116x over previous
import numpy as np
import ml_dtypes
from contextlib import ExitStack

import concourse.bass as bass
import concourse.mybir as mybir
import bass_rust
from concourse.bass_utils import run_bass_kernel_spmd
from concourse.alu_op_type import AluOpType as ALU

F32 = mybir.dt.float32
BF16 = mybir.dt.bfloat16
U32 = mybir.dt.uint32
I32 = mybir.dt.int32
AF = mybir.ActivationFunctionType
AX = mybir.AxisListType.X
bf = ml_dtypes.bfloat16

B, T, D, H, HD, FF, E = 8, 4096, 1024, 16, 64, 4096, 8
EPS = 1e-5
NTILE = T // 128
NTC = 8
DIFF_DEN = float(B * H * T * HD)

ENGS = ("sync", "tensor", "vector", "scalar", "gpsimd")
SELF_ENGS = ("vector", "scalar", "gpsimd")


class Prog:
    def __init__(self, selfsems, selfcnt):
        self.items = {e: [] for e in ENGS}
        self.selfsems = selfsems
        self.selfcnt = selfcnt

    def add(self, eng, fn, waits=(), incs=(), isdma=False):
        incs = list(incs)
        if eng in SELF_ENGS and not isdma:
            assert not incs, "self-engines carry only the self inc"
            self.selfcnt[eng] += 1
            incs = [(self.selfsems[eng], 1)]
            edge = (self.selfsems[eng], self.selfcnt[eng])
        else:
            edge = None
        assert len(incs) <= 1
        self.items[eng].append((tuple(waits), fn, tuple(incs)))
        return edge

    def realize(self, block):
        decs = {"sync": block.sync, "tensor": block.tensor,
                "vector": block.vector, "scalar": block.scalar,
                "gpsimd": block.gpsimd}
        for name in ENGS:
            items = self.items[name]
            if not items:
                continue
            def body(eng, items=items):
                for waits, fn, incs in items:
                    for s, cnt in waits:
                        if cnt > 0:
                            eng.wait_ge(s, cnt)
                    inst = fn(eng)
                    for s, amt in incs:
                        inst.then_inc(s, amt)
            decs[name](body)


class Chain:
    def __init__(self, prog, sem, count=0, last_edge=None):
        self.prog = prog
        self.sem = sem
        self.count = count
        self.last_edge = last_edge

    def step(self, eng, fn, dma=False, extra_waits=()):
        waits = list(extra_waits)
        if self.last_edge is not None:
            waits.append(self.last_edge)
        if eng in SELF_ENGS and not dma:
            self.last_edge = self.prog.add(eng, fn, waits=waits)
        else:
            inc = 16 if dma else 1
            self.prog.add(eng, fn, waits=waits,
                          incs=[(self.sem, inc)], isdma=dma)
            self.count += inc
            self.last_edge = (self.sem, self.count)

    def pstep(self, eng, fn):
        self.prog.add(eng, fn, waits=(), incs=[(self.sem, 16)], isdma=True)
        self.count += 16

    def barrier(self):
        self.last_edge = (self.sem, self.count)


def build_nc():
    nc = bass.Bass("TRN2", target_bir_lowering=False)

    xb = nc.dram_tensor("xb", [T, D], F32, kind="ExternalInput")
    x16 = nc.dram_tensor("x16", [128, D], F32, kind="ExternalInput")
    qwT = nc.dram_tensor("qwT", [128, 8 * D], F32, kind="ExternalInput")
    kwT = nc.dram_tensor("kwT", [128, 8 * D], F32, kind="ExternalInput")
    vwT = nc.dram_tensor("vwT", [128, 8 * D], F32, kind="ExternalInput")
    owT = nc.dram_tensor("owT", [128, 8 * D], F32, kind="ExternalInput")
    qb = nc.dram_tensor("qb", [D], F32, kind="ExternalInput")
    vb = nc.dram_tensor("vb", [D], F32, kind="ExternalInput")
    ob = nc.dram_tensor("ob", [D], F32, kind="ExternalInput")
    lna = nc.dram_tensor("lna", [D], F32, kind="ExternalInput")
    lnb = nc.dram_tensor("lnb", [D], F32, kind="ExternalInput")
    lnc = nc.dram_tensor("lnc", [HD], F32, kind="ExternalInput")
    lnd = nc.dram_tensor("lnd", [HD], F32, kind="ExternalInput")
    lqwT = nc.dram_tensor("lqwT", [HD, HD], F32, kind="ExternalInput")
    lkwT = nc.dram_tensor("lkwT", [HD, HD], F32, kind="ExternalInput")
    lvwT = nc.dram_tensor("lvwT", [HD, HD], F32, kind="ExternalInput")
    lqb = nc.dram_tensor("lqb", [HD], F32, kind="ExternalInput")
    lkb = nc.dram_tensor("lkb", [HD], F32, kind="ExternalInput")
    lvb = nc.dram_tensor("lvb", [HD], F32, kind="ExternalInput")
    gate_w = nc.dram_tensor("gate_w", [D, E], F32, kind="ExternalInput")
    w1 = nc.dram_tensor("w1", [E, D, FF], F32, kind="ExternalInput")
    b1 = nc.dram_tensor("b1", [E, FF], F32, kind="ExternalInput")
    w2 = nc.dram_tensor("w2", [E, FF, D], F32, kind="ExternalInput")
    b2 = nc.dram_tensor("b2", [E, D], F32, kind="ExternalInput")

    out = nc.dram_tensor("out", [T, D], F32, kind="ExternalOutput")
    dbg = nc.dram_tensor("dbg", [1, 64], F32, kind="ExternalOutput")

    ln1T_d = nc.dram_tensor("ln1T_d", [8, 128, T], BF16)
    x1s_d = nc.dram_tensor("x1s_d", [T, D], F32)
    y0x_d = nc.dram_tensor("y0x_d", [T, D], F32)
    b1r_d = nc.dram_tensor("b1r_d", [2, FF], BF16)

    ident_f = nc.inline_tensor(np.eye(128, dtype=np.float32), name="ident_f")
    ident_b = nc.inline_tensor(np.eye(128, dtype=bf), name="ident_b")
    ones_r_f = nc.inline_tensor(np.ones((1, 128), np.float32), name="ones_r_f")
    ones_c_f = nc.inline_tensor(np.ones((128, 1), np.float32), name="ones_c_f")
    ones_r_b = nc.inline_tensor(np.ones((1, 128), bf), name="ones_r_b")
    ones_c_b = nc.inline_tensor(np.ones((128, 1), bf), name="ones_c_b")
    ones_rw_b = nc.inline_tensor(np.ones((1, 512), bf), name="ones_rw_b")
    iota8_c = nc.inline_tensor(np.arange(8, dtype=np.float32)[None, :], name="iota8_c")
    iw1_c = nc.inline_tensor(
        (np.arange(128)[:, None] + 128 * np.arange(8)[None, :]).astype(np.float32),
        name="iw1_c")
    iw2_c = nc.inline_tensor(
        (np.arange(128)[:, None] + 128 * np.arange(32)[None, :]).astype(np.float32),
        name="iw2_c")
    bm = np.full((128, 128), -1e30, np.float32)
    for g in range(8):
        bm[g * 16:(g + 1) * 16, g * 16:(g + 1) * 16] = 0.0
    mask_c = nc.inline_tensor(bm, name="mask_c")

    with ExitStack() as top:
        def sbuf(*a, **k):
            return top.enter_context(nc.sbuf_tensor(*a, **k))
        def sem(name):
            return top.enter_context(nc.semaphore(name))

        identf = sbuf("identf", [128, 128], F32)
        identb = sbuf("identb", [128, 128], BF16)
        onesrf = sbuf("onesrf", [1, 128], F32)
        onescf = sbuf("onescf", [128, 1], F32)
        onesrb = sbuf("onesrb", [1, 128], BF16)
        onescb = sbuf("onescb", [128, 1], BF16)
        onesrwb = sbuf("onesrwb", [1, 512], BF16)
        iota8 = sbuf("iota8", [1, 8], F32)
        iw1 = sbuf("iw1", [128, 8], F32)
        iw2 = sbuf("iw2", [128, 32], F32)
        maskt = sbuf("maskt", [128, 128], F32)
        epst = sbuf("epst", [128, 1], F32)
        c0_128 = sbuf("c0_128", [128, 1], F32)
        c1_128 = sbuf("c1_128", [128, 1], F32)
        idxw1 = sbuf("idxw1", [128, 16], U32)
        idxw2 = sbuf("idxw2", [128, 64], U32)
        cb2c_bf = sbuf("cb2c_bf", [1, D], BF16)
        dbg_t = sbuf("dbg_t", [1, 64], F32)

        mid = ExitStack()
        ob_bc = mid.enter_context(nc.sbuf_tensor("ob_bc", [128, D], F32))
        o16 = mid.enter_context(nc.sbuf_tensor("o16", [128, D], F32))
        seqr = mid.enter_context(nc.sbuf_tensor("seqr", [1, D], F32))

        sV = sem("sV"); sS = sem("sS"); sG = sem("sG")
        selfsems = {"vector": sV, "scalar": sS, "gpsimd": sG}
        selfcnt = {"vector": 0, "scalar": 0, "gpsimd": 0}

        ch = sem("ch")
        dX = [sem("dXa"), sem("dXb"), sem("dXc"), sem("dXd")]
        dXO = [sem("dXOa"), sem("dXOb"), sem("dXOc"), sem("dXOd")]
        dLT = [sem("dLTa"), sem("dLTb")]
        dL = [sem("dLa"), sem("dLb")]
        peB = sem("peB")
        dX1 = [sem("dX1a"), sem("dX1b")]
        dY0 = [sem("dY0a"), sem("dY0b")]
        dY0r = [sem("dY0ra"), sem("dY0rb")]
        dOUT = [sem("dOUTa"), sem("dOUTb")]
        gW1a = sem("gW1a"); gW1b = sem("gW1b"); gW2 = sem("gW2")
        p1p = sem("p1p"); pe1 = sem("pe1"); pe2 = sem("pe2")
        peS = sem("peS"); dB1 = sem("dB1")

        p0t = sem("p0t")
        dW = sem("dW")
        with ExitStack() as p0:
            def s0o(*a, **k):
                return p0.enter_context(nc.sbuf_tensor(*a, **k))
            def ps0o(*a, **k):
                return p0.enter_context(nc.psum_tensor(*a, **k))

            NB = 4
            xt = [s0o(f"xt{j}", [128, D], F32) for j in range(NB)]
            x1t = [s0o(f"x1t{j}", [128, D], F32) for j in range(NB)]
            lnbt = [s0o(f"lnbt{j}", [128, D], BF16) for j in range(NB)]
            stageT = [s0o("stageT0", [128, D], BF16),
                      s0o("stageT1", [128, D], BF16)]
            nmP = [s0o(f"nmP{j}", [128, 1], F32) for j in range(NB)]
            nmeP = [s0o(f"nmeP{j}", [128, 1], F32) for j in range(NB)]
            vsP = [s0o(f"vsP{j}", [128, 1], F32) for j in range(NB)]
            lnvP = [s0o(f"lnvP{j}", [128, 1], F32) for j in range(NB)]
            rstdP = [s0o(f"rstdP{j}", [128, 1], F32) for j in range(NB)]
            nmrP = [s0o(f"nmrP{j}", [128, 1], F32) for j in range(NB)]
            psT = ps0o("psT", [128, 1024], BF16)
            ps_seq = [ps0o("ps_seq0", [1, 512], F32),
                      ps0o("ps_seq1", [1, 512], F32)]

            pp0 = ExitStack()
            def s0(*a, **k):
                return pp0.enter_context(nc.sbuf_tensor(*a, **k))
            def ps0_(*a, **k):
                return pp0.enter_context(nc.psum_tensor(*a, **k))

            wT_t = s0("wT_t", [128, 8 * D], BF16)
            xt16 = s0("xt16", [128, D], F32)
            xin = s0("xin", [128, D], F32)
            t_ln = s0("t_ln", [128, D], BF16)
            qcur = s0("qcur", [128, D], F32)
            k16 = s0("k16", [128, D], F32)
            v16 = s0("v16", [128, D], F32)
            kn = s0("kn", [128, D], F32)
            vi = s0("vi", [128, D], F32)
            qpre = s0("qpre", [128, D], F32)
            qn = s0("qn", [128, D], F32)
            iterout = s0("iterout", [128, D], F32)
            prevattn = s0("prevattn", [128, D], F32)
            tmpb = s0("tmpb", [128, D], F32)
            sqscr = s0("sqscr", [128, D], F32)
            trT = s0("trT", [128, D], BF16)
            trH = s0("trH", [64, H * 128], F32)
            knT64 = s0("knT64", [64, H * 128], F32)
            EX = s0("EX", [128, H * 128], F32)
            lna_bc = s0("lna_bc", [128, D], F32)
            lnb_bc = s0("lnb_bc", [128, D], F32)
            lnc_bc = s0("lnc_bc", [128, HD], F32)
            lnd_bc = s0("lnd_bc", [128, HD], F32)
            qb_bc = s0("qb_bc", [128, D], F32)
            vb_bc = s0("vb_bc", [128, D], F32)
            lqb_bc = s0("lqb_bc", [128, HD], F32)
            lkb_bc = s0("lkb_bc", [128, HD], F32)
            lvb_bc = s0("lvb_bc", [128, HD], F32)
            lqw2 = s0("lqw2", [64, HD], F32)
            lkw2 = s0("lkw2", [64, HD], F32)
            lvw2 = s0("lvw2", [64, HD], F32)
            mask128 = s0("mask128", [128, 128], F32)
            nm = s0("nm", [128, 1], F32)
            vs = s0("vs", [128, 1], F32)
            lnv = s0("lnv", [128, 1], F32)
            rstd = s0("rstd", [128, 1], F32)
            nmr = s0("nmr", [128, 1], F32)
            nm16 = s0("nm16", [128, H], F32)
            v16s = s0("v16s", [128, H], F32)
            rstd16 = s0("rstd16", [128, H], F32)
            msq16 = s0("msq16", [128, H], F32)
            rowS = s0("rowS", [128, 1], F32)
            sc1 = s0("sc1", [1, 1], F32)
            diffv = s0("diffv", [1, 1], F32)
            tthr = s0("tthr", [1, 1], F32)
            brkr = s0("brkr", [1, 1], F32)
            brk = s0("brk", [1, 1], F32)
            done = s0("done", [1, 1], F32)
            ndone = s0("ndone", [1, 1], F32)
            d128 = s0("d128", [128, 1], F32)
            d128i = s0("d128i", [128, 1], I32)
            g128 = s0("g128", [128, 1], F32)

            ps_tr = ps0_("ps_tr", [128, 512], F32)
            ps_a = ps0_("ps_a", [128, 512], F32)
            ps_b = ps0_("ps_b", [128, 512], F32)
            ps_big = ps0_("ps_big", [128, 1024], F32)

            pr0 = Prog(selfsems, selfcnt)
            cur = {"v": None, "s": None, "pe": 0, "ch": 0}
            evE = {"tr": None, "a": None, "b": None, "big": None}

            def V(fn, extra=()):
                w = [e for e in ([cur["v"]] + list(extra)) if e is not None]
                cur["v"] = pr0.add("vector", fn, waits=w)
                return cur["v"]

            def Sc(fn, extra=()):
                w = [e for e in ([cur["s"]] + list(extra)) if e is not None]
                cur["s"] = pr0.add("scalar", fn, waits=w)
                return cur["s"]

            def PE(fn, extra=()):
                w = [e for e in extra if e is not None]
                cur["pe"] += 1
                pr0.add("tensor", fn, waits=w, incs=[(p0t, 1)])
                return (p0t, cur["pe"])

            def SY(fn, extra=()):
                w = [e for e in extra if e is not None]
                pr0.add("sync", fn, waits=w, incs=[(ch, 16)], isdma=True)
                cur["ch"] += 16
                return (ch, cur["ch"])

            def ld(dst, srcap):
                SY(lambda e, d=dst, s=srcap: e.dma_start(out=d, in_=s))
            ld(identf[:, :], ident_f.ap())
            ld(identb[:, :], ident_b.ap())
            ld(onesrf[0:1, :], ones_r_f.ap())
            ld(onescf[:, :], ones_c_f.ap())
            ld(onesrb[0:1, :], ones_r_b.ap())
            ld(onescb[:, :], ones_c_b.ap())
            ld(onesrwb[0:1, :], ones_rw_b.ap())
            ld(iota8[0:1, :], iota8_c.ap())
            ld(iw1[:, :], iw1_c.ap())
            ld(iw2[:, :], iw2_c.ap())
            ld(mask128[:, :], mask_c.ap())
            ld(xt16[:, :], x16.ap())
            for bct, vec in ((ob_bc, ob), (lna_bc, lna), (lnb_bc, lnb),
                             (qb_bc, qb), (vb_bc, vb)):
                ld(bct[0:1, :], vec.ap().unsqueeze(0))
            for bct, vec in ((lnc_bc, lnc), (lnd_bc, lnd), (lqb_bc, lqb),
                             (lkb_bc, lkb), (lvb_bc, lvb)):
                ld(bct[0:1, :], vec.ap().unsqueeze(0))
            for dst, srct in ((lqw2, lqwT), (lkw2, lkwT), (lvw2, lvwT)):
                ld(dst[0:HD, :], srct.ap())
            LOADS = (ch, cur["ch"])
            dw_cnt = [0]

            def ldw(srct, extra=()):
                w = [e for e in extra if e is not None]
                pr0.add("gpsimd", lambda e, srct=srct: e.dma_start(
                    out=wT_t[:, :], in_=srct.ap()),
                    waits=w, incs=[(dW, 16)], isdma=True)
                dw_cnt[0] += 16
                return (dW, dw_cnt[0])

            ldq = ldw(qwT)

            V(lambda e: e.memset(epst[:, :], EPS), extra=[LOADS])
            V(lambda e: e.memset(done[:, :], 0.0))
            V(lambda e: e.memset(dbg_t[0:1, :], 0.0))
            for bct in (ob_bc, lna_bc, lnb_bc, qb_bc, vb_bc):
                for ih in range(2):
                    bank = ("a", ps_a) if ih == 0 else ("b", ps_b)
                    PE(lambda e, bct=bct, ih=ih, bank=bank: e.matmul(
                        bank[1][:, :], onesrf[0:1, :],
                        bct[0:1, ih * 512: ih * 512 + 512],
                        start=True, stop=True),
                        extra=[LOADS, evE[bank[0]]])
                    evE[bank[0]] = V(lambda e, bct=bct, ih=ih, bank=bank:
                                     e.tensor_copy(
                        bct[:, ih * 512: ih * 512 + 512], bank[1][:, :]),
                        extra=[(p0t, cur["pe"])])
            for bct in (lnc_bc, lnd_bc, lqb_bc, lkb_bc, lvb_bc):
                PE(lambda e, bct=bct: e.matmul(
                    ps_a[:, 0:HD], onesrf[0:1, :], bct[0:1, 0:HD],
                    start=True, stop=True), extra=[LOADS, evE["a"]])
                evE["a"] = V(lambda e, bct=bct: e.tensor_copy(
                    bct[:, :], ps_a[:, 0:HD]), extra=[(p0t, cur["pe"])])
            E_BC = evE["a"]
            E_OB_SET = cur["v"]

            def tr1024(src, dstT, src_edge=None):
                psb16 = ps_tr[:, :].bitcast(BF16)
                for j in range(8):
                    ee = [src_edge, evE["tr"]] if j == 0 else []
                    PE(lambda e, j=j: e.transpose(
                        psb16[:, j * 128:(j + 1) * 128],
                        src[:, j * 128:(j + 1) * 128], identb[:, :]),
                        extra=ee)
                evE["tr"] = Sc(lambda e: e.activation(
                    out=dstT[:, :], in_=psb16[:, :], func=AF.Identity),
                    extra=[(p0t, cur["pe"])])
                return evE["tr"]

            def tr_heads(src, dstT64, src_edge=None):
                for g in range(4):
                    ee = [src_edge, evE["tr"]] if g == 0 else [evE["tr"]]
                    for j in range(4):
                        h = g * 4 + j
                        PE(lambda e, h=h, j=j: e.transpose(
                            ps_tr[0:64, j * 128:(j + 1) * 128],
                            src[:, h * 64:(h + 1) * 64], identf[:, :]),
                            extra=(ee if j == 0 else []))
                    evE["tr"] = Sc(lambda e, g=g: e.activation(
                        out=dstT64[0:64, g * 512:(g + 1) * 512],
                        in_=ps_tr[0:64, 0:512], func=AF.Identity),
                        extra=[(p0t, cur["pe"])])
                return evE["tr"]

            def proj(srcT, dst, bias_bc, src_edge=None, w_edge=None):
                out_edges = []
                for ih in range(2):
                    bank = ("a", ps_a) if ih == 0 else ("b", ps_b)
                    for hc in range(8):
                        ee = ([src_edge, w_edge, evE[bank[0]]] if hc == 0 else [])
                        PE(lambda e, ih=ih, hc=hc, bank=bank: e.matmul(
                            bank[1][:, :], srcT[:, hc * 128:(hc + 1) * 128],
                            wT_t[:, hc * D + ih * 512: hc * D + ih * 512 + 512],
                            start=(hc == 0), stop=(hc == 7)), extra=ee)
                    if bias_bc is None:
                        evE[bank[0]] = V(lambda e, ih=ih, bank=bank: e.tensor_copy(
                            dst[:, ih * 512: ih * 512 + 512], bank[1][:, :]),
                            extra=[(p0t, cur["pe"])])
                    else:
                        evE[bank[0]] = V(lambda e, ih=ih, bank=bank: e.tensor_tensor(
                            dst[:, ih * 512: ih * 512 + 512], bank[1][:, :],
                            bias_bc[:, ih * 512: ih * 512 + 512], op=ALU.add),
                            extra=[(p0t, cur["pe"])])
                return evE["b"]

            def headproj(srcT64, w2t, bias_bc, dst, src_edge=None):
                for bank_i in range(2):
                    bank = ("a", ps_a) if bank_i == 0 else ("b", ps_b)
                    for hh in range(8):
                        h = bank_i * 8 + hh
                        ee = ([src_edge, evE[bank[0]]] if hh == 0 else [])
                        PE(lambda e, h=h, hh=hh, bank=bank: e.matmul(
                            bank[1][:, hh * 64:(hh + 1) * 64],
                            srcT64[0:64, h * 128:(h + 1) * 128],
                            w2t[0:64, :],
                            start=True, stop=True), extra=ee)
                    evE[bank[0]] = V(lambda e, bank=bank, bank_i=bank_i:
                                     e.tensor_tensor(
                        dst[:, bank_i * 512:(bank_i + 1) * 512].rearrange(
                            "p (a b) -> p a b", b=HD),
                        bank[1][:, :].rearrange("p (a b) -> p a b", b=HD),
                        bias_bc[:, :].unsqueeze(1).broadcast_to((128, 8, HD)),
                        op=ALU.add),
                        extra=[(p0t, cur["pe"])])
                return evE["b"]

            def row_ln(src, dst, w_bc=None, src_edge=None):
                ev = V(lambda e: e.tensor_reduce(nm[:, :], src[:, :], axis=AX,
                                                 op=ALU.add, negate=True),
                       extra=[src_edge])
                V(lambda e: e.tensor_scalar(nm[:, :], nm[:, :], 1.0 / D,
                                            None, op0=ALU.mult))
                es = Sc(lambda e: e.activation(out=sqscr[:, :], in_=src[:, :],
                                               func=AF.Square, bias=nm[:, 0:1],
                                               accum_out=vs[:, 0:1]),
                        extra=[cur["v"], src_edge])
                Sc(lambda e: e.activation(out=lnv[:, :], in_=vs[:, :],
                                          func=AF.Ln, scale=1.0 / D,
                                          bias=epst[:, 0:1]))
                es = Sc(lambda e: e.activation(out=rstd[:, :], in_=lnv[:, :],
                                               func=AF.Exp, scale=-0.5))
                V(lambda e: e.tensor_tensor(nmr[:, :], nm[:, :],
                                            rstd[:, :], op=ALU.mult),
                  extra=[es])
                ed = Sc(lambda e: e.activation(out=dst[:, :], in_=src[:, :],
                                               func=AF.Identity,
                                               bias=nmr[:, 0:1],
                                               scale=rstd[:, 0:1]),
                        extra=[cur["v"]])
                if w_bc is not None:
                    ed = V(lambda e: e.tensor_tensor(dst[:, :], dst[:, :],
                                                     w_bc[:, :], op=ALU.mult),
                           extra=[ed])
                return ed

            def seg_ln(src, dst, w_bc, src_edge=None):
                V(lambda e: e.tensor_reduce(
                    nm16[:, :], src[:, :].rearrange("p (a b) -> p a b", b=HD),
                    axis=AX, op=ALU.add, negate=True), extra=[src_edge])
                es = Sc(lambda e: e.activation(out=sqscr[:, :], in_=src[:, :],
                                               func=AF.Square),
                        extra=[src_edge])
                V(lambda e: e.tensor_reduce(
                    v16s[:, :], sqscr[:, :].rearrange("p (a b) -> p a b", b=HD),
                    axis=AX, op=ALU.add), extra=[es])
                V(lambda e: e.tensor_scalar(nm16[:, :], nm16[:, :],
                                            1.0 / HD, None, op0=ALU.mult))
                V(lambda e: e.tensor_tensor(msq16[:, :], nm16[:, :],
                                            nm16[:, :], op=ALU.mult))
                V(lambda e: e.scalar_tensor_tensor(
                    v16s[:, :], v16s[:, :], 1.0 / HD, msq16[:, :],
                    op0=ALU.mult, op1=ALU.subtract))
                Sc(lambda e: e.activation(out=v16s[:, :], in_=v16s[:, :],
                                          func=AF.Ln, bias=epst[:, 0:1]),
                   extra=[cur["v"]])
                es = Sc(lambda e: e.activation(out=rstd16[:, :], in_=v16s[:, :],
                                               func=AF.Exp, scale=-0.5))
                V(lambda e: e.tensor_tensor(
                    dst[:, :].rearrange("p (a b) -> p a b", b=HD),
                    src[:, :].rearrange("p (a b) -> p a b", b=HD),
                    nm16[:, :].unsqueeze(2).broadcast_to((128, H, HD)),
                    op=ALU.add))
                V(lambda e: e.tensor_tensor(
                    dst[:, :].rearrange("p (a b) -> p a b", b=HD),
                    dst[:, :].rearrange("p (a b) -> p a b", b=HD),
                    rstd16[:, :].unsqueeze(2).broadcast_to((128, H, HD)),
                    op=ALU.mult), extra=[es])
                if w_bc is None:
                    return cur["v"]
                return V(lambda e: e.tensor_tensor(
                    dst[:, :].rearrange("p (a b) -> p a b", b=HD),
                    dst[:, :].rearrange("p (a b) -> p a b", b=HD),
                    w_bc[:, :].unsqueeze(1).broadcast_to((128, H, HD)),
                    op=ALU.mult))

            pr1 = pr0
            ORDER = list(range(1, NTILE)) + [0]
            E_v1 = {}; E_v7 = {}; E_a4 = {}; E_a3 = {}; E_v6 = {}; E_v4 = {}
            E_OB = [E_OB_SET]; E_O16 = [None]

            def emit_xin(pos, wextra=()):
                tile = ORDER[pos]
                pr1.add("sync", lambda e, tile=tile, pos=pos: e.dma_start(
                    out=xt[pos % NB][:, :],
                    in_=xb.ap()[tile * 128:(tile + 1) * 128, :]),
                    waits=wextra, incs=[(dX[pos % NB], 16)], isdma=True)

            def stageA(pos):
                tile = ORDER[pos]
                bb = pos % NB
                w = [(dX[bb], 16 * (pos // NB + 1))]
                if pos == 0:
                    w.append(E_OB[0])
                if pos >= NB:
                    w.append((dXO[bb], 16 * (pos // NB)))
                if tile == 0:
                    w.append(E_O16[0])
                E_v1[pos] = pr1.add("gpsimd", lambda e, bb=bb: e.tensor_tensor(
                    x1t[bb][:, :], xt[bb][:, :], ob_bc[:, :], op=ALU.add),
                    waits=w)
                if tile == 0:
                    E_v1[pos] = pr1.add("gpsimd", lambda e, bb=bb: e.tensor_tensor(
                        x1t[bb][0:16, :], x1t[bb][0:16, :], o16[0:16, :],
                        op=ALU.add), waits=[E_v1[pos]])
                e_v3 = pr1.add("vector", lambda e, bb=bb: e.tensor_reduce(
                    nmP[bb][:, :], x1t[bb][:, :], axis=AX, op=ALU.add,
                    negate=True), waits=[E_v1[pos]])
                E_v4[pos] = pr1.add("vector", lambda e, bb=bb: e.tensor_scalar(
                    nmeP[bb][:, :], nmP[bb][:, :], 1.0 / D, None, op0=ALU.mult),
                    waits=[e_v3])
                wa1 = [E_v4[pos]]
                if pos >= NB:
                    wa1.append((p1p, pos - NB + 1))
                e_a1 = pr1.add("scalar", lambda e, bb=bb: e.activation(
                    out=lnbt[bb][:, :], in_=x1t[bb][:, :], func=AF.Square,
                    bias=nmeP[bb][:, 0:1], accum_out=vsP[bb][:, 0:1]),
                    waits=wa1)
                e_a2 = pr1.add("scalar", lambda e, bb=bb: e.activation(
                    out=lnvP[bb][:, :], in_=vsP[bb][:, :], func=AF.Ln,
                    scale=1.0 / D, bias=epst[:, 0:1]), waits=[e_a1])
                E_a3[pos] = pr1.add("scalar", lambda e, bb=bb: e.activation(
                    out=rstdP[bb][:, :], in_=lnvP[bb][:, :], func=AF.Exp,
                    scale=-0.5), waits=[e_a2])

            def stageB1(pos):
                bb = pos % NB
                E_v6[pos] = pr1.add("vector", lambda e, bb=bb: e.tensor_tensor(
                    nmrP[bb][:, :], nmeP[bb][:, :], rstdP[bb][:, :],
                    op=ALU.mult), waits=[E_a3[pos]])
                E_a4[pos] = pr1.add("scalar", lambda e, bb=bb: e.activation(
                    out=lnbt[bb][:, :], in_=x1t[bb][:, :], func=AF.Identity,
                    bias=nmrP[bb][:, 0:1], scale=rstdP[bb][:, 0:1]),
                    waits=[E_v6[pos]])
                w = [E_a4[pos]]
                if pos >= 1:
                    w.append(E_v7[pos - 1])
                for blkj in range(8):
                    pr1.add("tensor", lambda e, bb=bb, j=blkj: e.transpose(
                        psT[:, j * 128:(j + 1) * 128],
                        lnbt[bb][:, j * 128:(j + 1) * 128], identb[:, :]),
                        waits=(w if blkj == 0 else []))
                pr1.add("tensor", lambda e, bb=bb, pos=pos: e.matmul(
                    ps_seq[0][0:1, :], onescb[:, 0:1], lnbt[bb][:, 0:512],
                    start=(pos == 0), stop=(pos == NTILE - 1)))
                pr1.add("tensor", lambda e, bb=bb, pos=pos: e.matmul(
                    ps_seq[1][0:1, :], onescb[:, 0:1], lnbt[bb][:, 512:1024],
                    start=(pos == 0), stop=(pos == NTILE - 1)), incs=[(p1p, 1)])

            def stageB2(pos):
                tile = ORDER[pos]
                bb = pos % NB
                w = [(p1p, pos + 1)]
                if pos >= 2:
                    w.append((dLT[pos % 2], 16 * (pos // 2)))
                E_v7[pos] = pr1.add("vector", lambda e, pos=pos: e.tensor_copy(
                    stageT[pos % 2][:, :], psT[:, :]), waits=w)
                pr1.add("scalar", lambda e, pos=pos, tile=tile: e.dma_start(
                    out=ln1T_d.ap()[:, :, tile * 128:(tile + 1) * 128].rearrange(
                        "a p t -> p a t"),
                    in_=stageT[pos % 2][:, :].rearrange("p (a t) -> p a t", a=8)),
                    waits=[E_v7[pos]], incs=[(dLT[pos % 2], 16)], isdma=True)
                if pos + NB < NTILE:
                    emit_xin(pos + NB, wextra=[E_v1[pos]])
                pr1.add("sync", lambda e, bb=bb, tile=tile: e.dma_start(
                    out=x1s_d.ap()[tile * 128:(tile + 1) * 128, :],
                    in_=x1t[bb][:, :]),
                    waits=[E_a4[pos]], incs=[(dXO[bb], 16)], isdma=True)

            def p1_gen():
                for j in range(NB):
                    emit_xin(j)
                stageA(0)
                stageA(1)
                yield
                for k in range(NTILE - 1):
                    stageB1(k)
                    if k + 2 <= NTILE - 2:
                        stageA(k + 2)
                    stageB2(k)
                    yield
            _gen = p1_gen()

            def p1tick(n=1):
                for _ in range(n):
                    try:
                        next(_gen)
                    except StopIteration:
                        break

            e_xin = row_ln(xt16, xin, src_edge=LOADS)
            e_aln = row_ln(xin, t_ln, lna_bc, src_edge=e_xin)
            e_trT = tr1024(t_ln, trT, src_edge=e_aln)
            e_q = proj(trT, qcur, qb_bc, src_edge=e_trT, w_edge=ldq)
            p1tick()
            ldk = ldw(kwT, extra=[(p0t, cur["pe"])])
            e_bln = row_ln(xin, t_ln, lnb_bc, src_edge=e_aln)
            p1tick()
            e_trT = tr1024(t_ln, trT, src_edge=e_bln)
            e_k = proj(trT, k16, None, src_edge=e_trT, w_edge=ldk)
            p1tick()
            ldv = ldw(vwT, extra=[(p0t, cur["pe"])])
            e_v = proj(trT, v16, vb_bc, src_edge=e_trT, w_edge=ldv)
            p1tick()
            e_trT = tr_heads(k16, trH, src_edge=e_k)
            e_kp = headproj(trH, lkw2, lkb_bc, qpre, src_edge=e_trT)
            V(lambda e: e.tensor_tensor(lnd_bc[:, :], lnd_bc[:, :],
                                        lnc_bc[:, :], op=ALU.mult))
            e_kn = seg_ln(qpre, kn, lnd_bc, src_edge=e_kp)
            e_trT = tr_heads(v16, trH, src_edge=e_v)
            e_vi = headproj(trH, lvw2, lvb_bc, vi, src_edge=e_trT)
            e_knT = tr_heads(kn, knT64, src_edge=e_kn)
            p1tick()

            for it in range(3):
                e_trT = tr_heads(qcur, trH, src_edge=cur["v"])
                e_qp = headproj(trH, lqw2, lqb_bc, qpre, src_edge=e_trT)
                p1tick()
                e_qn = seg_ln(qpre, qn, None, src_edge=e_qp)
                p1tick()
                e_trT = tr_heads(qn, trH, src_edge=e_qn)
                p1tick(2)
                for half in range(2):
                    for hh in range(8):
                        h = half * 8 + hh
                        ee = ([e_trT, e_knT, evE["big"]] if hh == 0 else [])
                        PE(lambda e, h=h, hh=hh: e.matmul(
                            ps_big[:, hh * 128:(hh + 1) * 128],
                            trH[0:64, h * 128:(h + 1) * 128],
                            knT64[0:64, h * 128:(h + 1) * 128],
                            start=True, stop=True), extra=ee)
                    for g4 in range(2):
                        evE["big"] = V(lambda e, half=half, g4=g4:
                                       e.scalar_tensor_tensor(
                            EX[:, half * 1024 + g4 * 512:
                               half * 1024 + g4 * 512 + 512].rearrange(
                                "p (a b) -> p a b", b=128),
                            ps_big[:, g4 * 512:(g4 + 1) * 512].rearrange(
                                "p (a b) -> p a b", b=128),
                            0.125,
                            mask128[:, :].unsqueeze(1).broadcast_to((128, 4, 128)),
                            op0=ALU.mult, op1=ALU.add),
                            extra=([(p0t, cur["pe"])] if g4 == 0 else []))
                p1tick(2)
                ev = V(lambda e: e.tensor_reduce(
                    nm[:, :], EX[:, :], axis=AX, op=ALU.max, negate=True))
                es = Sc(lambda e: e.activation(out=EX[:, :], in_=EX[:, :],
                                               func=AF.Exp, bias=nm[:, 0:1]),
                        extra=[ev])
                V(lambda e: e.tensor_reduce(
                    v16s[:, :], EX[:, :].rearrange("p (a b) -> p a b", b=128),
                    axis=AX, op=ALU.add), extra=[es])
                V(lambda e: e.reciprocal(rstd16[:, :], v16s[:, :]))
                e_P = V(lambda e: e.tensor_tensor(
                    EX[:, :].rearrange("p (a b) -> p a b", b=128),
                    EX[:, :].rearrange("p (a b) -> p a b", b=128),
                    rstd16[:, :].unsqueeze(2).broadcast_to((128, H, 128)),
                    op=ALU.mult))
                ev_pt = {}
                for g in range(4):
                    for j in range(4):
                        h = g * 4 + j
                        ee = []
                        if h == 0:
                            ee = [e_P, evE["big"]]
                        elif j == 0 and g >= 2:
                            ee = [ev_pt[g - 2]]
                        PE(lambda e, h=h, j=j, g=g: e.transpose(
                            ps_big[:, (g % 2) * 512 + j * 128:
                                   (g % 2) * 512 + j * 128 + 128],
                            EX[:, h * 128:(h + 1) * 128], identf[:, :]),
                            extra=ee)
                    ev_pt[g] = Sc(lambda e, g=g: e.activation(
                        out=EX[:, g * 512:(g + 1) * 512],
                        in_=ps_big[:, (g % 2) * 512:(g % 2) * 512 + 512],
                        func=AF.Identity),
                        extra=[(p0t, cur["pe"])])
                evE["big"] = ev_pt[3]
                p1tick(2)
                for bank_i in range(2):
                    bank = ("a", ps_a) if bank_i == 0 else ("b", ps_b)
                    for hh in range(8):
                        h = bank_i * 8 + hh
                        ee = ([ev_pt[1] if bank_i == 0 else ev_pt[3],
                               e_vi, evE[bank[0]]] if hh == 0 else [])
                        PE(lambda e, h=h, hh=hh, bank=bank: e.matmul(
                            bank[1][:, hh * 64:(hh + 1) * 64],
                            EX[:, h * 128:(h + 1) * 128],
                            vi[:, h * 64:(h + 1) * 64], start=True, stop=True),
                            extra=ee)
                    evE[bank[0]] = Sc(lambda e, bank=bank, bank_i=bank_i:
                                      e.activation(
                        out=iterout[:, bank_i * 512:(bank_i + 1) * 512],
                        in_=bank[1][:, :], func=AF.Identity),
                        extra=[(p0t, cur["pe"])])

                if it == 0:
                    V(lambda e: e.tensor_copy(prevattn[:, :], iterout[:, :]),
                      extra=[evE["b"]])
                    V(lambda e: e.tensor_add(qcur[:, :], qcur[:, :],
                                             iterout[:, :]))
                else:
                    if it == 1:
                        V(lambda e: e.tensor_tensor(
                            tmpb[:, :], iterout[:, :], prevattn[:, :],
                            op=ALU.subtract), extra=[evE["b"]])
                        e_rs = V(lambda e: e.tensor_reduce(
                            rowS[:, :], tmpb[:, :], axis=AX, op=ALU.add,
                            apply_absolute_value=True))
                        pee = PE(lambda e: e.matmul(ps_tr[0:1, 0:1],
                                                    rowS[:, 0:1], onescf[:, 0:1],
                                                    start=True, stop=True),
                                 extra=[e_rs, evE["tr"]])
                        V(lambda e: e.tensor_copy(sc1[:, :], ps_tr[0:1, 0:1]),
                          extra=[pee])
                        evE["tr"] = cur["v"]
                        V(lambda e: e.tensor_scalar(
                            diffv[:, :], sc1[:, :], 1.0 / DIFF_DEN, None,
                            op0=ALU.mult))
                        V(lambda e: e.tensor_scalar(
                            tthr[:, :], diffv[:, :], 0.1, 0.01,
                            op0=ALU.mult, op1=ALU.add))
                        V(lambda e: e.tensor_tensor(
                            brkr[:, :], diffv[:, :], tthr[:, :], op=ALU.is_lt))
                        V(lambda e: e.tensor_scalar(
                            ndone[:, :], done[:, :], -1.0, 1.0,
                            op0=ALU.mult, op1=ALU.add))
                        V(lambda e: e.tensor_tensor(
                            brk[:, :], brkr[:, :], ndone[:, :], op=ALU.mult))
                    else:
                        V(lambda e: e.tensor_scalar(
                            ndone[:, :], done[:, :], -1.0, 1.0,
                            op0=ALU.mult, op1=ALU.add), extra=[evE["b"]])
                    pee = PE(lambda e: e.matmul(ps_tr[:, 1:2], onesrf[0:1, :],
                                                ndone[0:1, 0:1],
                                                start=True, stop=True),
                             extra=[cur["v"], evE["tr"]])
                    V(lambda e: e.tensor_copy(d128[:, :], ps_tr[:, 1:2]),
                      extra=[pee])
                    evE["tr"] = cur["v"]
                    V(lambda e: e.tensor_copy(d128i[:, :], d128[:, :]))
                    V(lambda e: e.copy_predicated(
                        prevattn[:, :], d128i[:, 0:1].broadcast_to((128, D)),
                        iterout[:, :]))
                    if it == 1:
                        V(lambda e: e.tensor_tensor(
                            done[:, :], done[:, :], brk[:, :], op=ALU.max))
                        V(lambda e: e.tensor_scalar(
                            ndone[:, :], done[:, :], -1.0, 1.0,
                            op0=ALU.mult, op1=ALU.add))
                        pee = PE(lambda e: e.matmul(
                            ps_tr[:, 2:3], onesrf[0:1, :], ndone[0:1, 0:1],
                            start=True, stop=True),
                            extra=[cur["v"], evE["tr"]])
                        V(lambda e: e.tensor_copy(g128[:, :], ps_tr[:, 2:3]),
                          extra=[pee])
                        evE["tr"] = cur["v"]
                        V(lambda e: e.tensor_scalar(
                            tmpb[:, :], iterout[:, :], g128[:, 0:1], None,
                            op0=ALU.mult))
                        V(lambda e: e.tensor_add(qcur[:, :], qcur[:, :],
                                                 tmpb[:, :]))

            ldo = ldw(owT, extra=[(p0t, cur["pe"])])
            e_pb = Sc(lambda e: e.activation(out=t_ln[:, :], in_=prevattn[:, :],
                                             func=AF.Identity),
                      extra=[cur["v"]])
            e_trT = tr1024(t_ln, trT, src_edge=e_pb)
            p1tick()
            E_O16[0] = proj(trT, o16, None, src_edge=e_trT, w_edge=ldo)
            V(lambda e: e.tensor_copy(dbg_t[0:1, 32:33], diffv[0:1, :]))
            V(lambda e: e.tensor_copy(dbg_t[0:1, 33:34], done[0:1, :]))

            p1tick(64)
            stageA(NTILE - 1)
            stageB1(NTILE - 1)
            stageB2(NTILE - 1)

            P0_EDGE = cur["v"]
            CH_P0 = cur["ch"]
            with nc.Block() as blk0:
                pr0.realize(blk0)
            pp0.close()

            with ExitStack() as rr:
                def sr(*a, **k):
                    return rr.enter_context(nc.sbuf_tensor(*a, **k))
                def psr(*a, **k):
                    return rr.enter_context(nc.psum_tensor(*a, **k))
                gate_t = sr("gate_t", [128, 8 * E], F32)
                b1_t = sr("b1_t", [8, FF], F32)
                b2_t = sr("b2_t", [8, D], F32)
                seqT = sr("seqT", [128, 8], F32)
                r8 = sr("r8", [8, 1], F32)
                lg8 = sr("lg8", [1, 8], F32)
                pr8 = sr("pr8", [1, 8], F32)
                pm8 = sr("pm8", [1, 8], F32)
                is1 = sr("is1", [1, 8], F32)
                is2 = sr("is2", [1, 8], F32)
                t8 = sr("t8", [1, 8], F32)
                scrow = sr("scrow", [1, 8], F32)
                m1 = sr("m1", [1, 1], F32)
                m2 = sr("m2", [1, 1], F32)
                e0v = sr("e0v", [1, 1], F32)
                e1v = sr("e1v", [1, 1], F32)
                c0v = sr("c0v", [1, 1], F32)
                c1v = sr("c1v", [1, 1], F32)
                s8 = sr("s8", [1, 1], F32)
                rec = sr("rec", [1, 1], F32)
                ebc = sr("ebc", [128, 4], F32)
                idxf = sr("idxf", [128, 32], F32)
                vt8 = sr("vt8", [8, 2], F32)
                b1row = sr("b1row", [1, FF], F32)
                b1row_bf = sr("b1row_bf", [1, FF], BF16)
                cb2r = sr("cb2r", [1, D], F32)
                ps_r = psr("ps_r", [128, 8], F32)
                ps_w = [psr("ps_w0", [1, 512], F32), psr("ps_w1", [1, 512], F32)]

                prR = Prog(selfsems, selfcnt)
                rpe = sem("rpe")
                curR = {"v": None, "s": None, "pe": 0, "ch": CH_P0}

                def RV(fn, extra=()):
                    w = [e for e in ([curR["v"]] + list(extra)) if e is not None]
                    curR["v"] = prR.add("vector", fn, waits=w)
                    return curR["v"]

                def RS(fn, extra=()):
                    w = [e for e in ([curR["s"]] + list(extra)) if e is not None]
                    curR["s"] = prR.add("scalar", fn, waits=w)
                    return curR["s"]

                def RPE(fn, extra=()):
                    w = [e for e in extra if e is not None]
                    curR["pe"] += 1
                    prR.add("tensor", fn, waits=w, incs=[(rpe, 1)])
                    return (rpe, curR["pe"])

                def RSY(fn, extra=()):
                    w = [e for e in extra if e is not None]
                    prR.add("sync", fn, waits=w, incs=[(ch, 16)], isdma=True)
                    curR["ch"] += 16
                    return (ch, curR["ch"])

                first_extra = [(p1p, NTILE), E_v7[NTILE - 1], E_a4[NTILE - 1],
                               (dLT[0], 16 * 16), (dLT[1], 16 * 16)] + [
                               (dXO[j], 16 * (NTILE // 4)) for j in range(4)]

                RSY(lambda e: e.dma_start(
                    out=gate_t[:, :].rearrange("p (hc e) -> p hc e", hc=8),
                    in_=gate_w.ap().rearrange("(hc p) e -> p hc e", p=128)),
                    extra=first_extra)
                RSY(lambda e: e.dma_start(out=b1_t[:, :], in_=b1.ap()))
                RSY(lambda e: e.dma_start(out=b2_t[:, :], in_=b2.ap()))
                LD_ALL = (ch, curR["ch"])

                ev1 = RV(lambda e: e.tensor_copy(seqr[0:1, 0:512],
                                                 ps_seq[0][0:1, :]),
                         extra=[(p1p, NTILE)])
                ev2 = RV(lambda e: e.tensor_copy(seqr[0:1, 512:1024],
                                                 ps_seq[1][0:1, :]))
                for hc in range(8):
                    RPE(lambda e, hc=hc: e.matmul(
                        ps_r[:, hc:hc + 1], seqr[0:1, hc * 128:(hc + 1) * 128],
                        identf[0:1, 0:1], start=True, stop=True),
                        extra=([ev2] if hc == 0 else []))
                evT = RV(lambda e: e.tensor_copy(seqT[:, :], ps_r[:, :]),
                         extra=[(rpe, curR["pe"])])
                for hc in range(8):
                    RPE(lambda e, hc=hc: e.matmul(
                        ps_r[0:8, 0:1], gate_t[:, hc * 8:(hc + 1) * 8],
                        seqT[:, hc:hc + 1], start=(hc == 0), stop=(hc == 7)),
                        extra=([evT, LD_ALL] if hc == 0 else []))
                evr = RV(lambda e: e.tensor_copy(r8[:, :], ps_r[0:8, 0:1]),
                         extra=[(rpe, curR["pe"])])
                RPE(lambda e: e.matmul(ps_r[0:1, 0:8], r8[:, 0:1],
                                       identf[0:8, 0:8], start=True, stop=True),
                    extra=[evr])
                RV(lambda e: e.tensor_copy(lg8[:, :], ps_r[0:1, 0:8]),
                   extra=[(rpe, curR["pe"])])
                RV(lambda e: e.tensor_scalar(lg8[:, :], lg8[:, :], 1.0 / T,
                                             None, op0=ALU.mult))
                RV(lambda e: e.tensor_reduce(m1[:, :], lg8[:, :], axis=AX,
                                             op=ALU.max, negate=True))
                RS(lambda e: e.activation(out=pr8[:, :], in_=lg8[:, :],
                                          func=AF.Exp, bias=m1[0:1, 0:1]),
                   extra=[curR["v"]])
                RV(lambda e: e.tensor_reduce(s8[:, :], pr8[:, :], axis=AX,
                                             op=ALU.add), extra=[curR["s"]])
                RV(lambda e: e.reciprocal(rec[:, :], s8[:, :]))
                RV(lambda e: e.tensor_scalar(pr8[:, :], pr8[:, :],
                                             rec[0:1, 0:1], None, op0=ALU.mult))
                RV(lambda e: e.tensor_reduce(m1[:, :], pr8[:, :], axis=AX,
                                             op=ALU.max))
                RV(lambda e: e.tensor_scalar(is1[:, :], pr8[:, :],
                                             m1[0:1, 0:1], None,
                                             op0=ALU.is_equal))
                RV(lambda e: e.scalar_tensor_tensor(
                    pm8[:, :], is1[:, :], -1e30, pr8[:, :],
                    op0=ALU.mult, op1=ALU.add))
                RV(lambda e: e.tensor_reduce(m2[:, :], pm8[:, :], axis=AX,
                                             op=ALU.max))
                RV(lambda e: e.tensor_scalar(is2[:, :], pm8[:, :],
                                             m2[0:1, 0:1], None,
                                             op0=ALU.is_equal))
                RV(lambda e: e.tensor_scalar(t8[:, :], is1[:, :], -1000.0,
                                             1000.0, op0=ALU.mult, op1=ALU.add))
                RV(lambda e: e.tensor_tensor(t8[:, :], t8[:, :], iota8[0:1, :],
                                             op=ALU.add))
                RV(lambda e: e.tensor_reduce(e0v[:, :], t8[:, :], axis=AX,
                                             op=ALU.min))
                RV(lambda e: e.tensor_scalar(t8[:, :], is2[:, :], -1000.0,
                                             1000.0, op0=ALU.mult, op1=ALU.add))
                RV(lambda e: e.tensor_tensor(t8[:, :], t8[:, :], iota8[0:1, :],
                                             op=ALU.add))
                RV(lambda e: e.tensor_reduce(e1v[:, :], t8[:, :], axis=AX,
                                             op=ALU.min))
                RV(lambda e: e.tensor_tensor(s8[:, :], m1[:, :], m2[:, :],
                                             op=ALU.add))
                RV(lambda e: e.reciprocal(rec[:, :], s8[:, :]))
                RV(lambda e: e.tensor_tensor(c0v[:, :], m1[:, :], rec[:, :],
                                             op=ALU.mult))
                E_C = RV(lambda e: e.tensor_tensor(c1v[:, :], m2[:, :],
                                                   rec[:, :], op=ALU.mult))
                for j, v in enumerate((e0v, e1v, c0v, c1v)):
                    RPE(lambda e, j=j, v=v: e.matmul(
                        ps_r[:, j:j + 1], onesrf[0:1, :], v[0:1, 0:1],
                        start=True, stop=True),
                        extra=([E_C] if j == 0 else []))
                RV(lambda e: e.tensor_copy(ebc[:, :], ps_r[:, 0:4]),
                   extra=[(rpe, curR["pe"])])
                RV(lambda e: e.tensor_copy(c0_128[:, :], ebc[:, 2:3]))
                RV(lambda e: e.tensor_copy(c1_128[:, :], ebc[:, 3:4]))
                for si in range(2):
                    RV(lambda e, si=si: e.scalar_tensor_tensor(
                        idxf[:, 0:8], ebc[:, si:si + 1].broadcast_to((128, 8)),
                        1024.0, iw1[:, :], op0=ALU.mult, op1=ALU.add))
                    RV(lambda e, si=si: e.tensor_copy(
                        idxw1[:, si * 8:(si + 1) * 8], idxf[:, 0:8]))
                    RV(lambda e, si=si: e.scalar_tensor_tensor(
                        idxf[:, 0:32], ebc[:, si:si + 1].broadcast_to((128, 32)),
                        4096.0, iw2[:, :], op0=ALU.mult, op1=ALU.add))
                    RV(lambda e, si=si: e.tensor_copy(
                        idxw2[:, si * 32:(si + 1) * 32], idxf[:, 0:32]))
                eb1st = [None]
                for si, isx in ((0, is1), (1, is2)):
                    RPE(lambda e, isx=isx, si=si: e.matmul(
                        ps_r[0:8, 4 + si:5 + si], isx[0:1, :],
                        identf[0:1, 0:1], start=True, stop=True),
                        extra=[E_C])
                    evis = RV(lambda e, si=si: e.tensor_copy(
                        vt8[:, si:si + 1], ps_r[0:8, 4 + si:5 + si]),
                        extra=[(rpe, curR["pe"])])
                    pee = {}
                    evj = {}
                    for j in range(8):
                        we = [evis, LD_ALL] if j == 0 else []
                        if j >= 2:
                            we = we + [evj[j - 2]]
                        pee[j] = RPE(lambda e, j=j, si=si: e.matmul(
                            ps_w[j % 2][0:1, :], vt8[:, si:si + 1],
                            b1_t[:, j * 512:(j + 1) * 512],
                            start=True, stop=True), extra=we)
                        evj[j] = RV(lambda e, j=j: e.tensor_copy(
                            b1row[0:1, j * 512:(j + 1) * 512],
                            ps_w[j % 2][0:1, :]), extra=[pee[j]])
                    RV(lambda e: e.tensor_copy(b1row_bf[0:1, :], b1row[0:1, :]),
                       extra=([eb1st[0]] if si == 1 else []))
                    eb1st[0] = RSY(lambda e, si=si: e.dma_start(
                        out=b1r_d.ap()[si:si + 1, :], in_=b1row_bf[0:1, :]),
                        extra=[curR["v"]])
                RV(lambda e: e.tensor_scalar(scrow[:, :], is1[:, :],
                                             c0v[0:1, 0:1], None, op0=ALU.mult))
                RV(lambda e: e.scalar_tensor_tensor(
                    scrow[:, :], is2[:, :], c1v[0:1, 0:1], scrow[:, :],
                    op0=ALU.mult, op1=ALU.add))
                RPE(lambda e: e.matmul(ps_r[0:8, 6:7], scrow[0:1, :],
                                       identf[0:1, 0:1], start=True, stop=True),
                    extra=[curR["v"]])
                RV(lambda e: e.tensor_copy(vt8[:, 0:1], ps_r[0:8, 6:7]),
                   extra=[(rpe, curR["pe"])])
                for j in range(2):
                    pee = RPE(lambda e, j=j: e.matmul(
                        ps_w[j][0:1, :], vt8[:, 0:1],
                        b2_t[:, j * 512:(j + 1) * 512],
                        start=True, stop=True), extra=[curR["v"]])
                    RV(lambda e, j=j: e.tensor_copy(
                        cb2r[0:1, j * 512:(j + 1) * 512], ps_w[j][0:1, :]),
                        extra=[pee])
                RV(lambda e: e.reciprocal(rec[:, :], c1v[:, :]))
                RV(lambda e: e.tensor_scalar(cb2c_bf[0:1, :], cb2r[0:1, :],
                                             rec[0:1, 0:1], None, op0=ALU.mult))
                RV(lambda e: e.tensor_copy(dbg_t[0:1, 0:8], pr8[0:1, :]))
                RV(lambda e: e.tensor_copy(dbg_t[0:1, 8:16], lg8[0:1, :]))
                RV(lambda e: e.tensor_copy(dbg_t[0:1, 16:17], e0v[0:1, :]))
                RV(lambda e: e.tensor_copy(dbg_t[0:1, 17:18], e1v[0:1, :]))
                RV(lambda e: e.tensor_copy(dbg_t[0:1, 18:19], c0v[0:1, :]))
                RV(lambda e: e.tensor_copy(dbg_t[0:1, 19:20], c1v[0:1, :]))
                edbg = RSY(lambda e: e.dma_start(out=dbg.ap(),
                                                 in_=dbg_t[0:1, :]),
                           extra=[curR["v"]])
                prR.add("sync", lambda e: e.nop(), waits=[edbg])

                R_EDGES = [curR["v"], curR["s"], (rpe, curR["pe"]),
                           (ch, curR["ch"])]
                with nc.Block() as blkR:
                    prR.realize(blkR)

        mid.close()
        with ExitStack() as ss:
            def s2(*a, **k):
                return ss.enter_context(nc.sbuf_tensor(*a, **k))
            def ps2(*a, **k):
                return ss.enter_context(nc.psum_tensor(*a, **k))
            b1rs = s2("b1rs", [1, FF], BF16)
            b1rT = s2("b1rT", [128, 32], F32)
            w1s = s2("w1s", [128, 8 * FF], BF16)
            w2s = s2("w2s", [128, 32 * D], BF16)
            hmidT = s2("hmidT", [128, 32 * 512], BF16)
            ln1c = [s2("ln1c0", [128, 8 * 512], BF16),
                    s2("ln1c1", [128, 8 * 512], BF16)]
            x1c = [s2("x1c0", [128, D], F32), s2("x1c1", [128, D], F32)]
            ostage = [s2("ostage0", [128, D], F32), s2("ostage1", [128, D], F32)]
            ps_b1 = ps2("ps_b1", [128, 32], F32)
            m1p = [ps2("m1p0", [128, 512], F32), ps2("m1p1", [128, 512], F32)]
            m2p = [ps2("m2p0", [128, 512], F32), ps2("m2p1", [128, 512], F32),
                   ps2("m2p2", [128, 512], F32), ps2("m2p3", [128, 512], F32)]

            prS = Prog(selfsems, selfcnt)
            pe1_cnt = [0]
            pe2_cnt = [0]
            E_gelu = {}
            E_evac = {}
            dl_cnt = [0]

            E_b1e = {}
            for s in range(2):
                wb1 = list(R_EDGES) if s == 0 else [(peB, 1)]
                prS.add("sync", lambda e, s=s: e.dma_start(
                    out=b1rs[0:1, :], in_=b1r_d.ap()[s:s + 1, :]),
                    waits=wb1, incs=[(dB1, 16)], isdma=True)
                for fc in range(32):
                    w = [(dB1, 16 * (s + 1))] if fc == 0 else []
                    if fc == 0 and s == 1:
                        w.append(E_b1e[0])
                    prS.add("tensor", lambda e, fc=fc: e.matmul(
                        ps_b1[:, fc:fc + 1], b1rs[0:1, fc * 128:(fc + 1) * 128],
                        identb[0:1, 0:1], start=True, stop=True),
                        waits=w, incs=([(peB, 1)] if fc == 31 else []))
                wev = [(peB, s + 1)]
                if s == 1:
                    wev.append(E_gelu[255])
                E_b1e[s] = prS.add("vector", lambda e: e.tensor_copy(
                    b1rT[:, :], ps_b1[:, :]), waits=wev)

                gwaits = list(R_EDGES) if s == 0 else [(pe1, 256)]
                first = True
                for fh in range(2):
                    gsem = gW1a if fh == 0 else gW1b
                    for hc in range(8):
                        w = gwaits if first else []
                        first = False
                        prS.add("gpsimd", lambda e, s=s, fh=fh, hc=hc:
                                e.indirect_dma_start(
                                    w1s[:, hc * FF + fh * 2048: hc * FF + fh * 2048 + 2048],
                                    None,
                                    w1.ap().rearrange("e h f -> (e h) f"),
                                    bass_rust.IndirectOffsetOnAxis(
                                        idxw1[:, s * 8 + hc: s * 8 + hc + 1], 0),
                                    element_offset=fh * 2048),
                                waits=w, incs=[(gsem, 16)], isdma=True)
                for fc in range(32):
                    w = [(peS, 1)] if (fc == 0 and s == 1) else []
                    prS.add("gpsimd", lambda e, s=s, fc=fc: e.indirect_dma_start(
                        w2s[:, fc * D: (fc + 1) * D], None,
                        w2.ap().rearrange("e f d -> (e f) d"),
                        bass_rust.IndirectOffsetOnAxis(
                            idxw2[:, s * 32 + fc: s * 32 + fc + 1], 0),
                        element_offset=0),
                        waits=w, incs=[(gW2, 16)], isdma=True)
                GW1 = 16 * 8 * (s + 1)
                GW2ALL = 16 * 32 * (s + 1)

                for tc in range(NTC):
                    q = s * NTC + tc
                    w = []
                    if q == 0:
                        w += [(dLT[0], 16 * (NTILE // 2)),
                              (dLT[1], 16 * (NTILE // 2))] + list(R_EDGES)
                    if q >= 2:
                        w.append((pe1, 32 * (q - 1)))
                    prS.add("scalar", lambda e, tc=tc, q=q: e.dma_start(
                        out=ln1c[q % 2][:, :].rearrange("p (a t) -> p a t", a=8),
                        in_=ln1T_d.ap()[:, :, tc * 512:(tc + 1) * 512].rearrange(
                            "a p t -> p a t")),
                        waits=w, incs=[(dL[q % 2], 16)], isdma=True)
                    DL_W = (dL[q % 2], 16 * (q // 2 + 1))

                    for j in range(32):
                        gidx1 = pe1_cnt[0]
                        w = []
                        if j == 0:
                            w.append(DL_W)
                        if tc == 0:
                            if j == 0:
                                w.append((gW1a, GW1))
                            if j == 16:
                                w.append((gW1b, GW1))
                        if gidx1 >= 2:
                            w.append(E_gelu[gidx1 - 2])
                        for hc in range(8):
                            prS.add("tensor", lambda e, j=j, hc=hc, q=q: e.matmul(
                                m1p[j % 2][:, :],
                                w1s[:, hc * FF + j * 128: hc * FF + j * 128 + 128],
                                ln1c[q % 2][:, hc * 512:(hc + 1) * 512],
                                start=(hc == 0), stop=(hc == 7)),
                                waits=(w if hc == 0 else []),
                                incs=([(pe1, 1)] if hc == 7 else []))
                        pe1_cnt[0] += 1
                        wg = [(pe1, pe1_cnt[0])]
                        if gidx1 == 256 * s:
                            wg.append(E_b1e[s])
                        E_gelu[gidx1] = prS.add("scalar", lambda e, j=j, s=s: e.activation(
                            out=hmidT[:, j * 512:(j + 1) * 512],
                            in_=m1p[j % 2][:, :], func=AF.Gelu,
                            bias=b1rT[:, j:j + 1]),
                            waits=wg)

                    def load_evac_edge(load_g):
                        g2 = load_g - 2
                        s_, rem = divmod(g2, 32)
                        tc_, ts_ = divmod(rem, 4)
                        return E_evac[s_ * 64 + tc_ * 8 + ts_ * 2 + 1]

                    def emit_ld(ts):
                        gidx = tc * 4 + ts
                        gall = s * 32 + gidx
                        row0 = gidx * 128
                        w = []
                        if gall >= 2:
                            w.append(load_evac_edge(gall))
                        if s == 0:
                            if gidx == 0:
                                w += [(dXO[j], 16 * (NTILE // 4))
                                      for j in range(4)] + list(R_EDGES)
                            prS.add("sync", lambda e, row0=row0, gidx=gidx:
                                    e.dma_start(out=x1c[gidx % 2][:, :],
                                                in_=x1s_d.ap()[row0:row0 + 128, :]),
                                    waits=w, incs=[(dX1[gidx % 2], 16)], isdma=True)
                        else:
                            if gidx == 0:
                                w += [(dY0[0], 16 * 16), (dY0[1], 16 * 16)]
                            prS.add("sync", lambda e, row0=row0, gidx=gidx:
                                    e.dma_start(out=x1c[gidx % 2][:, :],
                                                in_=y0x_d.ap()[row0:row0 + 128, :]),
                                    waits=w, incs=[(dY0r[gidx % 2], 16)], isdma=True)
                    emit_ld(0)
                    emit_ld(1)

                    for ts in range(4):
                        for dh in range(2):
                            g = ts * 2 + dh
                            G = s * 64 + tc * 8 + g
                            w = []
                            if g == 0:
                                w.append(E_gelu[pe1_cnt[0] - 1])
                            if tc == 0 and g == 0:
                                w.append((gW2, GW2ALL))
                            if G >= 4:
                                w.append(E_evac[G - 4])
                            for fc in range(32):
                                wf = list(w) if fc == 0 else []
                                last = (fc == 31 and s == 0)
                                prS.add("tensor",
                                        lambda e, ts=ts, dh=dh, fc=fc, G=G, last=last:
                                        e.matmul(
                                    m2p[G % 4][:, :],
                                    hmidT[:, fc * 512 + ts * 128: fc * 512 + ts * 128 + 128],
                                    w2s[:, fc * D + dh * 512: fc * D + dh * 512 + 512],
                                    start=(fc == 0), stop=last),
                                    waits=wf,
                                    incs=([(pe2, 1)] if last else []))
                            if s == 1:
                                prS.add("tensor", lambda e, dh=dh, G=G: e.matmul(
                                    m2p[G % 4][:, :], onesrb[0:1, :],
                                    cb2c_bf[0:1, dh * 512: dh * 512 + 512],
                                    start=False, stop=True), incs=[(pe2, 1)])
                            pe2_cnt[0] += 1

                        gidx = tc * 4 + ts
                        gall = s * 32 + gidx
                        row0 = gidx * 128
                        csrc = c0_128 if s == 0 else c1_128
                        dsem = (dX1 if s == 0 else dY0r)[gidx % 2]
                        dcnt = 16 * (gidx // 2 + 1)
                        ost = ostage[gall % 2]
                        for dh in range(2):
                            G = s * 64 + tc * 8 + ts * 2 + dh
                            w = [(pe2, G + 1), (dsem, dcnt)]
                            if dh == 0 and gall >= 2:
                                if s == 0:
                                    w.append((dY0[gidx % 2], 16 * (gidx // 2)))
                                elif gidx >= 2:
                                    w.append((dOUT[gidx % 2], 16 * (gidx // 2)))
                                else:
                                    w.append((dY0[gidx % 2], 16 * 16))
                            ed = prS.add("vector", lambda e, dh=dh, G=G, csrc=csrc,
                                         gidx=gidx, ost=ost:
                                         e.scalar_tensor_tensor(
                                             ost[:, dh * 512:(dh + 1) * 512],
                                             m2p[G % 4][:, :], csrc[:, 0:1],
                                             x1c[gidx % 2][:, dh * 512:(dh + 1) * 512],
                                             op0=ALU.mult, op1=ALU.add),
                                         waits=w)
                            E_evac[G] = ed
                        gG = s * 64 + tc * 8 + ts * 2
                        if s == 0:
                            prS.add("sync", lambda e, row0=row0, ost=ost: e.dma_start(
                                out=y0x_d.ap()[row0:row0 + 128, :], in_=ost[:, :]),
                                waits=[E_evac[gG + 1]],
                                incs=[(dY0[gidx % 2], 16)], isdma=True)
                        else:
                            prS.add("sync", lambda e, row0=row0, ost=ost: e.dma_start(
                                out=out.ap()[row0:row0 + 128, :], in_=ost[:, :]),
                                waits=[E_evac[gG + 1]],
                                incs=[(dOUT[gidx % 2], 16)], isdma=True)
                        if ts < 2:
                            emit_ld(ts + 2)

                if s == 0:
                    prS.add("tensor", lambda e: e.nop(),
                            waits=[(pe2, pe2_cnt[0])], incs=[(peS, 1)])

            prS.add("sync", lambda e: e.nop(),
                    waits=[(dOUT[0], 16 * 16), (dOUT[1], 16 * 16)])

            with nc.Block() as blkS:
                prS.realize(blkS)

    return nc


_NC_CACHE = {}


def kernel(**inputs):
    x = np.ascontiguousarray(np.asarray(inputs["L0_attn_x"], dtype=np.float32))
    tp = lambda a: np.ascontiguousarray(np.asarray(a, dtype=np.float32).T)
    cc = lambda a: np.ascontiguousarray(np.asarray(a, dtype=np.float32))
    wr = lambda a: np.ascontiguousarray(
        np.asarray(a, dtype=np.float32).T.reshape(8, 128, D)
        .transpose(1, 0, 2).reshape(128, 8 * D))
    shared = {
        "qwT": wr(inputs["L0_attn_qw"]), "kwT": wr(inputs["L0_attn_kw"]),
        "vwT": wr(inputs["L0_attn_vw"]), "owT": wr(inputs["L0_attn_ow"]),
        "qb": cc(inputs["L0_attn_qb"]), "vb": cc(inputs["L0_attn_vb"]),
        "ob": cc(inputs["L0_attn_ob"]),
        "lna": cc(inputs["L0_attn_lna_w"]), "lnb": cc(inputs["L0_attn_lnb_w"]),
        "lnc": cc(inputs["L0_attn_lnc_w"]), "lnd": cc(inputs["L0_attn_lnd_w"]),
        "lqwT": tp(inputs["L0_attn_lqw"]), "lkwT": tp(inputs["L0_attn_lkw"]),
        "lvwT": tp(inputs["L0_attn_lvw"]),
        "lqb": cc(inputs["L0_attn_lqb"]), "lkb": cc(inputs["L0_attn_lkb"]),
        "lvb": cc(inputs["L0_attn_lvb"]),
        "gate_w": cc(inputs["L0_ffn_gate_w"]),
        "w1": cc(inputs["L0_ffn_w1"]), "b1": cc(inputs["L0_ffn_b1"]),
        "w2": cc(inputs["L0_ffn_w2"]), "b2": cc(inputs["L0_ffn_b2"]),
    }
    x16_full = x[:, :16, :]
    in_maps = []
    for c in range(8):
        rot = np.ascontiguousarray(
            np.concatenate([x16_full[(c + i) % 8] for i in range(8)], axis=0))
        m = dict(shared)
        m["xb"] = np.ascontiguousarray(x[c])
        m["x16"] = rot
        in_maps.append(m)

    if "nc" not in _NC_CACHE:
        _NC_CACHE["nc"] = build_nc()
    nc = _NC_CACHE["nc"]
    res = run_bass_kernel_spmd(nc, in_maps, core_ids=list(range(8)))
    outp = np.stack([np.asarray(res.results[c]["out"], dtype=np.float32)
                     for c in range(8)], axis=0)
    return outp
